# version 02
# baseline (speedup 1.0000x reference)
import math
import numpy as np
import ml_dtypes
from contextlib import ExitStack

import concourse.bass as bass
import concourse.mybir as mybir
from concourse.bass_utils import run_bass_kernel_spmd

F32 = mybir.dt.float32
BF16 = mybir.dt.bfloat16
I32 = mybir.dt.int32
AF = mybir.ActivationFunctionType
ALU = mybir.AluOpType
AX = mybir.AxisListType.X

NCORE = 8
B = 2
S = 8192
D = 1024
T = 2048
NFC = D // 128
NTC = T // 128
NTT = T // 512
GROUPS = [[0, 1, 2, 3], [4, 5, 6, 7]]
LN_EPS = 1e-5

ENGINES = ("sync", "vector", "scalar", "tensor", "gpsimd")


class Sem:
    __slots__ = ("h", "n", "name")

    def __init__(self, h, name):
        self.h = h
        self.n = 0
        self.name = name


class Ctx:
    def __init__(self, nc, stack, block):
        self.nc = nc
        self.gstack = stack
        self.stack = stack
        self.block = block
        self.q = {e: [] for e in ENGINES}
        self.sems = {}
        self.trace = []
        self.sem_owner = {}
        self.tqueues = {e: [] for e in ENGINES}

    def verify(self):
        qs = {e: [] for e in ENGINES}
        for rec in self.trace:
            qs[rec[0]].append(rec)
        heads = {e: 0 for e in ENGINES}
        counts = {}
        progress = True
        total = sum(len(v) for v in qs.values())
        done = 0
        while progress:
            progress = False
            for e in ENGINES:
                while heads[e] < len(qs[e]):
                    _, waits, inc, _ = qs[e][heads[e]]
                    if all(counts.get(h, 0) >= v for h, v in waits):
                        if inc:
                            counts[inc[0]] = counts.get(inc[0], 0) + inc[1]
                        heads[e] += 1
                        done += 1
                        progress = True
                    else:
                        break
        if done != total:
            name2id = {id(s.h): n for n, s in self.sems.items()}
            msg = [f"DEADLOCK: {done}/{total} ops executed"]
            for e in ENGINES:
                if heads[e] < len(qs[e]):
                    _, waits, inc, nm = qs[e][heads[e]]
                    wd = [(name2id.get(h, h), v, counts.get(h, 0))
                          for h, v in waits]
                    msg.append(f"  {e}[{heads[e]}] {nm} waits {wd}")
            raise RuntimeError("\n".join(msg))

    def sem(self, name):
        if name not in self.sems:
            h = self.gstack.enter_context(self.nc.semaphore(name))
            self.sems[name] = Sem(h, name)
        return self.sems[name]

    def usem(self, name):
        return self.sem(name[:3])

    _uid = 0

    def sbuf(self, name, shape, dtype):
        Ctx._uid += 1
        return self.stack.enter_context(
            self.nc.sbuf_tensor(f"{name}_{Ctx._uid}", list(shape), dtype))

    def psum(self, name, shape, dtype=F32):
        return self.stack.enter_context(
            self.nc.psum_tensor(name, list(shape), dtype))

    def scope(self):
        from contextlib import contextmanager

        @contextmanager
        def _s():
            st = ExitStack()
            old = self.stack
            self.stack = st
            try:
                yield
                self.barrier()
            finally:
                self.flush()
                self.stack = old
                st.close()

        return _s()

    AUTO_ENGINES = ("vector", "scalar", "gpsimd")

    def op(self, eng, f, wait=(), inc=None, incn=1, chain=True, record=True):
        waits = []
        for w in wait:
            if isinstance(w, tuple):
                s, v = w
                waits.append((s.h, s.n if v is None else v))
            else:
                waits.append((w.h, w.n))
        auto = eng in self.AUTO_ENGINES
        if auto and chain:
            prev = getattr(self, "_last_done", {}).get(eng)
            if prev is not None:
                waits.append(prev)
            if inc is None:
                inc = self.sem("auto_" + eng)
                incn = 1
        incs = None
        if inc is not None:
            if inc.name != "bar":
                owner = self.sem_owner.setdefault(inc.name, eng)
                assert owner == eng, (
                    f"sem {inc.name} inc'd from {eng} and {owner}: "
                    f"multi-engine sems make coarse waits unsound")
            inc.n += incn
            incs = (inc.h, incn)
            if auto and chain and record:
                if not hasattr(self, "_last_done"):
                    self._last_done = {}
                self._last_done[eng] = (inc.h, inc.n)

        dd = {}
        for h, v in waits:
            k = id(h)
            if k not in dd or dd[k][1] < v:
                dd[k] = (h, v)
        waits = list(dd.values())
        self.trace.append((eng, [(id(h), v) for h, v in waits],
                           (id(incs[0]), incs[1]) if incs else None,
                           getattr(f, "__qualname__", "?")))

        def run(e, f=f, waits=waits, incs=incs):
            for i, (h, v) in enumerate(waits):
                e.wait_ge(h, v)
                if i % 2 == 1 and i + 1 < len(waits):
                    e.nop()
            r = f(e)
            if incs is not None:
                if r is None:
                    e.sem_inc(incs[0], incs[1])
                else:
                    r.then_inc(*incs)

        self.q[eng].append(run)

    def dma(self, eng, out, in_, sem, wait=()):
        self.op(eng, lambda e, o=out, i=in_: e.dma_start(out=o, in_=i),
                wait=wait, inc=sem, incn=16)

    def barrier(self):
        s = self.sem("bar")
        base = s.n
        for eng in ENGINES:
            self.op(eng, lambda e: None, inc=s, record=False)
        for eng in ENGINES:
            self.op(eng, lambda e, h=s.h, v=base + 5: e.wait_ge(h, v),
                    chain=False)
        if not hasattr(self, "_last_done"):
            self._last_done = {}
        for eng in self.AUTO_ENGINES:
            self._last_done[eng] = (s.h, base + 5)

    def flush(self):
        for eng in ENGINES:
            fns = self.q[eng]
            if not fns:
                continue
            self.q[eng] = []

            def body(e, fns=fns):
                for f in fns:
                    f(e)

            getattr(self.block, eng)(body)


def dma_q(ctx, q, out, in_, sem_s, sem_g, wait=()):
    if q == 0:
        ctx.dma("sync", out, in_, sem_s, wait=wait)
    else:
        ctx.op("gpsimd", lambda e, o=out, i=in_: e.dma_start(out=o, in_=i),
               wait=wait, inc=sem_g, incn=16, chain=False, record=False)


def to_fm(x2d):
    Tn, Dn = x2d.shape
    return np.ascontiguousarray(
        x2d.T.reshape(Dn // 128, 128, Tn).transpose(1, 0, 2).reshape(128, -1))


def from_fm(dev, Tn):
    nfc = dev.shape[1] // Tn
    return np.ascontiguousarray(
        dev.reshape(128, nfc, Tn).transpose(1, 0, 2).reshape(nfc * 128, Tn).T)


class Com:
    def __init__(self, ctx):
        self.ones_row = ctx.sbuf("ones_row", [1, 128], BF16)
        self.onesf_row = ctx.sbuf("onesf_row", [1, 128], F32)
        self.ones_col = ctx.sbuf("ones_col", [128, 1], BF16)
        self.ident = ctx.sbuf("ident", [128, 128], BF16)
        self.identf = ctx.sbuf("identf", [128, 128], F32)
        self.tril = ctx.sbuf("trilm", [128, 128], F32)
        self.cst = ctx.sbuf("cstc", [128, 8], F32)
        self.init_done = ctx.sem("com_init")

    def emit_init(self, ctx):
        c = self
        ctx.op("gpsimd", lambda e: e.memset(c.identf[:], 1.0))
        ctx.op("gpsimd", lambda e: e.affine_select(
            c.ident[:], c.identf[:], pattern=[[-1, 128]],
            compare_op=ALU.is_equal, fill=0.0, base=0, channel_multiplier=1))
        ctx.op("gpsimd", lambda e: e.affine_select(
            c.tril[:], c.identf[:], pattern=[[1, 128]],
            compare_op=ALU.is_ge, fill=0.0, base=0, channel_multiplier=-1))

        def gmisc(e):
            e.memset(c.ones_row[:], 1.0)
            e.memset(c.onesf_row[:], 1.0)
            e.memset(c.ones_col[:], 1.0)
            e.memset(c.cst[:, 0:1], 1.0)
            e.memset(c.cst[:, 1:2], 0.5 * math.pi)
            e.memset(c.cst[:, 2:3], LN_EPS)
            e.memset(c.cst[:, 3:4], 1e-6)
            return e.memset(c.cst[:, 4:8], 0.0)

        ctx.op("gpsimd", gmisc)
        ctx.op("gpsimd", lambda e: e.tensor_copy(c.identf[:], c.ident[:]),
               inc=self.init_done)
        for eng in ENGINES:
            ctx.op(eng, lambda e: None, wait=[self.init_done])


def emit_ln(ctx, com, xbuf, xn_bf, pstat, pbc, out_f32=None, tag="ln",
            gcols=None, bcols=None, Tn=T, cs=None):
    sv = ctx.sem("lnv")
    ss = ctx.sem("lns")
    st = ctx.sem("lnt")
    su = ctx.sem("lnu")
    sb = ctx.sem("lnb")
    NT = Tn // 512
    if cs is None:
        cs = Tn
    ctx.barrier()
    sqb = ctx.sbuf(tag + "_sqb", [128, 2 * Tn], BF16)
    rowm = ctx.sbuf(tag + "_rowm", [1, Tn], F32)
    rowv = ctx.sbuf(tag + "_rowv", [1, Tn], F32)
    rowt = ctx.sbuf(tag + "_rowt", [1, Tn], F32)
    rowr = ctx.sbuf(tag + "_rowr", [1, Tn], F32)
    bca = ctx.sbuf(tag + "_bca", [128, Tn], F32)
    bcb = ctx.sbuf(tag + "_bcb", [128, Tn], F32)
    lnt = ctx.sbuf(tag + "_lnt", [128, 512], F32)

    for f in range(NFC):
        xc = xbuf[:, f * cs:f * cs + Tn]
        ctx.op("vector", lambda e, o=xn_bf[:, f * Tn:(f + 1) * Tn], i=xc:
               e.tensor_copy(o, i), inc=sv)
        sq = sqb[:, (f % 2) * Tn:(f % 2 + 1) * Tn]
        ctx.op("scalar", lambda e, o=sq, i=xc: e.activation(o, i, AF.Square),
               wait=[(st, None)], inc=ss)
        for tt in range(NT):
            ctx.op("tensor",
                   lambda e, ps=pstat[0:1, tt * 512:(tt + 1) * 512],
                   r=xn_bf[:, f * Tn + tt * 512:f * Tn + (tt + 1) * 512],
                   s0=(f == 0), s1=(f == NFC - 1):
                   e.matmul(ps, com.ones_col[:], r, start=s0, stop=s1),
                   wait=[(sv, None)],
                   inc=(su if (f == NFC - 1) else None))
            ctx.op("tensor",
                   lambda e, ps=pstat[0:1, 2048 + tt * 512:2048 + (tt + 1) * 512],
                   r=sq[:, tt * 512:(tt + 1) * 512],
                   s0=(f == 0), s1=(f == NFC - 1):
                   e.matmul(ps, com.ones_col[:], r, start=s0, stop=s1),
                   wait=[(ss, None)],
                   inc=(st if tt == NT - 1 else None))

    mean = rowm[0:1, :]
    var = rowv[0:1, :]
    tmp = rowt[0:1, :]
    rstd = rowr[0:1, :]
    ctx.op("scalar", lambda e: e.mul(mean, pstat[0:1, 0:Tn], 1.0 / D),
           wait=[(st, None), (su, None)], inc=ss)
    ctx.op("scalar", lambda e: e.mul(var, pstat[0:1, 2048:2048 + Tn], 1.0 / D),
           inc=ss)
    ctx.op("vector", lambda e: e.tensor_mul(tmp, mean, mean),
           wait=[(ss, None)], inc=sv)
    ctx.op("vector", lambda e: e.scalar_tensor_tensor(
        var, tmp, -1.0, var, ALU.mult, ALU.add), inc=sv)
    ctx.op("scalar", lambda e: e.activation(tmp, var, AF.Ln,
                                            bias=com.cst[0:1, 2:3]),
           wait=[(sv, None)], inc=ss)
    ctx.op("scalar", lambda e: e.activation(rstd, tmp, AF.Exp, scale=-0.5),
           inc=ss)

    for tt in range(NT):
        sl = slice(tt * 512, (tt + 1) * 512)
        ctx.op("tensor", lambda e, ps=pbc[:, 0:512], r=mean[0:1, sl]:
               e.matmul(ps, com.onesf_row[:], r, start=True, stop=True),
               wait=[(ss, None), (sb, None), (sv, None)])
        ctx.op("tensor", lambda e, ps=pbc[:, 512:1024], r=rstd[0:1, sl]:
               e.matmul(ps, com.onesf_row[:], r, start=True, stop=True),
               inc=su)
        ctx.op("scalar", lambda e, o=bca[:, sl]: e.copy(o, pbc[:, 0:512]),
               wait=[(su, None)], inc=ss)
        ctx.op("vector", lambda e, o=bcb[:, sl]:
               e.tensor_copy(o, pbc[:, 512:1024]), wait=[(su, None)], inc=sb)

    for f in range(NFC):
        for tt in range(NT):
            sl = slice(f * cs + tt * 512, f * cs + (tt + 1) * 512)
            nsl = slice(f * Tn + tt * 512, f * Tn + (tt + 1) * 512)
            bsl = slice(tt * 512, (tt + 1) * 512)
            xt = xbuf[:, sl]
            dst = xn_bf[:, nsl] if out_f32 is None else out_f32[:, sl]
            ctx.op("vector", lambda e, x=xt, m=bca[:, bsl]:
                   e.tensor_sub(lnt[:], x, m),
                   wait=[(sb, None), (ss, None)], inc=sv)
            if gcols is None:
                ctx.op("vector", lambda e, o=dst, r=bcb[:, bsl]:
                       e.tensor_mul(o, lnt[:], r), inc=sv)
            else:
                ctx.op("vector", lambda e, r=bcb[:, bsl]:
                       e.tensor_mul(lnt[:], lnt[:], r), inc=sv)
                ctx.op("vector", lambda e, o=dst, f=f:
                       e.tensor_scalar(o, lnt[:], gcols[:, f:f + 1],
                                       bcols[:, f:f + 1], ALU.mult, ALU.add),
                       inc=sv)
            if out_f32 is not None:
                ctx.op("vector", lambda e, o=xn_bf[:, nsl], i=out_f32[:, sl]:
                       e.tensor_copy(o, i), inc=sv)


def emit_l0_gates(ctx, com, hw, xn_bf, gt_dram, sd, P):
    gtc = ctx.sbuf("l0_gtc", [1, 1024], F32)
    lg_dbg = hw.get("lg_dbg")
    sv = ctx.sem("f0v")
    ss = ctx.sem("f0s")
    st = ctx.sem("f0t")
    wgst = ctx.sbuf("l0_wgst", [128, 64], F32)
    wgb = ctx.sbuf("l0_wgb", [128, 64], BF16)
    bgr = ctx.sbuf("l0_bgr", [1, 8], F32)
    bgb = ctx.sbuf("l0_bgb", [1, 8], BF16)
    gs = ctx.sbuf("l0_gs", [128, 40], F32)
    m1 = ctx.sbuf("l0_m1", [128, 4], F32)
    pg = P[:, 0:8]
    pgt = P[0:1, 512:640]

    ctx.dma("sync", wgst[:], hw["l0_wg"], sd)
    ctx.dma("sync", bgr[:], hw["l0_bg"], sd)
    ctx.op("vector", lambda e: e.tensor_copy(wgb[:], wgst[:]),
           wait=[(sd, None)], inc=sv)
    ctx.op("vector", lambda e: e.tensor_copy(bgb[:], bgr[:]), inc=sv)
    for tc in range(NTC):
        for f in range(NFC):
            ctx.op("tensor",
                   lambda e, l=xn_bf[:, f * T + tc * 128:f * T + (tc + 1) * 128],
                   r=wgb[:, f * 8:(f + 1) * 8], s0=(f == 0), s1=(f == NFC - 1):
                   e.matmul(pg[:, 0:8], l, r, start=s0, stop=False),
                   wait=([(sv, None)] if f == 0 else []))
        ctx.op("tensor", lambda e: e.matmul(
            pg[:, 0:8], com.ones_row[:], bgb[:], start=False, stop=True),
            inc=st)
        if lg_dbg is not None:
            ctx.op("vector", lambda e: e.tensor_copy(gs[:, 24:32], pg[:, 0:8]),
                   wait=[(st, None)], inc=sv)
            ctx.dma("sync", lg_dbg[:, tc * 8:(tc + 1) * 8], gs[:, 24:32], sd,
                    wait=[(sv, None)])
        ctx.op("vector", lambda e: e.reduce_max(m1[:, 0:1], pg[:, 0:8], axis=AX),
               wait=[(st, None)], inc=sv)
        ctx.op("vector", lambda e: e.tensor_scalar(
            gs[:, 0:8], pg[:, 0:8], m1[:, 0:1], None, ALU.is_ge), inc=sv)
        ctx.op("vector", lambda e: e.scalar_tensor_tensor(
            gs[:, 8:16], gs[:, 0:8], -1e9, pg[:, 0:8], ALU.mult, ALU.add),
            inc=sv)
        ctx.op("vector", lambda e: e.reduce_max(m1[:, 1:2], gs[:, 8:16], axis=AX),
               inc=sv)
        ctx.op("vector", lambda e: e.tensor_scalar(
            gs[:, 16:24], gs[:, 8:16], m1[:, 1:2], None, ALU.is_ge), inc=sv)
        ctx.op("vector", lambda e: e.tensor_sub(m1[:, 2:3], m1[:, 0:1],
                                                m1[:, 1:2]), inc=sv)
        ctx.op("scalar", lambda e: e.activation(m1[:, 3:4], m1[:, 2:3],
                                                AF.Sigmoid),
               wait=[(sv, None)], inc=ss)
        ctx.op("vector", lambda e: e.tensor_scalar(
            m1[:, 2:3], m1[:, 3:4], -1.0, 1.0, ALU.mult, ALU.add),
            wait=[(ss, None)], inc=sv)
        ctx.op("vector", lambda e: e.tensor_scalar(
            gs[:, 24:32], gs[:, 0:8], m1[:, 3:4], None, ALU.mult), inc=sv)
        ctx.op("vector", lambda e: e.scalar_tensor_tensor(
            gs[:, 32:40], gs[:, 16:24], m1[:, 2:3], gs[:, 24:32],
            ALU.mult, ALU.add), inc=sv)
        for e_ in range(8):
            ctx.op("tensor", lambda e, e_=e_: e.transpose(
                pgt[0:1, 0:128], gs[:, 32 + e_:33 + e_], com.identf[:]),
                wait=[(sv, None)], inc=st)
            ctx.op("vector",
                   lambda e, o=gtc[0:1, e_ * 128:(e_ + 1) * 128]:
                   e.tensor_copy(o, pgt[0:1, 0:128]),
                   wait=[(st, None)] + ([(sd, None)] if e_ == 0 else []),
                   inc=sv)
        ctx.dma("sync", gt_dram[:, tc * 128:(tc + 1) * 128],
                gtc[0:1, :], sd, wait=[(sv, None)])
    ctx.op("sync", lambda e: None, wait=[(sd, None)])
    ctx.barrier()


def emit_l0_ffn(ctx, com, hw, xn_bf, gdram, P):
    sd = ctx.sem("f0d")
    sv = ctx.sem("f0v")
    ss = ctx.sem("f0s")
    st = ctx.sem("f0t")
    sgd = ctx.sem("f0gd")

    gt_dram = hw["gt_dram"]
    with ctx.scope():
        emit_l0_gates(ctx, com, hw, xn_bf, gt_dram, sd, P)
    grow = ctx.sbuf("l0_grow", [1, 1024], F32)

    acc = ctx.sbuf("l0_acc", [128, 8 * 1024], F32)
    xg = ctx.sbuf("l0_xg", [128, 8 * 1024], BF16)
    gb = ctx.sbuf("l0_gb", [128, 1024], F32)
    gbb = ctx.sbuf("l0_gbb", [128, 1024], BF16)
    wst = ctx.sbuf("l0_wst", [128, 2048], F32)
    w1b = ctx.sbuf("l0_w1b", [128, 4096], BF16)
    w3b = ctx.sbuf("l0_w3b", [128, 4096], BF16)
    w2b = ctx.sbuf("l0_w2b", [128, 4096], BF16)
    slt = ctx.sbuf("l0_slt", [128, 2048], F32)
    ht = ctx.sbuf("l0_ht", [128, 2048], BF16)
    gtile = ctx.sbuf("l0_gtile", [128, 512], F32)
    py = [P[:, i * 512:(i + 1) * 512] for i in range(8)]
    pw = [P[:, i * 512:(i + 1) * 512] for i in range(2)]

    for th in range(2):
        tsl = slice(th * 1024, (th + 1) * 1024)
        ctx.op("gpsimd", lambda e: e.memset(acc[:], 0.0),
               wait=[(sgd, None), (sv, None)], inc=ctx.sem("f0ga"))
        for e_ in range(8):
            ctx.dma("sync", grow[:],
                    gt_dram[e_:e_ + 1, th * 1024:(th + 1) * 1024], sd,
                    wait=[(sv, None)])
            for q in range(2):
                ctx.op("tensor",
                       lambda e, ps=pw[q], r=grow[0:1, q * 512:(q + 1) * 512]:
                       e.matmul(ps, com.onesf_row[:], r, start=True, stop=True),
                       wait=[(sv, None), (sd, None)], inc=st)
            for q in range(2):
                ctx.op("vector", lambda e, q=q:
                       e.tensor_copy(gb[:, q * 512:(q + 1) * 512], pw[q][:]),
                       wait=[(st, None)], inc=sv)
            ctx.op("vector", lambda e: e.tensor_copy(gbb[:], gb[:]), inc=sv)
            for f in range(NFC):
                ctx.op("vector",
                       lambda e, o=xg[:, f * 1024:(f + 1) * 1024],
                       i=xn_bf[:, f * T:(f + 1) * T][:, tsl]:
                       e.tensor_mul(o, i, gbb[:]),
                       wait=[(st, None)], inc=sv)
            for hh in range(4):
                sq0 = ctx.sem("f0q")
                for g2 in range(2):
                    for kf in range(4):
                        dma_q(ctx, kf % 2, wst[:, kf * 512:(kf + 1) * 512],
                              hw["l0_w1"][e_, g2 * 4 + kf, :,
                                          hh * 512:(hh + 1) * 512], sd, sq0,
                              wait=[(sv, None)] if kf < 2 else ())
                    ctx.op("vector", lambda e, g2=g2: e.tensor_copy(
                        w1b[:, g2 * 2048:(g2 + 1) * 2048], wst[:]),
                        wait=[(sd, None), (sq0, None), (st, None)], inc=sv)
                for g2 in range(2):
                    for kf in range(4):
                        dma_q(ctx, kf % 2, wst[:, kf * 512:(kf + 1) * 512],
                              hw["l0_w3"][e_, g2 * 4 + kf, :,
                                          hh * 512:(hh + 1) * 512], sd, sq0,
                              wait=[(sv, None)] if kf < 2 else ())
                    ctx.op("vector", lambda e, g2=g2: e.tensor_copy(
                        w3b[:, g2 * 2048:(g2 + 1) * 2048], wst[:]),
                        wait=[(sd, None), (sq0, None), (st, None)], inc=sv)
                for g2 in range(2):
                    for hc in range(2):
                        dma_q(ctx, hc % 2, wst[:, hc * 1024:(hc + 1) * 1024],
                              hw["l0_w2"][e_, hh * 4 + g2 * 2 + hc, :, :],
                              sd, sq0, wait=[(sv, None)])
                    ctx.op("vector", lambda e, g2=g2: e.tensor_copy(
                        w2b[:, g2 * 2048:(g2 + 1) * 2048], wst[:]),
                        wait=[(sd, None), (sq0, None), (st, None)], inc=sv)
                for tq in range(2):
                    tk = slice(th * 1024 + tq * 512, th * 1024 + (tq + 1) * 512)
                    tkh = slice(tq * 512, (tq + 1) * 512)
                    for hc in range(4):
                        for kf in range(NFC):
                            ctx.op("tensor",
                                   lambda e, ps=py[hc],
                                   l=w1b[:, kf * 512 + hc * 128:
                                         kf * 512 + (hc + 1) * 128],
                                   r=xn_bf[:, kf * T:(kf + 1) * T][:, tk],
                                   s0=(kf == 0), s1=(kf == NFC - 1):
                                   e.matmul(ps, l, r, start=s0, stop=s1),
                                   wait=([(sv, None), (ctx.sem("f0ga"), None)]
                                         if (hc == 0 and kf == 0) else []),
                                   inc=(st if kf == NFC - 1 else None))
                        for kf in range(NFC):
                            ctx.op("tensor",
                                   lambda e, ps=py[4 + hc],
                                   l=w3b[:, kf * 512 + hc * 128:
                                         kf * 512 + (hc + 1) * 128],
                                   r=xg[:, kf * 1024:(kf + 1) * 1024][:, tkh],
                                   s0=(kf == 0), s1=(kf == NFC - 1):
                                   e.matmul(ps, l, r, start=s0, stop=s1),
                                   inc=(st if kf == NFC - 1 else None))
                    for hc in range(4):
                        ctx.op("scalar", lambda e, ps=py[hc], hc=hc:
                               e.activation(slt[:, hc * 512:(hc + 1) * 512],
                                            ps[:], AF.Silu),
                               wait=[(st, None)], inc=ss)
                    for hc in range(4):
                        ctx.op("vector",
                               lambda e, o=ht[:, hc * 512:(hc + 1) * 512],
                               i=slt[:, hc * 512:(hc + 1) * 512],
                               ps=py[4 + hc]:
                               e.tensor_mul(o, i, ps[:]),
                               wait=[(ss, None), (st, None)], inc=sv)
                    for oc in range(8):
                        pwb = pw[oc % 2]
                        for hc in range(4):
                            ctx.op("tensor",
                                   lambda e, ps=pwb,
                                   l=w2b[:, hc * 1024 + oc * 128:
                                         hc * 1024 + (oc + 1) * 128],
                                   r=ht[:, hc * 512:(hc + 1) * 512],
                                   s0=(hc == 0), s1=(hc == 3):
                                   e.matmul(ps, l, r, start=s0, stop=s1),
                                   wait=([(sv, None)] if hc == 0 else []),
                                   inc=(st if hc == 3 else None))
                        ctx.op("vector",
                               lambda e, ps=pwb,
                               o=acc[:, oc * 1024:(oc + 1) * 1024][:, tkh]:
                               e.tensor_add(o, o, ps[:]),
                               wait=[(st, None)], inc=sv)
        for oc in range(8):
            for tq in range(2):
                ctx.op("scalar",
                       lambda e, i=acc[:, oc * 1024 + tq * 512:
                                       oc * 1024 + (tq + 1) * 512]:
                       e.activation(gtile[:], i, AF.Silu),
                       wait=[(sv, None), (sgd, None)], inc=ss)
                ctx.dma("sync",
                        gdram[:, oc * T + th * 1024 + tq * 512:
                              oc * T + th * 1024 + (tq + 1) * 512],
                        gtile[:], sgd, wait=[(ss, None)])
        ctx.op("sync", lambda e: None, wait=[(sgd, None)])
        ctx.barrier()


def emit_l0_attn(ctx, com, hw, xn_bf, xbuf, gdram, cc_in, cc_out, P):
    sd = ctx.sem("a0d")
    sv = ctx.sem("a0v")
    ss = ctx.sem("a0s")
    st = ctx.sem("a0t")
    scc = ctx.sem("a0cc")

    pv = P[:, 0:1024]
    pu = P[:, 1024:1280]
    pkv = P[:, 1536:4096]

    def kvtile(j):
        bank, sl = divmod(j, 7)
        return pkv[:, bank * 512 + sl * 65: bank * 512 + sl * 65 + 65]

    with ctx.scope():
        KB = ctx.sbuf("l0_kb", [128, NFC * T], BF16)
        QB = ctx.sbuf("l0_qb", [128, NFC * T], BF16)
        kvb = ctx.sbuf("l0_kvb", [128, 2080], BF16)
        qpt = ctx.sbuf("l0_qpt", [128, 1024], BF16)
        ABF = KB

        with ctx.scope():
            WS = ctx.sbuf("l0_ws", [128, 512], F32)
            WB = ctx.sbuf("l0_wb", [128, 8 * 1024], BF16)
            bcol = ctx.sbuf("l0_bcol", [128, 8], F32)
            bvrow_b = ctx.sbuf("l0_bvrowb", [1, 1024], BF16)
            rmb = ctx.sbuf("l0_rmb", [128, 256], BF16)
            vt = ctx.sbuf("l0_vt", [128, 16 * 65], BF16)
            elt = ctx.sbuf("l0_elt", [128, 512], F32)
            el2 = ctx.sbuf("l0_el2", [128, 512], F32)
            kpt = ctx.sbuf("l0_kpt", [128, 256], BF16)
            kvsb = ctx.sbuf("l0_kvsb", [128, 2080], F32)
            recb = ctx.sbuf("l0_recb", [128, 512], F32)
            drow = ctx.sbuf("l0_drow", [1, 512], F32)

            ctx.dma("sync", WS[0:64, 0:256], hw["l0_rm"], sd)
            ctx.dma("sync", WS[64:128, 0:256], hw["l0_rm"], sd)
            ctx.op("vector", lambda e: e.tensor_copy(rmb[:], WS[:, 0:256]),
                   wait=[(sd, None)], inc=sv)

            for (wname, bname, dst) in (("l0_wk", "l0_bk", KB),
                                        ("l0_wq", "l0_bq", QB)):
                ctx.dma("sync", bcol[:], hw[bname], sd, wait=[(sv, None)])
                for kf in range(NFC):
                    for hf in range(2):
                        ctx.dma("sync", WS[:],
                                hw[wname][kf, :, hf * 512:(hf + 1) * 512], sd,
                                wait=[(sv, None)])
                        ctx.op("vector", lambda e, kf=kf, hf=hf: e.tensor_copy(
                            WB[:, kf * 1024 + hf * 512:kf * 1024 + (hf + 1) * 512],
                            WS[:]),
                            wait=[(sd, None), (st, None)], inc=sv)
                for mc in range(NFC):
                    for tt in range(NTT):
                        pb = pv[:, (tt % 2) * 512:(tt % 2 + 1) * 512]
                        for kf in range(NFC):
                            ctx.op("tensor",
                                   lambda e, ps=pb,
                                   l=WB[:, kf * 1024 + mc * 128:
                                         kf * 1024 + (mc + 1) * 128],
                                   r=xn_bf[:, kf * T + tt * 512:
                                           kf * T + (tt + 1) * 512],
                                   s0=(kf == 0), s1=(kf == NFC - 1):
                                   e.matmul(ps, l, r, start=s0, stop=s1),
                                   wait=([(sv, None)] if kf == 0 else []),
                                   inc=(st if kf == NFC - 1 else None))
                        ctx.op("vector",
                               lambda e, ps=pb, mc=mc,
                               o=dst[:, mc * T + tt * 512:mc * T + (tt + 1) * 512]:
                               e.tensor_scalar(o, ps[:], bcol[:, mc:mc + 1],
                                               None, ALU.add),
                               wait=[(st, None)], inc=sv)
            ctx.barrier()

            for hf in range(2):
                ctx.dma("sync", WS[0:1, 0:512],
                        hw["l0_bv_row"][0:1, hf * 512:(hf + 1) * 512], sd,
                        wait=[(sv, None)])
                ctx.op("vector", lambda e, hf=hf: e.tensor_copy(
                    bvrow_b[0:1, hf * 512:(hf + 1) * 512], WS[0:1, 0:512]),
                    wait=[(sd, None)], inc=sv)
            for kf in range(NFC):
                for hf in range(2):
                    ctx.dma("sync", WS[:],
                            hw["l0_wv"][kf, :, hf * 512:(hf + 1) * 512], sd,
                            wait=[(sv, None)])
                    ctx.op("vector", lambda e, kf=kf, hf=hf: e.tensor_copy(
                        WB[:, kf * 1024 + hf * 512:kf * 1024 + (hf + 1) * 512],
                        WS[:]),
                        wait=[(sd, None)], inc=sv)
            ctx.op("gpsimd", lambda e: [
                e.memset(vt[:, h * 65 + 64:h * 65 + 65], 1.0)
                for h in range(16)] and None)
            ctx.barrier()

            for tc in range(NTC):
                tsl = slice(tc * 128, (tc + 1) * 128)
                for kf in range(NFC):
                    for nn in range(2):
                        ctx.op("tensor",
                               lambda e, ps=pv[:, nn * 512:(nn + 1) * 512],
                               l=xn_bf[:, kf * T:(kf + 1) * T][:, tsl],
                               r=WB[:, kf * 1024 + nn * 512:
                                     kf * 1024 + (nn + 1) * 512],
                               s0=(kf == 0):
                               e.matmul(ps, l, r, start=s0, stop=False),
                               wait=([(sv, None)] if (kf == 0 and nn == 0)
                                     else []))
                for nn in range(2):
                    ctx.op("tensor",
                           lambda e, ps=pv[:, nn * 512:(nn + 1) * 512],
                           r=bvrow_b[0:1, nn * 512:(nn + 1) * 512]:
                           e.matmul(ps, com.ones_row[:], r, start=False,
                                    stop=True),
                           inc=st)
                for h in range(16):
                    ctx.op("vector",
                           lambda e, h=h:
                           e.tensor_copy(vt[:, h * 65:h * 65 + 64],
                                         pv[:, h * 64:(h + 1) * 64]),
                           wait=[(st, None)], inc=sv)
                for h in range(16):
                    hp, hb = divmod(h, 2)
                    ctx.op("tensor",
                           lambda e, hp=hp, hb=hb, tc=tc:
                           e.matmul(pu[:, 0:256],
                                    KB[64 * hb:64 * (hb + 1),
                                       hp * T + tc * 128:hp * T + (tc + 1) * 128],
                                    rmb[64 * hb:64 * (hb + 1), :],
                                    start=True, stop=True),
                           wait=[(sv, None)], inc=st)
                    ctx.op("vector", lambda e: e.tensor_scalar(
                        elt[:, 0:256], pu[:, 0:256], 0.0, None, ALU.min),
                        wait=[(st, None), (ss, None)], inc=sv)
                    ctx.op("scalar", lambda e: e.activation(
                        el2[:, 0:256], elt[:, 0:256], AF.Exp),
                        wait=[(sv, None)], inc=ss)
                    ctx.op("vector", lambda e: e.scalar_tensor_tensor(
                        kpt[:], pu[:, 0:256], 0.0, el2[:, 0:256],
                        ALU.max, ALU.add),
                        wait=[(ss, None), (st, None)], inc=sv)
                    for mc in range(2):
                        ctx.op("tensor",
                               lambda e, ps=kvtile(h * 2 + mc), mc=mc, h=h,
                               tc=tc:
                               e.matmul(ps, kpt[:, mc * 128:(mc + 1) * 128],
                                        vt[:, h * 65:(h + 1) * 65],
                                        start=(tc == 0), stop=(tc == NTC - 1)),
                               wait=[(sv, None)],
                               inc=(st if mc == 1 else None))
            for j in range(32):
                ctx.op("vector", lambda e, j=j:
                       e.tensor_copy(kvsb[:, j * 65:(j + 1) * 65], kvtile(j)),
                       wait=[(st, None)], inc=sv)
            sg0 = ctx.sem("a0g")
            ctx.dma("gpsimd", cc_in, kvsb[:], sg0, wait=[(sv, None)])
            ctx.op("gpsimd", lambda e: e.collective_compute(
                "AllReduce", ALU.add, replica_groups=GROUPS,
                ins=[cc_in], outs=[cc_out]), wait=[(sg0, None)], inc=scc)
            ctx.dma("gpsimd", kvsb[:], cc_out, sg0, wait=[(scc, None)])
            ctx.op("vector", lambda e: e.tensor_copy(kvb[:], kvsb[:]),
                   wait=[(sg0, None)], inc=sv)
            ctx.barrier()

            for h in range(16):
                hp, hb = divmod(h, 2)
                for tt in range(NTT):
                    qsl = slice(hp * T + tt * 512, hp * T + (tt + 1) * 512)
                    for mc in range(2):
                        ctx.op("tensor",
                               lambda e, ps=pkv[:, mc * 512:(mc + 1) * 512],
                               mc=mc, hb=hb, qsl=qsl:
                               e.matmul(ps, rmb[64 * hb:64 * (hb + 1),
                                                mc * 128:(mc + 1) * 128],
                                        QB[64 * hb:64 * (hb + 1), qsl],
                                        start=True, stop=True),
                               wait=[(sv, None)], inc=st)
                        ctx.op("vector",
                               lambda e, ps=pkv[:, mc * 512:(mc + 1) * 512]:
                               e.tensor_scalar(elt[:], ps[:], 0.0, None,
                                               ALU.min),
                               wait=[(st, None), (ss, None)], inc=sv)
                        ctx.op("scalar", lambda e: e.activation(
                            el2[:], elt[:], AF.Exp),
                            wait=[(sv, None)], inc=ss)
                        ctx.op("vector",
                               lambda e, ps=pkv[:, mc * 512:(mc + 1) * 512],
                               mc=mc:
                               e.scalar_tensor_tensor(
                                   qpt[:, mc * 512:(mc + 1) * 512],
                                   ps[:], 0.0, el2[:], ALU.max, ALU.add),
                               wait=[(ss, None), (st, None)], inc=sv)
                    for mc in range(2):
                        ctx.op("tensor",
                               lambda e, mc=mc, h=h:
                               e.matmul(pv[0:65, 0:512],
                                        kvb[:, (h * 2 + mc) * 65:
                                            (h * 2 + mc + 1) * 65],
                                        qpt[:, mc * 512:(mc + 1) * 512],
                                        start=(mc == 0), stop=(mc == 1)),
                               wait=[(sv, None)],
                               inc=(st if mc == 1 else None))
                    ctx.op("vector", lambda e: e.tensor_scalar(
                        drow[0:1, 0:512], pv[64:65, 0:512], 1e-6, None,
                        ALU.max),
                        wait=[(st, None)], inc=sv)
                    ctx.op("vector", lambda e: e.reciprocal(
                        el2[0:1, 0:512], drow[0:1, 0:512]), inc=sv)
                    ctx.op("tensor", lambda e: e.matmul(
                        pv[0:64, 512:1024], com.onesf_row[0:1, 0:64],
                        el2[0:1, 0:512], start=True, stop=True),
                        wait=[(sv, None)], inc=st)
                    ctx.op("scalar", lambda e: e.copy(recb[0:64, :],
                                                      pv[0:64, 512:1024]),
                           wait=[(st, None), (sv, None)], inc=ss)
                    ctx.op("vector",
                           lambda e, hb=hb, qsl=qsl:
                           e.tensor_mul(ABF[64 * hb:64 * (hb + 1), qsl],
                                        pv[0:64, 0:512], recb[0:64, :]),
                           wait=[(ss, None)], inc=sv)

        with ctx.scope():
            WS = ctx.sbuf("l0_ws2", [128, 1024], F32)
            WB = ctx.sbuf("l0_wb2", [128, 8 * 1024], BF16)
            bcol = ctx.sbuf("l0_bcol2", [128, 8], F32)
            stile = ctx.sbuf("l0_stile", [128, 512], F32)
            gbuf = ctx.sbuf("l0_gbuf", [128, 512], F32)
            ctx.dma("sync", bcol[:], hw["l0_bo"], sd)
            for kf in range(NFC):
                ctx.dma("sync", WS[:], hw["l0_wo"][kf], sd, wait=[(sv, None)])
                ctx.op("vector", lambda e, kf=kf: e.tensor_copy(
                    WB[:, kf * 1024:(kf + 1) * 1024], WS[:]),
                    wait=[(sd, None)], inc=sv)
            ctx.barrier()
            for mc in range(NFC):
                for tt in range(NTT):
                    pb = pv[:, (tt % 2) * 512:(tt % 2 + 1) * 512]
                    sg0 = ctx.sem("a0g")
                    ctx.dma("gpsimd", gbuf[:],
                            gdram[:, mc * T + tt * 512:mc * T + (tt + 1) * 512],
                            sg0, wait=[(sv, None)])
                    for kf in range(NFC):
                        ctx.op("tensor",
                               lambda e, ps=pb,
                               l=WB[:, kf * 1024 + mc * 128:
                                     kf * 1024 + (mc + 1) * 128],
                               r=ABF[:, kf * T + tt * 512:kf * T + (tt + 1) * 512],
                               s0=(kf == 0), s1=(kf == NFC - 1):
                               e.matmul(ps, l, r, start=s0, stop=s1),
                               wait=([(sv, None)] if kf == 0 else []),
                               inc=(st if kf == NFC - 1 else None))
                    ctx.op("vector",
                           lambda e, ps=pb, mc=mc:
                           e.tensor_scalar(stile[:], ps[:], bcol[:, mc:mc + 1],
                                           None, ALU.add),
                           wait=[(st, None), (ctx.sem("a0g"), None)], inc=sv)
                    ctx.op("vector", lambda e: e.tensor_mul(
                        stile[:], stile[:], gbuf[:]), inc=sv)
                    ctx.op("vector",
                           lambda e,
                           o=xbuf[:, mc * T + tt * 512:mc * T + (tt + 1) * 512]:
                           e.tensor_add(o, o, stile[:]), inc=sv)


def emit_l0(ctx, com, hw, glob):
    P = glob["P"]
    with ctx.scope():
        glob["xbuf"] = ctx.sbuf("l0_xbuf", [128, NFC * T], F32)
        glob["xn"] = ctx.sbuf("l0_xn", [128, NFC * T], BF16)
        sld = ctx.sem("xld")
        ctx.dma("sync", glob["xbuf"][:], glob["x_spill"], sld)
        for eng in ENGINES:
            ctx.op(eng, lambda e: None, wait=[sld])
        with ctx.scope():
            emit_ln(ctx, com, glob["xbuf"], glob["xn"],
                    P[0:1, 0:4096], P[:, 0:1024], tag="ln0")
        with ctx.scope():
            emit_l0_ffn(ctx, com, hw, glob["xn"], glob["gdram"], P)
        emit_l0_attn(ctx, com, hw, glob["xn"], glob["xbuf"], glob["gdram"],
                     hw["cc_kv_in"], hw["cc_kv_out"], P)
        ctx.dma("sync", glob["x_spill"], glob["xbuf"][:], sld)
        ctx.op("sync", lambda e: None, wait=[sld])

def emit_mlp_half(ctx, com, src_bf, w1ap, b1cols, w2ap, th, acc, bufs, P):
    sd = ctx.sem("m1d")
    sv = ctx.sem("m1v")
    ss = ctx.sem("m1s")
    st = ctx.sem("m1t")
    wst, w1b, w2b, rtile = bufs["wst"], bufs["w1b"], bufs["w2b"], bufs["rtile"]
    py = [P[:, i * 512:(i + 1) * 512] for i in range(4)]
    pw = [P[:, (4 + i) * 512:(5 + i) * 512] for i in range(2)]
    ctx.barrier()

    ctx.op("gpsimd", lambda e: e.memset(acc[:], 0.0),
           wait=[(sv, None), (ctx.sem("m1g"), None)], inc=ctx.sem("m1g"))
    sg = ctx.sem("m1g")
    sq = ctx.sem("m1q")
    for hh in range(8):
        for g2 in range(2):
            for kf in range(4):
                dma_q(ctx, kf % 2, wst[:, kf * 512:(kf + 1) * 512],
                      w1ap[g2 * 4 + kf, :, hh * 512:(hh + 1) * 512], sd, sq,
                      wait=[(sv, None)])
            ctx.op("vector", lambda e, g2=g2: e.tensor_copy(
                w1b[:, g2 * 2048:(g2 + 1) * 2048], wst[:]),
                wait=[(sd, None), (sq, None), (st, None)], inc=sv)
        for g2 in range(2):
            for hc in range(2):
                dma_q(ctx, hc % 2, wst[:, hc * 1024:(hc + 1) * 1024],
                      w2ap[hh * 4 + g2 * 2 + hc, :, :], sd, sq,
                      wait=[(sv, None)])
            ctx.op("vector", lambda e, g2=g2: e.tensor_copy(
                w2b[:, g2 * 2048:(g2 + 1) * 2048], wst[:]),
                wait=[(sd, None), (sq, None), (st, None)], inc=sv)
        for tq in range(2):
            tk = slice(th * 1024 + tq * 512, th * 1024 + (tq + 1) * 512)
            tkh = slice(tq * 512, (tq + 1) * 512)
            for hc in range(4):
                for kf in range(NFC):
                    ctx.op("tensor",
                           lambda e, ps=py[hc],
                           l=w1b[:, kf * 512 + hc * 128:kf * 512 + (hc + 1) * 128],
                           r=src_bf[:, kf * T:(kf + 1) * T][:, tk],
                           s0=(kf == 0), s1=(kf == NFC - 1):
                           e.matmul(ps, l, r, start=s0, stop=s1),
                           wait=([(sv, None), (sg, None)]
                                 if (hc == 0 and kf == 0) else []),
                           inc=(st if kf == NFC - 1 else None))
            for hc in range(4):
                ctx.op("scalar", lambda e, ps=py[hc], hc=hc, hh=hh:
                       e.activation(rtile[:, hc * 512:(hc + 1) * 512], ps[:],
                                    AF.Relu, bias=b1cols[:, hh * 4 + hc:
                                                         hh * 4 + hc + 1]),
                       wait=[(st, None), (sv, None)], inc=ss)
            for oc in range(8):
                pwb = pw[oc % 2]
                for hc in range(4):
                    ctx.op("tensor",
                           lambda e, ps=pwb,
                           l=w2b[:, hc * 1024 + oc * 128:hc * 1024 + (oc + 1) * 128],
                           r=rtile[:, hc * 512:(hc + 1) * 512],
                           s0=(hc == 0), s1=(hc == 3):
                           e.matmul(ps, l, r, start=s0, stop=s1),
                           wait=([(ss, None), (sv, None)] if hc == 0 else []),
                           inc=(st if hc == 3 else None))
                ctx.op("vector",
                       lambda e, ps=pwb,
                       o=acc[:, oc * 1024:(oc + 1) * 1024][:, tkh]:
                       e.tensor_add(o, o, ps[:]),
                       wait=[(st, None)], inc=sv)


def mlp_bufs(ctx, tag):
    return {
        "wst": ctx.sbuf(tag + "_wst", [128, 2048], F32),
        "w1b": ctx.sbuf(tag + "_w1b", [128, 4096], BF16),
        "w2b": ctx.sbuf(tag + "_w2b", [128, 4096], BF16),
        "rtile": ctx.sbuf(tag + "_rtile", [128, 2048], BF16),
    }


def declare_l1_inputs(nc):
    hw = {}
    E = "ExternalInput"
    hw["l1_wq"] = nc.dram_tensor("l1_wq", [3, 8, 128, 1024], F32, kind=E).ap()
    hw["l1_wk"] = nc.dram_tensor("l1_wk", [3, 8, 128, 1024], F32, kind=E).ap()
    hw["l1_wv"] = nc.dram_tensor("l1_wv", [3, 8, 128, 1024], F32, kind=E).ap()
    hw["l1_wo"] = nc.dram_tensor("l1_wo", [3, 8, 128, 1024], F32, kind=E).ap()
    hw["l1_bq"] = nc.dram_tensor("l1_bq", [3, 128, 8], F32, kind=E).ap()
    hw["l1_bk"] = nc.dram_tensor("l1_bk", [3, 128, 8], F32, kind=E).ap()
    hw["l1_bo"] = nc.dram_tensor("l1_bo", [3, 128, 8], F32, kind=E).ap()
    hw["l1_bv_row"] = nc.dram_tensor("l1_bv_row", [3, 1, 1024], F32, kind=E).ap()
    hw["l1_om"] = nc.dram_tensor("l1_om", [3, 64, 32], F32, kind=E).ap()
    hw["l1_ln1g"] = nc.dram_tensor("l1_ln1g", [3, 128, 8], F32, kind=E).ap()
    hw["l1_ln1b"] = nc.dram_tensor("l1_ln1b", [3, 128, 8], F32, kind=E).ap()
    hw["l1_ln2g"] = nc.dram_tensor("l1_ln2g", [3, 128, 8], F32, kind=E).ap()
    hw["l1_ln2b"] = nc.dram_tensor("l1_ln2b", [3, 128, 8], F32, kind=E).ap()
    hw["l1_w1"] = nc.dram_tensor("l1_w1", [3, 8, 128, 4096], F32, kind=E).ap()
    hw["l1_bf1"] = nc.dram_tensor("l1_bf1", [3, 128, 32], F32, kind=E).ap()
    hw["l1_w2"] = nc.dram_tensor("l1_w2", [3, 32, 128, 1024], F32, kind=E).ap()
    hw["l1_bf2"] = nc.dram_tensor("l1_bf2", [3, 128, 8], F32, kind=E).ap()
    hw["l1f_w1"] = nc.dram_tensor("l1f_w1", [8, 128, 4096], F32, kind=E).ap()
    hw["l1f_b1"] = nc.dram_tensor("l1f_b1", [128, 32], F32, kind=E).ap()
    hw["l1f_w2"] = nc.dram_tensor("l1f_w2", [32, 128, 1024], F32, kind=E).ap()
    hw["l1f_b2"] = nc.dram_tensor("l1f_b2", [128, 8], F32, kind=E).ap()
    hw["l1f_gamma"] = nc.dram_tensor("l1f_gamma", [128, 8], F32, kind=E).ap()
    hw["l1f_beta"] = nc.dram_tensor("l1f_beta", [128, 8], F32, kind=E).ap()
    hw["meta_wpref"] = nc.dram_tensor("meta_wpref", [1, 8], F32, kind=E).ap()
    hw["x_spill"] = nc.dram_tensor("x_spill", [128, NFC * T], F32).ap()
    hw["adram"] = nc.dram_tensor("adram", [128, NFC * T], BF16).ap()
    hw["cc_st_in"] = nc.dram_tensor("cc_st_in", [128, 520], F32).ap()
    hw["cc_st_out"] = nc.dram_tensor("cc_st_out", [512, 520], F32).ap()
    return hw


def cols8(v):
    return np.ascontiguousarray(np.asarray(v).reshape(8, 128).T)


def prep_l1_inputs(inputs):
    _ = inputs["L1_ffn_router_w"]
    sh = {}
    sh["l1_wq"] = np.ascontiguousarray(inputs["L1_attn_Wq"].reshape(3, 8, 128, 1024))
    sh["l1_wk"] = np.ascontiguousarray(inputs["L1_attn_Wk"].reshape(3, 8, 128, 1024))
    sh["l1_wv"] = np.ascontiguousarray(inputs["L1_attn_Wv"].reshape(3, 8, 128, 1024))
    sh["l1_wo"] = np.ascontiguousarray(inputs["L1_attn_Wo"].reshape(3, 8, 128, 1024))
    sh["l1_bq"] = np.stack([cols8(inputs["L1_attn_bq"][l]) for l in range(3)])
    sh["l1_bk"] = np.stack([cols8(inputs["L1_attn_bk"][l]) for l in range(3)])
    sh["l1_bo"] = np.stack([cols8(inputs["L1_attn_bo"][l]) for l in range(3)])
    sh["l1_bv_row"] = np.ascontiguousarray(inputs["L1_attn_bv"].reshape(3, 1, 1024))
    sh["l1_om"] = np.ascontiguousarray(inputs["L1_attn_omega"])
    sh["l1_ln1g"] = np.stack([cols8(inputs["L1_attn_ln1_g"][l]) for l in range(3)])
    sh["l1_ln1b"] = np.stack([cols8(inputs["L1_attn_ln1_b"][l]) for l in range(3)])
    sh["l1_ln2g"] = np.stack([cols8(inputs["L1_attn_ln2_g"][l]) for l in range(3)])
    sh["l1_ln2b"] = np.stack([cols8(inputs["L1_attn_ln2_b"][l]) for l in range(3)])
    sh["l1_w1"] = np.ascontiguousarray(inputs["L1_attn_W1"].reshape(3, 8, 128, 4096))
    sh["l1_bf1"] = np.stack([np.ascontiguousarray(
        inputs["L1_attn_bf1"][l].reshape(32, 128).T) for l in range(3)])
    sh["l1_w2"] = np.ascontiguousarray(inputs["L1_attn_W2"].reshape(3, 32, 128, 1024))
    sh["l1_bf2"] = np.stack([cols8(inputs["L1_attn_bf2"][l]) for l in range(3)])
    sh["l1f_w1"] = np.ascontiguousarray(inputs["L1_ffn_w1"][0].reshape(8, 128, 4096))
    sh["l1f_b1"] = np.ascontiguousarray(inputs["L1_ffn_b1"][0].reshape(32, 128).T)
    sh["l1f_w2"] = np.ascontiguousarray(inputs["L1_ffn_w2"][0].reshape(32, 128, 1024))
    sh["l1f_b2"] = cols8(inputs["L1_ffn_b2"][0])
    sh["l1f_gamma"] = cols8(inputs["L1_ffn_gamma"])
    sh["l1f_beta"] = cols8(inputs["L1_ffn_beta"])
    return sh


LN8 = math.log(8.0)


def emit_l1_ffn(ctx, com, hw, glob):
    P = glob["P"]
    xbuf, xn, gdram = glob["xbuf"], glob["xn"], glob["gdram"]
    sd = ctx.sem("m1d")
    sv = ctx.sem("m1v")
    ss = ctx.sem("m1s")
    sgd = ctx.sem("m1gd")
    with ctx.scope():
        bufs = mlp_bufs(ctx, "l1f")
        acc = ctx.sbuf("l1f_acc", [128, 8 * 1024], F32)
        stage = ctx.sbuf("l1f_stage", [128, 8 * 512], BF16)
        b1c = ctx.sbuf("l1f_b1c", [128, 32], F32)
        b2c = ctx.sbuf("l1f_b2c", [128, 8], F32)
        gmc = ctx.sbuf("l1f_gmc", [128, 8], F32)
        btc = ctx.sbuf("l1f_btc", [128, 8], F32)
        gtile = ctx.sbuf("l1f_gtile", [128, 512], F32)
        ctx.dma("sync", b1c[:], hw["l1f_b1"], sd)
        ctx.dma("sync", b2c[:], hw["l1f_b2"], sd)
        ctx.dma("sync", gmc[:], hw["l1f_gamma"], sd)
        ctx.dma("sync", btc[:], hw["l1f_beta"], sd)
        ctx.op("vector", lambda e: None, wait=[(sd, None)])
        for th in range(2):
            emit_mlp_half(ctx, com, xn, hw["l1f_w1"], b1c, hw["l1f_w2"],
                          th, acc, bufs, P)
            for oc in range(NFC):
                for tq in range(2):
                    asl = slice(oc * 1024 + tq * 512, oc * 1024 + (tq + 1) * 512)
                    xsl = slice(oc * T + th * 1024 + tq * 512,
                                oc * T + th * 1024 + (tq + 1) * 512)
                    ctx.op("vector", lambda e, a=acc[:, asl], oc=oc:
                           e.tensor_scalar(a, a, b2c[:, oc:oc + 1], None,
                                           ALU.add), inc=sv)
                    ctx.op("vector", lambda e, a=acc[:, asl], x=xbuf[:, xsl]:
                           e.tensor_add(a, a, x), inc=sv)
            for q_ in range(2):
                with ctx.scope():
                    av = acc[:, q_ * 512:]
                    emit_ln(ctx, com, av, stage, P[0:1, 0:4096], P[:, 0:1024],
                            out_f32=av, tag="l1fln", gcols=gmc, bcols=btc,
                            Tn=512, cs=1024)
            for oc in range(NFC):
                for tq in range(2):
                    asl = slice(oc * 1024 + tq * 512, oc * 1024 + (tq + 1) * 512)
                    gsl = slice(oc * T + th * 1024 + tq * 512,
                                oc * T + th * 1024 + (tq + 1) * 512)
                    ctx.op("scalar", lambda e, a=acc[:, asl]:
                           e.activation(gtile[:], a, AF.Silu),
                           wait=[(sv, None), (sgd, None)], inc=ss)
                    ctx.dma("sync", gdram[:, gsl], gtile[:], sgd,
                            wait=[(ss, None)])
        ctx.op("sync", lambda e: None, wait=[(sgd, None)])


def emit_l1_sub(ctx, com, hw, glob, l):
    P = glob["P"]
    xbuf, xn = glob["xbuf"], glob["xn"]
    sd = ctx.sem("s1d")
    sv = ctx.sem("s1v")
    ss = ctx.sem("s1s")
    st = ctx.sem("s1t")
    scc = ctx.sem("s1cc")
    adram = hw["adram"]

    with ctx.scope():
        QB = ctx.sbuf("s1_qb", [128, NFC * T], BF16)
        KB = ctx.sbuf("s1_kb", [128, NFC * T], BF16)
        VT = ctx.sbuf("s1_vt", [128, 16 * 1040], BF16)
        omb2 = ctx.sbuf("s1_omb2", [128, 64], BF16)
        negc = ctx.sbuf("s1_negc", [128, 1], BF16)
        sinf2 = ctx.sbuf("s1_sinf2", [64, 1040], BF16)

        if True:
            pass
        with ctx.scope():
            WS = ctx.sbuf("s1_ws", [128, 1024], F32)
            WBmc = ctx.sbuf("s1_wbmc", [128, 8 * 128], BF16)
            bcol = ctx.sbuf("s1_bcol", [128, 8], F32)
            omf = ctx.sbuf("s1_omf", [128, 64], F32)
            for hb in range(2):
                ctx.dma("sync", omf[64 * hb:64 * (hb + 1), 0:32],
                        hw["l1_om"][l], sd)
            ctx.op("scalar", lambda e: e.mul(omf[:, 32:64], omf[:, 0:32],
                                             -(64.0 ** -0.25)),
                   wait=[(sd, None)], inc=ss)
            ctx.op("scalar", lambda e: e.mul(omf[:, 0:32], omf[:, 0:32],
                                             64.0 ** -0.25), inc=ss)
            ctx.op("vector", lambda e: e.tensor_copy(omb2[:], omf[:]),
                   wait=[(ss, None)], inc=sv)
            ctx.op("gpsimd", lambda e: e.memset(negc[:], -1.0 / 16.0))
            ctx.op("gpsimd", lambda e: e.memset(VT[:], 1.0))
            for (wname, bname, dst) in (("l1_wk", "l1_bk", KB),
                                        ("l1_wq", "l1_bq", QB)):
                ctx.dma("sync", bcol[:], hw[bname][l], sd, wait=[(sv, None)])
                for mc in range(NFC):
                    for kf in range(NFC):
                        ctx.dma("sync", WS[:, kf * 128:(kf + 1) * 128],
                                hw[wname][l, kf, :, mc * 128:(mc + 1) * 128],
                                sd, wait=[(sv, None)] if kf == 0 else ())
                    ctx.op("vector", lambda e: e.tensor_copy(WBmc[:], WS[:]),
                           wait=[(sd, None), (st, None)], inc=sv)
                    for tt in range(NTT):
                        pb = P[:, (tt % 2) * 512:(tt % 2 + 1) * 512]
                        for kf in range(NFC):
                            ctx.op("tensor",
                                   lambda e, ps=pb,
                                   l_=WBmc[:, kf * 128:(kf + 1) * 128],
                                   r=xn[:, kf * T + tt * 512:
                                        kf * T + (tt + 1) * 512],
                                   s0=(kf == 0), s1=(kf == NFC - 1):
                                   e.matmul(ps, l_, r, start=s0, stop=s1),
                                   wait=([(sv, None)] if kf == 0 else []),
                                   inc=(st if kf == NFC - 1 else None))
                        ctx.op("vector",
                               lambda e, ps=pb, mc=mc,
                               o=dst[:, mc * T + tt * 512:mc * T + (tt + 1) * 512]:
                               e.tensor_scalar(o, ps[:], bcol[:, mc:mc + 1],
                                               None, ALU.add),
                               wait=[(st, None)], inc=sv)
        with ctx.scope():
            WS = ctx.sbuf("s1v_ws", [128, 512], F32)
            WBn = ctx.sbuf("s1v_wbn", [128, 4 * 512], BF16)
            bvb = ctx.sbuf("s1v_bvb", [1, 1024], BF16)
            for hf in range(2):
                ctx.dma("sync", WS[0:1, 0:512],
                        hw["l1_bv_row"][l, 0:1, hf * 512:(hf + 1) * 512], sd,
                        wait=[(sv, None)])
                ctx.op("vector", lambda e, hf=hf: e.tensor_copy(
                    bvb[0:1, hf * 512:(hf + 1) * 512], WS[0:1, 0:512]),
                    wait=[(sd, None)], inc=sv)
            for nn in range(4):
                for kf in range(NFC):
                    ctx.dma("sync", WS[:, 0:256],
                            hw["l1_wv"][l, kf, :, nn * 256:(nn + 1) * 256],
                            sd, wait=[(sv, None)])
                    ctx.op("vector", lambda e, kf=kf: e.tensor_copy(
                        WBn[:, kf * 256:(kf + 1) * 256], WS[:, 0:256]),
                        wait=[(sd, None), (st, None)], inc=sv)
                for tc in range(NTC):
                    tsl = slice(tc * 128, (tc + 1) * 128)
                    for kf in range(NFC):
                        ctx.op("tensor",
                               lambda e, ps=P[:, 0:256],
                               l_=xn[:, kf * T:(kf + 1) * T][:, tsl],
                               r=WBn[:, kf * 256:(kf + 1) * 256],
                               s0=(kf == 0):
                               e.matmul(ps, l_, r, start=s0, stop=False),
                               wait=([(sv, None)] if kf == 0 else []))
                    ctx.op("tensor",
                           lambda e, ps=P[:, 0:256], nn=nn,
                           r=bvb[0:1, nn * 256:(nn + 1) * 256]:
                           e.matmul(ps, com.ones_row[:], r, start=False,
                                    stop=True), inc=st)
                    for h4 in range(4):
                        h = nn * 4 + h4
                        ctx.op("vector",
                               lambda e, h=h, h4=h4, tc=tc:
                               e.tensor_copy(
                                   VT[:, tc * 1040 + h * 65:
                                      tc * 1040 + h * 65 + 64],
                                   P[:, h4 * 64:(h4 + 1) * 64]),
                               wait=[(st, None)], inc=sv)
            ctx.barrier()

        with ctx.scope():
            kfb = ctx.sbuf("s1_kfb", [128, 64], BF16)
            kfbias = ctx.sbuf("s1_kfbias", [128, 1], F32)
            kff = ctx.sbuf("s1_kff", [128, 64], F32)
            kstage = ctx.sbuf("s1_kstage", [128, 128], BF16)
            kffm = ctx.sbuf("s1_kffm", [64, 128], BF16)
            qfb = ctx.sbuf("s1_qfb", [64, 128], BF16)
            atb = ctx.sbuf("s1_atb", [128, 128], BF16)
            stb = ctx.sbuf("s1_stb", [64, 65], BF16)
            stf32 = ctx.sbuf("s1_stf32", [64, 65], F32)
            stf32c = ctx.sbuf("s1_stf32c", [64, 65], F32)
            dcol = ctx.sbuf("s1_dcol", [128, 1], F32)
            rcol = ctx.sbuf("s1_rcol", [128, 1], F32)
            abuf = ctx.sbuf("s1_abuf", [128, 512], F32)
            afm = ctx.sbuf("s1_afm", [64, 128], BF16)
            stot = ctx.sbuf("s1_stot", [128, 520], F32)
            sacc = ctx.sbuf("s1_sacc", [128, 520], F32)
            wcol = ctx.sbuf("s1_wcol", [128, 8], F32)
            wrow = ctx.sbuf("s1_wrow", [1, 8], F32)

            def emit_kf(h, tc):
                hp, hb = divmod(h, 2)
                tslq = slice(hp * T + tc * 128, hp * T + (tc + 1) * 128)
                ctx.op("scalar", lambda e, hb=hb, tslq=tslq:
                       e.activation(kstage[64 * hb:64 * (hb + 1), :],
                                    KB[64 * hb:64 * (hb + 1), tslq],
                                    AF.Square),
                       wait=[(st, None)], inc=ss)
                ctx.op("tensor", lambda e, hb=hb:
                       e.matmul(P[:, 2112:2113],
                                kstage[64 * hb:64 * (hb + 1), :],
                                negc[64 * hb:64 * (hb + 1), :],
                                start=True, stop=True),
                       wait=[(ss, None)], inc=st)
                ctx.op("vector", lambda e:
                       e.tensor_scalar(kfbias[:], P[:, 2112:2113], -LN8,
                                       None, ALU.add),
                       wait=[(st, None)], inc=sv)
                ctx.op("tensor", lambda e, hb=hb, tslq=tslq:
                       e.matmul(P[:, 2048:2112],
                                KB[64 * hb:64 * (hb + 1), tslq],
                                omb2[64 * hb:64 * (hb + 1), :],
                                start=True, stop=True),
                       wait=[(ss, None)], inc=st)
                ctx.op("scalar", lambda e:
                       e.activation(kff[:, 0:64], P[:, 2048:2112], AF.Exp,
                                    bias=kfbias[:]),
                       wait=[(st, None), (sv, None)], inc=ss)
                ctx.op("vector", lambda e: e.tensor_copy(kfb[:], kff[:]),
                       wait=[(ss, None)], inc=sv)

            for h in (range(16) if "a0" in L1_PARTS else []):
                for tc in range(NTC):
                    emit_kf(h, tc)
                    ctx.op("tensor", lambda e, h=h, tc=tc:
                           e.matmul(P[0:64, 1024:1089], kfb[:, 0:64],
                                    VT[:, tc * 1040 + h * 65:
                                       tc * 1040 + (h + 1) * 65],
                                    start=(tc == 0), stop=(tc == NTC - 1)),
                           wait=[(sv, None)], inc=st)
                ctx.op("vector", lambda e, h=h:
                       e.tensor_copy(stot[64 * (h % 2):64 * (h % 2) + 64,
                                          (h // 2) * 65:(h // 2) * 65 + 65],
                                     P[0:64, 1024:1089]),
                       wait=[(st, None)], inc=sv)
            if "a0" in L1_PARTS:
                sg1 = ctx.sem("s1g")
                ctx.dma("gpsimd", hw["cc_st_in"], stot[:], sg1, wait=[(sv, None)])
                ctx.op("gpsimd", lambda e: e.collective_compute(
                    "AllGather", ALU.bypass, replica_groups=GROUPS,
                    ins=[hw["cc_st_in"]], outs=[hw["cc_st_out"]]),
                    wait=[(sg1, None)], inc=scc)
            ctx.dma("sync", wrow[:], hw["meta_wpref"], sd)
            ctx.op("tensor", lambda e: e.matmul(
                P[:, 0:8], com.onesf_row[:], wrow[:], start=True, stop=True),
                wait=[(sd, None)], inc=st)
            ctx.op("vector", lambda e: e.tensor_copy(wcol[:], P[:, 0:8]),
                   wait=[(st, None)], inc=sv)
            sg1 = ctx.sem("s1g")
            for r in range(4):
                ctx.dma("gpsimd", stot[:],
                        hw["cc_st_out"][128 * r:128 * (r + 1), :], sg1,
                        wait=[(scc, None), (sv, None)])
                if r == 0:
                    ctx.op("vector", lambda e:
                           e.tensor_scalar(sacc[:], stot[:], wcol[:, 0:1],
                                           None, ALU.mult),
                           wait=[(sg1, None)], inc=sv)
                else:
                    ctx.op("vector", lambda e, r=r:
                           e.scalar_tensor_tensor(sacc[:], stot[:],
                                                  wcol[:, r:r + 1], sacc[:],
                                                  ALU.mult, ALU.add),
                           wait=[(sg1, None)], inc=sv)
            for h in range(16):
                ctx.op("vector", lambda e, h=h:
                       e.tensor_copy(sinf2[0:64, h * 65:(h + 1) * 65],
                                     sacc[64 * (h % 2):64 * (h % 2) + 64,
                                          (h // 2) * 65:(h // 2) * 65 + 65]),
                       inc=sv)

            LV = L1_A1_LEVEL
            for h in (range(L1_A1_HEADS) if "a1" in L1_PARTS else []):
                hp, hb = divmod(h, 2)
                ctx.op("vector", lambda e: e.memset(stf32[:], 0.0), inc=sv)
                for tc in range(NTC):
                    emit_kf(h, tc)
                    tslq = slice(hp * T + tc * 128, hp * T + (tc + 1) * 128)
                    if LV >= 1:
                        ctx.op("tensor", lambda e: e.transpose(
                            P[0:64, 2176:2304], kff[:], com.identf[:]),
                            wait=[(ss, None)], inc=st)
                        ctx.op("scalar", lambda e: e.copy(kffm[:],
                                                          P[0:64, 2176:2304]),
                               wait=[(st, None)], inc=ss)
                    if LV >= 2:
                        ctx.op("tensor", lambda e, hb=hb, tslq=tslq:
                               e.matmul(P[0:64, 2304:2432],
                                        omb2[64 * hb:64 * (hb + 1), :],
                                        QB[64 * hb:64 * (hb + 1), tslq],
                                        start=True, stop=True),
                               wait=[(ss, None)], inc=st)
                        ctx.op("scalar", lambda e: e.activation(
                            qfb[:], P[0:64, 2304:2432], AF.Exp),
                            wait=[(st, None)], inc=ss)
                    if LV >= 3:
                        ctx.op("tensor", lambda e: e.matmul(
                            P[:, 2560:2688], kffm[:], qfb[:],
                            start=True, stop=True),
                            wait=[(ss, None)], inc=st)
                        ctx.op("vector", lambda e: e.tensor_mul(
                            atb[:], P[:, 2560:2688], com.tril[:]),
                            wait=[(st, None)], inc=sv)
                    if LV >= 4:
                        ctx.op("vector", lambda e, h=h: e.tensor_add(
                            stf32c[:], stf32[:],
                            sinf2[0:64, h * 65:(h + 1) * 65]), inc=sv)
                        ctx.op("vector", lambda e: e.tensor_copy(
                            stb[:], stf32c[:]), inc=sv)
                    if LV >= 5:
                        ctx.op("tensor", lambda e: e.matmul(
                            P[:, 3072:3137], qfb[:], stb[:],
                            start=True, stop=False),
                            wait=[(sv, None), (ss, None)], inc=st)
                        ctx.op("tensor", lambda e, h=h, tc=tc: e.matmul(
                            P[:, 3072:3137], atb[:],
                            VT[:, tc * 1040 + h * 65:tc * 1040 + (h + 1) * 65],
                            start=False, stop=True),
                            wait=[(sv, None)], inc=st)
                    if LV >= 4:
                        ctx.op("tensor", lambda e, h=h, tc=tc: e.matmul(
                            P[0:64, 3584:3649], kfb[:],
                            VT[:, tc * 1040 + h * 65:tc * 1040 + (h + 1) * 65],
                            start=True, stop=True),
                            wait=[(sv, None)], inc=st)
                        ctx.op("vector", lambda e: e.tensor_add(
                            stf32[:], stf32[:], P[0:64, 3584:3649]),
                            wait=[(st, None)], inc=sv)
                    if LV >= 7:
                        ctx.op("vector", lambda e: e.tensor_scalar(
                            dcol[:], P[:, 3136:3137], 1e-6, None, ALU.add),
                            wait=[(st, None)], inc=sv)
                        ctx.op("vector", lambda e: e.reciprocal(rcol[:],
                                                                dcol[:]),
                               inc=sv)
                        ctx.op("vector", lambda e, tc=tc: e.tensor_scalar(
                            abuf[:, (tc % 8) * 64:(tc % 8 + 1) * 64],
                            P[:, 3072:3136], rcol[:], None, ALU.mult),
                            inc=sv)
                    if LV >= 8 and tc % 8 == 7:
                        for tci in range(tc - 7, tc + 1):
                            ctx.op("tensor", lambda e, tci=tci: e.transpose(
                                P[0:64, 2816:2944],
                                abuf[:, (tci % 8) * 64:(tci % 8 + 1) * 64],
                                com.identf[:]),
                                wait=([(sv, None)] if tci % 8 == 0
                                      else [(ss, None)]),
                                inc=st)
                            ctx.op("scalar", lambda e: e.copy(
                                afm[:], P[0:64, 2816:2944]),
                                wait=[(st, None), (sd, None)], inc=ss)
                            ctx.dma("sync",
                                    adram[64 * hb:64 * (hb + 1),
                                          hp * T + tci * 128:
                                          hp * T + (tci + 1) * 128],
                                    afm[:], sd, wait=[(ss, None)])
            ctx.barrier()

    if "wo" not in L1_PARTS:
        return
    with ctx.scope():
        WS = ctx.sbuf("s1o_ws", [128, 512], F32)
        WB = ctx.sbuf("s1o_wb", [128, 8 * 1024], BF16)
        bcol = ctx.sbuf("s1o_bcol", [128, 8], F32)
        astage = ctx.sbuf("s1o_astage", [128, 4096], BF16)
        stile = ctx.sbuf("s1o_stile", [128, 512], F32)
        g1c = ctx.sbuf("s1o_g1c", [128, 8], F32)
        b1c = ctx.sbuf("s1o_b1c", [128, 8], F32)
        ctx.dma("sync", bcol[:], hw["l1_bo"][l], sd)
        ctx.dma("sync", g1c[:], hw["l1_ln1g"][l], sd)
        ctx.dma("sync", b1c[:], hw["l1_ln1b"][l], sd)
        for kf in range(NFC):
            for hf in range(2):
                ctx.dma("sync", WS[:],
                        hw["l1_wo"][l, kf, :, hf * 512:(hf + 1) * 512],
                        sd, wait=[(sv, None)])
                ctx.op("vector", lambda e, kf=kf, hf=hf: e.tensor_copy(
                    WB[:, kf * 1024 + hf * 512:kf * 1024 + (hf + 1) * 512],
                    WS[:]),
                    wait=[(sd, None)], inc=sv)
        ctx.barrier()
        for tt in range(NTT):
            sg1 = ctx.sem("s1g")
            for kf in range(NFC):
                ctx.dma("gpsimd", astage[:, kf * 512:(kf + 1) * 512],
                        adram[:, kf * T + tt * 512:kf * T + (tt + 1) * 512],
                        sg1, wait=[(st, None)])
            for mc in range(NFC):
                pb = P[:, (mc % 2) * 512:(mc % 2 + 1) * 512]
                for kf in range(NFC):
                    ctx.op("tensor",
                           lambda e, ps=pb,
                           l_=WB[:, kf * 1024 + mc * 128:
                                 kf * 1024 + (mc + 1) * 128],
                           r=astage[:, kf * 512:(kf + 1) * 512],
                           s0=(kf == 0), s1=(kf == NFC - 1):
                           e.matmul(ps, l_, r, start=s0, stop=s1),
                           wait=([(ctx.sem("s1g"), None), (sv, None)]
                                 if kf == 0 else []),
                           inc=(st if kf == NFC - 1 else None))
                ctx.op("vector",
                       lambda e, ps=pb, mc=mc:
                       e.tensor_scalar(stile[:], ps[:], bcol[:, mc:mc + 1],
                                       None, ALU.add),
                       wait=[(st, None)], inc=sv)
                ctx.op("vector",
                       lambda e, mc=mc, tt=tt,
                       o=xbuf[:, mc * T + tt * 512:mc * T + (tt + 1) * 512]:
                       e.tensor_add(o, o, stile[:]), inc=sv)
        with ctx.scope():
            emit_ln(ctx, com, xbuf, xn, P[0:1, 0:4096], P[:, 0:1024],
                    out_f32=xbuf, tag="s1ln1", gcols=g1c, bcols=b1c)

    if "ffn" not in L1_PARTS:
        return
    with ctx.scope():
        bufs = mlp_bufs(ctx, "s1m")
        acc = ctx.sbuf("s1m_acc", [128, 8 * 1024], F32)
        b1c = ctx.sbuf("s1m_b1c", [128, 32], F32)
        b2c = ctx.sbuf("s1m_b2c", [128, 8], F32)
        ctx.dma("sync", b1c[:], hw["l1_bf1"][l], sd)
        ctx.dma("sync", b2c[:], hw["l1_bf2"][l], sd)
        ctx.op("vector", lambda e: None, wait=[(sd, None)])
        for th in range(2):
            emit_mlp_half(ctx, com, xn, hw["l1_w1"][l], b1c, hw["l1_w2"][l],
                          th, acc, bufs, P)
            for oc in range(NFC):
                for tq in range(2):
                    asl = slice(oc * 1024 + tq * 512, oc * 1024 + (tq + 1) * 512)
                    xsl = slice(oc * T + th * 1024 + tq * 512,
                                oc * T + th * 1024 + (tq + 1) * 512)
                    ctx.op("vector",
                           lambda e, a=acc[:, asl], x=xbuf[:, xsl], oc=oc:
                           e.scalar_tensor_tensor(x, a, b2c[:, oc:oc + 1], x,
                                                  ALU.add, ALU.add),
                           wait=[(ctx.sem("m1v"), None)], inc=sv)
    with ctx.scope():
        g2c = ctx.sbuf("s1m_g2c", [128, 8], F32)
        bb2c = ctx.sbuf("s1m_bb2c", [128, 8], F32)
        ctx.dma("sync", g2c[:], hw["l1_ln2g"][l], sd)
        ctx.dma("sync", bb2c[:], hw["l1_ln2b"][l], sd)
        ctx.op("vector", lambda e: None, wait=[(sd, None)])
        emit_ln(ctx, com, xbuf, xn, P[0:1, 0:4096], P[:, 0:1024],
                out_f32=xbuf, tag="s1ln2", gcols=g2c, bcols=bb2c)


L1_SUBS = [0, 1, 2]
L1_PARTS = {"proj", "a0", "a1", "wo", "ffn"}
L1_A1_LEVEL = 99
L1_A1_HEADS = 16


def emit_l1(ctx, com, hw, glob):
    P = glob["P"]
    gdram = glob["gdram"]
    sd = ctx.sem("l1d")
    sv = ctx.sem("l1v")
    with ctx.scope():
        glob["xbuf"] = ctx.sbuf("l1_xbuf", [128, NFC * T], F32)
        glob["xn"] = ctx.sbuf("l1_xn", [128, NFC * T], BF16)
        xbuf, xn = glob["xbuf"], glob["xn"]
        ctx.dma("sync", xbuf[:], glob["x_spill"], sd)
        for eng in ENGINES:
            ctx.op(eng, lambda e: None, wait=[(sd, None)])
        with ctx.scope():
            emit_ln(ctx, com, xbuf, xn, P[0:1, 0:4096], P[:, 0:1024],
                    out_f32=xbuf, tag="l1n")
        emit_l1_ffn(ctx, com, hw, glob)
        for l in L1_SUBS:
            emit_l1_sub(ctx, com, hw, glob, l)
        with ctx.scope():
            xt = ctx.sbuf("l1u_xt", [128, 512], F32)
            gt2 = ctx.sbuf("l1u_gt", [128, 512], F32)
            stile = ctx.sbuf("l1u_stile", [128, 512], F32)
            for f in range(NFC):
                for tt in range(NTT):
                    sl = slice(f * T + tt * 512, f * T + (tt + 1) * 512)
                    ctx.dma("sync", xt[:], glob["x_spill"][:, sl], sd,
                            wait=[(sv, None)])
                    ctx.dma("gpsimd", gt2[:], gdram[:, sl], ctx.sem("l1g"),
                            wait=[(sv, None)])
                    ctx.op("vector", lambda e, sl=sl:
                           e.tensor_mul(stile[:], xbuf[:, sl], gt2[:]),
                           wait=[(sd, None), (ctx.sem("l1g"), None)], inc=sv)
                    ctx.op("vector", lambda e, sl=sl:
                           e.tensor_add(xbuf[:, sl], xt[:], stile[:]), inc=sv)
        ctx.dma("sync", glob["x_spill"], glob["xbuf"][:], sd)
        ctx.op("sync", lambda e: None, wait=[(sd, None)])


def declare_l2_inputs(nc):
    hw = {}
    E = "ExternalInput"
    _wq = nc.dram_tensor("l2_wq", [8, 128, 1024], F32, kind=E)
    hw["l2_wq_h"] = _wq
    _wk = nc.dram_tensor("l2_wupk", [2, 128, 1024], F32, kind=E)
    hw["l2_wupk_h"] = _wk
    _wv = nc.dram_tensor("l2_wupv", [2, 128, 1024], F32, kind=E)
    hw["l2_wupv_h"] = _wv
    hw["l2_wdkv"] = nc.dram_tensor("l2_wdkv", [8, 128, 256], F32, kind=E).ap()
    hw["l2_wo"] = nc.dram_tensor("l2_wo", [8, 128, 1024], F32, kind=E).ap()
    hw["l2_wg"] = nc.dram_tensor("l2_wg", [128, 64], F32, kind=E).ap()
    hw["l2_we"] = nc.dram_tensor("l2_we", [8, 8, 128, 1024], F32, kind=E).ap()
    hw["l2_be"] = nc.dram_tensor("l2_be", [8, 1024], F32, kind=E).ap()
    hw["l2_theta"] = nc.dram_tensor("l2_theta", [1, 128], F32, kind=E).ap()
    hw["l2_pos"] = nc.dram_tensor("l2_pos", [1, 8192], F32, kind=E).ap()
    hw["l2_off"] = nc.dram_tensor("l2_off", [1, 8], I32, kind=E).ap()
    hw["gt2_dram"] = nc.dram_tensor("gt2_dram", [8, T], F32).ap()
    hw["cc_x_in"] = nc.dram_tensor("cc_x_in", [128, 16384], BF16).ap()
    hw["cc_x_out"] = nc.dram_tensor("cc_x_out", [512, 16384], BF16).ap()
    hw["cc_c_in"] = nc.dram_tensor("cc_c_in", [128, 4096], BF16).ap()
    hw["cc_c_out"] = nc.dram_tensor("cc_c_out", [512, 4096], BF16).ap()
    hw["cc_o_in"] = nc.dram_tensor("cc_o_in", [128, 16384], BF16).ap()
    hw["wq_stage"] = nc.dram_tensor("wq_stage", [1024, 256], F32).ap()
    hw["wk_stage"] = nc.dram_tensor("wk_stage", [256, 256], F32).ap()
    hw["wv_stage"] = nc.dram_tensor("wv_stage", [256, 256], F32).ap()
    hw["a_stage"] = nc.dram_tensor("a_stage", [512, 4096], BF16).ap()
    _co = nc.dram_tensor("cc_o_out", [512, 16384], BF16)
    hw["cc_o_out"] = _co.ap()
    hw["cc_o_out_h"] = _co
    return hw


def prep_l2_inputs(inputs):
    sh = {}
    perm = np.zeros(1024, np.int64)
    for h in range(16):
        perm[h * 64:h * 64 + 32] = h * 64 + np.arange(0, 64, 2)
        perm[h * 64 + 32:h * 64 + 64] = h * 64 + np.arange(1, 64, 2)
    wq = np.asarray(inputs["L2_attn_W_Q"])[:, perm]
    sh["l2_wq"] = np.ascontiguousarray(wq.reshape(8, 128, 1024))
    wk = np.asarray(inputs["L2_attn_W_up_K"])[:, perm]
    sh["l2_wupk"] = np.ascontiguousarray(wk.reshape(2, 128, 1024))
    sh["l2_wupv"] = np.ascontiguousarray(
        inputs["L2_attn_W_up_V"].reshape(2, 128, 1024))
    sh["l2_wdkv"] = np.ascontiguousarray(
        inputs["L2_attn_W_down_kv"].reshape(8, 128, 256))
    sh["l2_wo"] = np.ascontiguousarray(
        inputs["L2_attn_W_O"].reshape(8, 128, 1024))
    wg = np.asarray(inputs["L2_ffn_Wg"])
    sh["l2_wg"] = np.ascontiguousarray(
        wg.reshape(8, 128, 8).transpose(1, 0, 2).reshape(128, 64))
    sh["l2_we"] = np.ascontiguousarray(
        inputs["L2_ffn_We"].reshape(8, 8, 128, 1024))
    sh["l2_be"] = np.ascontiguousarray(inputs["L2_ffn_be"])
    th = 10000.0 ** (-np.arange(32, dtype=np.float64) / 32.0)
    tc = np.zeros((1, 128), np.float32)
    for blk in range(4):
        sgn = -1.0 if blk % 2 == 0 else 1.0
        tc[0, blk * 32:(blk + 1) * 32] = sgn * th
    sh["l2_theta"] = tc
    sh["l2_pos"] = np.arange(8192, dtype=np.float32).reshape(1, 8192)
    return sh


def l2_core_meta(c):
    j = c % 4
    off = np.array([[j * 256, j * 256 + 128, j * 256,
                     j * 2048, j * 2048 + 8192, 0, 0, 0]], np.int32)
    return {"l2_off": off}


def emit_l2_gates(ctx, com, hw, xn_bf, gt_dram, sd, P):
    gtc = ctx.sbuf("l2_gtc", [1, 1024], F32)
    sv = ctx.sem("f2v")
    ss = ctx.sem("f2s")
    st = ctx.sem("f2t")
    wgst = ctx.sbuf("l2_wgst", [128, 64], F32)
    wgb = ctx.sbuf("l2_wgb", [128, 64], BF16)
    gs = ctx.sbuf("l2_gs", [128, 40], F32)
    m1 = ctx.sbuf("l2_m1", [128, 4], F32)
    pg = P[:, 0:8]
    pgt = P[0:1, 512:640]
    ctx.dma("sync", wgst[:], hw["l2_wg"], sd)
    ctx.op("vector", lambda e: e.tensor_copy(wgb[:], wgst[:]),
           wait=[(sd, None)], inc=sv)
    for tc in range(NTC):
        for f in range(NFC):
            ctx.op("tensor",
                   lambda e, l=xn_bf[:, f * T + tc * 128:f * T + (tc + 1) * 128],
                   r=wgb[:, f * 8:(f + 1) * 8], s0=(f == 0), s1=(f == NFC - 1):
                   e.matmul(pg[:, 0:8], l, r, start=s0, stop=s1),
                   wait=([(sv, None)] if f == 0 else []),
                   inc=(st if f == NFC - 1 else None))
        ctx.op("vector", lambda e: e.reduce_max(m1[:, 0:1], pg[:, 0:8],
                                                axis=AX),
               wait=[(st, None)], inc=sv)
        ctx.op("vector", lambda e: e.tensor_scalar(
            gs[:, 0:8], pg[:, 0:8], m1[:, 0:1], None, ALU.is_ge), inc=sv)
        ctx.op("vector", lambda e: e.scalar_tensor_tensor(
            gs[:, 8:16], gs[:, 0:8], -1e9, pg[:, 0:8], ALU.mult, ALU.add),
            inc=sv)
        ctx.op("vector", lambda e: e.reduce_max(m1[:, 1:2], gs[:, 8:16],
                                                axis=AX), inc=sv)
        ctx.op("vector", lambda e: e.tensor_scalar(
            gs[:, 16:24], gs[:, 8:16], m1[:, 1:2], None, ALU.is_ge), inc=sv)
        ctx.op("vector", lambda e: e.tensor_sub(m1[:, 2:3], m1[:, 0:1],
                                                m1[:, 1:2]), inc=sv)
        ctx.op("scalar", lambda e: e.activation(m1[:, 3:4], m1[:, 2:3],
                                                AF.Sigmoid),
               wait=[(sv, None)], inc=ss)
        ctx.op("vector", lambda e: e.tensor_scalar(
            m1[:, 2:3], m1[:, 3:4], -1.0, 1.0, ALU.mult, ALU.add),
            wait=[(ss, None)], inc=sv)
        ctx.op("vector", lambda e: e.tensor_scalar(
            gs[:, 24:32], gs[:, 0:8], m1[:, 3:4], None, ALU.mult), inc=sv)
        ctx.op("vector", lambda e: e.scalar_tensor_tensor(
            gs[:, 32:40], gs[:, 16:24], m1[:, 2:3], gs[:, 24:32],
            ALU.mult, ALU.add), inc=sv)
        for e_ in range(8):
            ctx.op("tensor", lambda e, e_=e_: e.transpose(
                pgt[0:1, 0:128], gs[:, 32 + e_:33 + e_], com.identf[:]),
                wait=[(sv, None)], inc=st)
            ctx.op("vector",
                   lambda e, o=gtc[0:1, e_ * 128:(e_ + 1) * 128]:
                   e.tensor_copy(o, pgt[0:1, 0:128]),
                   wait=[(st, None)] + ([(sd, None)] if e_ == 0 else []),
                   inc=sv)
        ctx.dma("sync", gt_dram[:, tc * 128:(tc + 1) * 128],
                gtc[0:1, :], sd, wait=[(sv, None)])
    ctx.op("sync", lambda e: None, wait=[(sd, None)])
    ctx.barrier()


def emit_l2_ffn(ctx, com, hw, xn_bf, gdram, P):
    sd = ctx.sem("f2d")
    sv = ctx.sem("f2v")
    ss = ctx.sem("f2s")
    st = ctx.sem("f2t")
    sgd = ctx.sem("f2gd")
    gt_dram = hw["gt2_dram"]
    with ctx.scope():
        emit_l2_gates(ctx, com, hw, xn_bf, gt_dram, sd, P)
    with ctx.scope():
        grow = ctx.sbuf("l2_grow", [1, T], F32)
        gtsb = ctx.sbuf("l2_gtsb", [8, T], F32)
        gtb = ctx.sbuf("l2_gtb", [8, T], BF16)
        besb = ctx.sbuf("l2_besb", [8, 1024], F32)
        beb = ctx.sbuf("l2_beb", [8, 1024], BF16)
        acc = ctx.sbuf("l2_acc", [128, NFC * T], F32)
        xg = ctx.sbuf("l2_xg", [128, NFC * T], BF16)
        gb = ctx.sbuf("l2_gb", [128, 512], F32)
        gbb = ctx.sbuf("l2_gbb", [128, T], BF16)
        WS2 = ctx.sbuf("l2_ws", [128, 256], F32)
        web = ctx.sbuf("l2_web", [128, 8 * 128], BF16)
        gtile = ctx.sbuf("l2_gtile", [128, 512], F32)
        pw = [P[:, i * 512:(i + 1) * 512] for i in range(2)]
        ctx.dma("sync", gtsb[:], gt_dram, sd)
        ctx.dma("sync", besb[:], hw["l2_be"], sd)
        ctx.op("vector", lambda e: e.tensor_copy(gtb[:], gtsb[:]),
               wait=[(sd, None)], inc=sv)
        ctx.op("vector", lambda e: e.tensor_copy(beb[:], besb[:]), inc=sv)
        for e_ in range(8):
            ctx.dma("sync", grow[:], gt_dram[e_:e_ + 1, :], sd,
                    wait=[(sv, None)])
            for q in range(NTT):
                ctx.op("tensor",
                       lambda e, ps=pw[q % 2], q=q,
                       r=grow[0:1, q * 512:(q + 1) * 512]:
                       e.matmul(ps, com.onesf_row[:], r, start=True,
                                stop=True),
                       wait=[(sv, None), (sd, None)], inc=st)
                ctx.op("vector", lambda e, ps=pw[q % 2], q=q: e.tensor_copy(
                    gbb[:, q * 512:(q + 1) * 512], ps[:]),
                    wait=[(st, None)], inc=sv)
            for f in range(NFC):
                ctx.op("vector",
                       lambda e, o=xg[:, f * T:(f + 1) * T],
                       i=xn_bf[:, f * T:(f + 1) * T]:
                       e.tensor_mul(o, i, gbb[:]), inc=sv)
            sq2 = ctx.sem("f2q")
            for mc in range(NFC):
                for kf in range(NFC):
                    dma_q(ctx, kf % 2, WS2[:, (kf % 2) * 128:
                                           (kf % 2) * 128 + 128],
                          hw["l2_we"][e_, kf, :, mc * 128:(mc + 1) * 128],
                          sd, sq2, wait=[(sv, None)])
                    ctx.op("vector", lambda e, kf=kf: e.tensor_copy(
                        web[:, kf * 128:(kf + 1) * 128],
                        WS2[:, (kf % 2) * 128:(kf % 2) * 128 + 128]),
                        wait=[(sd, None), (sq2, None), (st, None)], inc=sv)
                for tt in range(NTT):
                    pb = pw[tt % 2]
                    for kf in range(NFC):
                        ctx.op("tensor",
                               lambda e, ps=pb,
                               l=web[:, kf * 128:(kf + 1) * 128],
                               r=xg[:, kf * T + tt * 512:
                                    kf * T + (tt + 1) * 512],
                               s0=(kf == 0),
                               s1=(kf == NFC - 1 and e_ != 7):
                               e.matmul(ps, l, r, start=s0, stop=s1),
                               wait=([(sv, None)] if kf == 0 else []),
                               inc=(st if (kf == NFC - 1 and e_ != 7)
                                    else None))
                    if e_ == 7:
                        ctx.op("tensor",
                               lambda e, ps=pb,
                               l=beb[0:8, mc * 128:(mc + 1) * 128],
                               r=gtb[0:8, tt * 512:(tt + 1) * 512]:
                               e.matmul(ps, l, r, start=False, stop=True),
                               inc=st)
                    sl = slice(mc * T + tt * 512, mc * T + (tt + 1) * 512)
                    if e_ == 0:
                        ctx.op("vector", lambda e, ps=pb, sl=sl:
                               e.tensor_copy(acc[:, sl], ps[:]),
                               wait=[(st, None)], inc=sv)
                    else:
                        ctx.op("vector", lambda e, ps=pb, sl=sl:
                               e.tensor_add(acc[:, sl], acc[:, sl], ps[:]),
                               wait=[(st, None)], inc=sv)
        for mc in range(NFC):
            for tt in range(NTT):
                sl = slice(mc * T + tt * 512, mc * T + (tt + 1) * 512)
                ctx.op("scalar", lambda e, sl=sl: e.activation(
                    gtile[:], acc[:, sl], AF.Silu),
                    wait=[(sv, None), (sgd, None)], inc=ss)
                ctx.dma("sync", gdram[:, sl], gtile[:], sgd,
                        wait=[(ss, None)])
        ctx.op("sync", lambda e: None, wait=[(sgd, None), (sd, None)])


def emit_l2(ctx, com, hw, glob):
    P = glob["P"]
    gdram = glob["gdram"]
    x_spill = glob["x_spill"]
    sd = ctx.sem("a2d")
    sdo = ctx.sem("a2o")
    sg = ctx.sem("a2g")
    sv = ctx.sem("a2v")
    ss = ctx.sem("a2s")
    st = ctx.sem("a2t")
    sp = ctx.sem("a2p")
    scc = ctx.sem("a2cc")
    BK = [P[:, k * 512:(k + 1) * 512] for k in range(8)]
    C1 = 6.28125
    C2 = 0.0019353071795864769
    TWO_PI = C1 + C2
    PI = math.pi

    with ctx.scope():
        with ctx.scope():
            xn = ctx.sbuf("l2_xn", [128, NFC * T], BF16)
            with ctx.scope():
                xtmp = ctx.sbuf("l2_xtmp", [128, NFC * T], F32)
                ctx.dma("sync", xtmp[:], x_spill, sd)
                for eng in ENGINES:
                    ctx.op(eng, lambda e: None, wait=[(sd, None)])
                emit_ln(ctx, com, xtmp, xn, P[0:1, 0:4096], P[:, 0:1024],
                        tag="l2n")
            emit_l2_ffn(ctx, com, hw, xn, gdram, P)
            with ctx.scope():
                WS = ctx.sbuf("l2a_ws", [128, 256], F32)
                WB = ctx.sbuf("l2a_wb", [128, 8 * 256], BF16)
                CT = ctx.sbuf("l2a_ct", [128, 2 * T], BF16)
                ctx.dma("sync", hw["cc_x_in"], xn[:], sdo)
                for kf in range(NFC):
                    ctx.dma("sync", WS[:], hw["l2_wdkv"][kf, :, :], sd,
                            wait=[(sv, None)])
                    ctx.op("vector", lambda e, kf=kf: e.tensor_copy(
                        WB[:, kf * 256:(kf + 1) * 256], WS[:]),
                        wait=[(sd, None), (st, None)], inc=sv)
                for cc in range(2):
                    for tt in range(NTT):
                        pb = BK[2 + (tt % 2)]
                        for kf in range(NFC):
                            ctx.op("tensor",
                                   lambda e, ps=pb,
                                   l=WB[:, kf * 256 + cc * 128:
                                        kf * 256 + (cc + 1) * 128],
                                   r=xn[:, kf * T + tt * 512:
                                        kf * T + (tt + 1) * 512],
                                   s0=(kf == 0), s1=(kf == NFC - 1):
                                   e.matmul(ps, l, r, start=s0, stop=s1),
                                   wait=([(sv, None)] if kf == 0 else []),
                                   inc=(st if kf == NFC - 1 else None))
                        ctx.op("vector",
                               lambda e, ps=pb, cc=cc, tt=tt:
                               e.tensor_copy(
                                   CT[:, cc * T + tt * 512:
                                      cc * T + (tt + 1) * 512], ps[:]),
                               wait=[(st, None)], inc=sv)
                ctx.dma("sync", hw["cc_c_in"], CT[:], sdo,
                        wait=[(sv, None)])
                ctx.op("sync", lambda e: None,
                       wait=[(sdo, None), (sd, None)])
                ctx.barrier()
                ctx.op("gpsimd", lambda e: e.collective_compute(
                    "AllGather", ALU.bypass, replica_groups=GROUPS,
                    ins=[hw["cc_x_in"]], outs=[hw["cc_x_out"]]), inc=scc)
                ctx.op("gpsimd", lambda e: e.collective_compute(
                    "AllGather", ALU.bypass, replica_groups=GROUPS,
                    ins=[hw["cc_c_in"]], outs=[hw["cc_c_out"]]), inc=scc)
                for eng in ENGINES:
                    ctx.op(eng, lambda e: None, wait=[(scc, None)])

        with ctx.scope():
            OT = ctx.sbuf("l2_ot", [128, 2 * 8192], BF16)
            ofs = ctx.sbuf("l2_ofs", [1, 8], I32)
            ctx.dma("sync", ofs[:], hw["l2_off"], sd)
            ctx.op("gpsimd", lambda e: None, wait=[(sd, None)])

            def dstage_w(e, src, dst, nk):
                with e.register() as r1:
                    e.reg_load(r1, ofs[0:1, 0:1])
                    return e.dma_start(
                        out=dst,
                        in_=bass.AP(src, r1, [[131072, nk], [1024, 128],
                                              [1, 256]]))
            ctx.op("gpsimd", lambda e: dstage_w(e, hw["l2_wq_h"],
                                                hw["wq_stage"], 8),
                   inc=sg, incn=16)
            ctx.op("gpsimd", lambda e: dstage_w(e, hw["l2_wupk_h"],
                                                hw["wk_stage"], 2),
                   inc=sg, incn=16)
            ctx.op("gpsimd", lambda e: dstage_w(e, hw["l2_wupv_h"],
                                                hw["wv_stage"], 2),
                   inc=sg, incn=16)
            ctx.op("sync", lambda e: None, wait=[(sg, None)])
            with ctx.scope():
                QR = ctx.sbuf("l2_qr", [128, 8192], BF16)
                KR = ctx.sbuf("l2_kr", [128, 8192], BF16)
                VS = ctx.sbuf("l2_vs", [128, 2 * 64 * 65], BF16)
                XT = ctx.sbuf("l2_xt", [128, 8 * 512], BF16)
                CT2 = ctx.sbuf("l2_ct2", [128, 2 * 512], BF16)
                WQb = ctx.sbuf("l2_wqb", [128, 8 * 128], BF16)
                WKb = ctx.sbuf("l2_wkb", [128, 2 * 128], BF16)
                VWb = ctx.sbuf("l2_vwb", [128, 2 * 256], BF16)
                WS = ctx.sbuf("l2h_ws", [128, 256], F32)
                posr = ctx.sbuf("l2_posr", [1, 512], F32)
                ftile = ctx.sbuf("l2_ft", [128, 512], F32)
                nint = ctx.sbuf("l2_ni", [128, 512], I32)
                nf = ctx.sbuf("l2_nf", [128, 512], F32)
                rt = ctx.sbuf("l2_rt", [128, 512], F32)
                mk = ctx.sbuf("l2_mk", [128, 512], F32)
                Ctile = ctx.sbuf("l2_cos", [128, 512], F32)
                Stile = ctx.sbuf("l2_sin", [128, 512], F32)
                qtmp = ctx.sbuf("l2_qtmp", [128, 512], BF16)
                qsw = ctx.sbuf("l2_qsw", [128, 512], BF16)
                t1 = ctx.sbuf("l2_t1", [128, 512], F32)
                t2 = ctx.sbuf("l2_t2", [128, 512], F32)
                thc = ctx.sbuf("l2_thc", [1, 128], F32)
                Et = ctx.sbuf("l2_et", [128, 2 * 512], BF16)
                Em = ctx.sbuf("l2_em", [128, 512], BF16)
                denr = ctx.sbuf("l2_den", [1, 512], F32)
                rden = ctx.sbuf("l2_rden", [1, 512], F32)
                rbc = ctx.sbuf("l2_rbc", [64, 512], F32)
                ctx.dma("sync", thc[:], hw["l2_theta"], sd)
                for kc in range(2):
                    ctx.dma("sync", WS[:],
                            hw["wv_stage"][kc * 128:(kc + 1) * 128, :], sd,
                            wait=[(sv, None)])
                    ctx.op("vector", lambda e, kc=kc: e.tensor_copy(
                        VWb[:, kc * 256:(kc + 1) * 256], WS[:]),
                        wait=[(sd, None)], inc=sv)
                for ps_ in range(2):
                    for kf in range(NFC):
                        ctx.dma("sync", WS[:, 0:128],
                                hw["wq_stage"][kf * 128:(kf + 1) * 128,
                                               ps_ * 128:(ps_ + 1) * 128],
                                sd, wait=[(sv, None)])
                        ctx.op("vector", lambda e, kf=kf: e.tensor_copy(
                            WQb[:, kf * 128:(kf + 1) * 128], WS[:, 0:128]),
                            wait=[(sd, None), (st, None)], inc=sv)
                    for kc in range(2):
                        ctx.dma("sync", WS[:, 0:128],
                                hw["wk_stage"][kc * 128:(kc + 1) * 128,
                                               ps_ * 128:(ps_ + 1) * 128],
                                sd, wait=[(sv, None)])
                        ctx.op("vector", lambda e, kc=kc: e.tensor_copy(
                            WKb[:, kc * 128:(kc + 1) * 128], WS[:, 0:128]),
                            wait=[(sd, None), (st, None)], inc=sv)
                    ctx.op("vector", lambda e: e.memset(VS[:], 1.0),
                           wait=[(st, None)], inc=sv)
                    for tau in range(16):
                        r_ = tau // 4
                        lt = tau % 4
                        for kf in range(NFC):
                            ctx.dma("sync", XT[:, kf * 512:(kf + 1) * 512],
                                    hw["cc_x_out"][r_ * 128:(r_ + 1) * 128,
                                                   kf * T + lt * 512:
                                                   kf * T + (lt + 1) * 512],
                                    sd, wait=[(sv, None)] if kf == 0 else ())
                        for kc in range(2):
                            ctx.dma("sync", CT2[:, kc * 512:(kc + 1) * 512],
                                    hw["cc_c_out"][r_ * 128:(r_ + 1) * 128,
                                                   kc * T + lt * 512:
                                                   kc * T + (lt + 1) * 512],
                                    sd)
                        ctx.dma("sync", posr[:],
                                hw["l2_pos"][0:1, tau * 512:(tau + 1) * 512],
                                sd)
                        ctx.op("tensor", lambda e: e.matmul(
                            BK[5], thc[:], posr[:], start=True, stop=True),
                            wait=[(sd, None), (ss, None)], inc=st)
                        ctx.op("vector", lambda e: e.tensor_scalar(
                            ftile[:], BK[5][:], 1.0 / TWO_PI, None,
                            ALU.mult), wait=[(st, None)], inc=sv)
                        ctx.op("vector", lambda e: e.tensor_copy(
                            nint[:], ftile[:]), inc=sv)
                        ctx.op("vector", lambda e: e.tensor_copy(
                            nf[:], nint[:]), inc=sv)
                        ctx.op("vector", lambda e: e.scalar_tensor_tensor(
                            rt[:], nf[:], -C1, BK[5][:], ALU.mult, ALU.add),
                            inc=sv)
                        ctx.op("vector", lambda e: e.scalar_tensor_tensor(
                            rt[:], nf[:], -C2, rt[:], ALU.mult, ALU.add),
                            inc=sv)
                        ctx.op("vector", lambda e: e.tensor_scalar(
                            mk[:], rt[:], PI, None, ALU.is_ge), inc=sv)
                        ctx.op("vector", lambda e: e.scalar_tensor_tensor(
                            rt[:], mk[:], -TWO_PI, rt[:], ALU.mult,
                            ALU.add), inc=sv)
                        ctx.op("vector", lambda e: e.tensor_scalar(
                            mk[:], rt[:], -PI, None, ALU.is_ge), inc=sv)
                        ctx.op("vector", lambda e: e.tensor_scalar(
                            mk[:], mk[:], -1.0, 1.0, ALU.mult, ALU.add),
                            inc=sv)
                        ctx.op("vector", lambda e: e.scalar_tensor_tensor(
                            rt[:], mk[:], TWO_PI, rt[:], ALU.mult, ALU.add),
                            inc=sv)
                        ctx.op("scalar", lambda e: e.activation(
                            Stile[:], rt[:], AF.Sin), wait=[(sv, None)],
                            inc=ss)
                        ctx.op("scalar", lambda e: e.activation(
                            rt[:], rt[:], AF.Abs), inc=ss)
                        ctx.op("scalar", lambda e: e.activation(
                            Ctile[:], rt[:], AF.Sin, scale=-1.0,
                            bias=com.cst[:, 1:2]), inc=ss)
                        for kf in range(NFC):
                            ctx.op("tensor",
                                   lambda e, l=WQb[:, kf * 128:(kf + 1) * 128],
                                   r=XT[:, kf * 512:(kf + 1) * 512],
                                   s0=(kf == 0), s1=(kf == NFC - 1):
                                   e.matmul(BK[4], l, r, start=s0, stop=s1),
                                   wait=([(sv, None), (sd, None)]
                                         if kf == 0 else []),
                                   inc=(st if kf == NFC - 1 else None))
                        for src, dst, nmm in ((BK[4], QR, None),
                                              (None, KR, 2)):
                            if nmm == 2:
                                for kc in range(2):
                                    ctx.op("tensor",
                                           lambda e,
                                           l=WKb[:, kc * 128:(kc + 1) * 128],
                                           r=CT2[:, kc * 512:(kc + 1) * 512],
                                           s0=(kc == 0), s1=(kc == 1):
                                           e.matmul(BK[4], l, r, start=s0,
                                                    stop=s1),
                                           wait=[(sv, None)],
                                           inc=(st if kc == 1 else None))
                            ctx.op("vector", lambda e: e.tensor_copy(
                                qtmp[:], BK[4][:]),
                                wait=[(st, None)], inc=sv)
                            for bs in range(4):
                                a0 = bs * 32
                                b0 = (bs ^ 1) * 32
                                ctx.dma("gpsimd", qsw[b0:b0 + 32, :],
                                        qtmp[a0:a0 + 32, :], sg,
                                        wait=[(sv, None)] if bs == 0 else ())
                            ctx.op("vector", lambda e: e.tensor_mul(
                                t1[:], qtmp[:], Ctile[:]),
                                wait=[(ss, None)], inc=sv)
                            ctx.op("vector", lambda e: e.tensor_mul(
                                t2[:], qsw[:], Stile[:]),
                                wait=[(sg, None)], inc=sv)
                            ctx.op("vector",
                                   lambda e, dst=dst, tau=tau: e.tensor_add(
                                       dst[:, tau * 512:(tau + 1) * 512],
                                       t1[:], t2[:]), inc=sv)
                        for sub in range(4):
                            kt = tau * 4 + sub
                            for s_ in range(2):
                                for kc in range(2):
                                    ctx.op("tensor",
                                           lambda e, sub=sub, s_=s_, kc=kc,
                                           l=CT2[:, kc * 512 + sub * 128:
                                                kc * 512 + (sub + 1) * 128],
                                           r=VWb[:, kc * 256 + ps_ * 128 +
                                                s_ * 64:kc * 256 +
                                                ps_ * 128 + s_ * 64 + 64],
                                           s0=(kc == 0), s1=(kc == 1):
                                           e.matmul(BK[6][:, 0:64], l, r,
                                                    start=s0, stop=s1),
                                           wait=([(sv, None)]
                                                 if kc == 0 else []),
                                           inc=(st if kc == 1 else None))
                                ctx.op("vector",
                                       lambda e, kt=kt, s_=s_: e.tensor_copy(
                                           VS[:, s_ * 4160 + kt * 65:
                                              s_ * 4160 + kt * 65 + 64],
                                           BK[6][:, 0:64]),
                                       wait=[(st, None)], inc=sv)
                    for s_ in range(2):
                        hb = 64 * s_
                        for tau in range(16):
                            kmax = 4 * tau + 3
                            for kt in range(kmax + 1):
                                pe_ = BK[0 + (kt % 2)]
                                ctx.op("tensor",
                                       lambda e, pe_=pe_, kt=kt, tau=tau,
                                       hb=hb:
                                       e.matmul(
                                           pe_,
                                           KR[hb:hb + 64,
                                              kt * 128:(kt + 1) * 128],
                                           QR[hb:hb + 64,
                                              tau * 512:(tau + 1) * 512],
                                           start=True, stop=True),
                                       wait=[(ss, None)], inc=st)
                                eb = Et[:, (kt % 2) * 512:
                                        (kt % 2 + 1) * 512]
                                ctx.op("scalar",
                                       lambda e, pe_=pe_, eb=eb:
                                       e.activation(eb, pe_[:], AF.Exp,
                                                    scale=0.125),
                                       wait=[(st, None), (sp, None)],
                                       inc=ss)
                                if kt >= 4 * tau:
                                    ctx.op("gpsimd",
                                           lambda e, eb=eb, tau=tau, kt=kt:
                                           e.affine_select(
                                               Em[:], eb,
                                               pattern=[[1, 512]],
                                               compare_op=ALU.is_ge,
                                               fill=0.0,
                                               base=tau * 512 - kt * 128,
                                               channel_multiplier=-1),
                                           wait=[(ss, None), (st, None)],
                                           inc=sp)
                                    rhs_ap = Em[:]
                                    avw = [(sp, None)]
                                else:
                                    rhs_ap = eb
                                    avw = [(ss, None)]
                                if kt == 0:
                                    avw = avw + [(sv, None)]
                                ctx.op("tensor",
                                       lambda e, s_=s_, kt=kt,
                                       rhs_ap=rhs_ap, kmax=kmax:
                                       e.matmul(
                                           BK[2][0:65, :],
                                           VS[:, s_ * 4160 + kt * 65:
                                              s_ * 4160 + (kt + 1) * 65],
                                           rhs_ap,
                                           start=(kt == 0),
                                           stop=(kt == kmax)),
                                       wait=avw, inc=st)
                            ctx.op("vector", lambda e: e.tensor_copy(
                                denr[:], BK[2][64:65, :]),
                                wait=[(st, None)], inc=sv)
                            ctx.op("vector", lambda e: e.reciprocal(
                                rden[:], denr[:]), inc=sv)
                            ctx.op("tensor", lambda e: e.matmul(
                                BK[3][0:64, :], com.onesf_row[0:1, 0:64],
                                rden[:], start=True, stop=True),
                                wait=[(sv, None)], inc=st)
                            ctx.op("vector", lambda e: e.tensor_copy(
                                rbc[:], BK[3][0:64, :]),
                                wait=[(st, None)], inc=sv)
                            ctx.op("vector",
                                   lambda e, s_=s_, ps_=ps_, tau=tau, hb=hb:
                                   e.tensor_mul(
                                       OT[hb:hb + 64,
                                          ps_ * 8192 + tau * 512:
                                          ps_ * 8192 + (tau + 1) * 512],
                                       BK[2][0:64, :], rbc[:]),
                                   inc=sv)
            ctx.dma("sync", hw["cc_o_in"], OT[:], sdo, wait=[(sv, None)])
            ctx.op("sync", lambda e: None, wait=[(sdo, None), (sd, None)])
            ctx.op("gpsimd", lambda e: None, wait=[(sg, None)])
            ctx.barrier()
            ctx.op("gpsimd", lambda e: e.collective_compute(
                "AllGather", ALU.bypass, replica_groups=GROUPS,
                ins=[hw["cc_o_in"]], outs=[hw["cc_o_out"]]), inc=scc)
            for eng in ENGINES:
                ctx.op(eng, lambda e: None, wait=[(scc, None)])
            with ctx.scope():
                AFULL = ctx.sbuf("l2_af", [128, NFC * T], BF16)
                for r_ in range(4):
                    def dext(e, r_=r_):
                        with e.register() as r1:
                            e.reg_load(r1, ofs[0:1, 3:4])
                            e.reg_add(r1, r1, r_ * 128 * 16384)
                            return e.dma_start(
                                out=hw["a_stage"][r_ * 128:(r_ + 1) * 128,
                                                  :],
                                in_=bass.AP(hw["cc_o_out_h"], r1,
                                            [[16384, 128], [8192, 2],
                                             [1, 2048]]))
                    ctx.op("gpsimd", dext, inc=sg, incn=16)
                ctx.op("sync", lambda e: None, wait=[(sg, None)])
                for r_ in range(4):
                    for pb in range(2):
                        ctx.dma("sync",
                                AFULL[:, (2 * r_ + pb) * T:
                                      (2 * r_ + pb + 1) * T],
                                hw["a_stage"][r_ * 128:(r_ + 1) * 128,
                                              pb * T:(pb + 1) * T], sd)
                ctx.op("sync", lambda e: None, wait=[(sd, None)])
                ctx.barrier()
                WS = ctx.sbuf("l2o_ws", [128, 128], F32)
                WB = ctx.sbuf("l2o_wb", [128, 8 * 128], BF16)
                xt = ctx.sbuf("l2o_xt", [128, 512], F32)
                gt2 = ctx.sbuf("l2o_gt", [128, 512], F32)
                stile = ctx.sbuf("l2o_stile", [128, 512], F32)
                for mc in range(NFC):
                    for kf in range(NFC):
                        ctx.dma("sync", WS[:],
                                hw["l2_wo"][kf, :, mc * 128:(mc + 1) * 128],
                                sd, wait=[(sv, None)])
                        ctx.op("vector", lambda e, kf=kf: e.tensor_copy(
                            WB[:, kf * 128:(kf + 1) * 128], WS[:]),
                            wait=[(sd, None), (st, None)], inc=sv)
                    for tt in range(NTT):
                        pb = BK[2 + (tt % 2)]
                        sl = slice(mc * T + tt * 512, mc * T + (tt + 1) * 512)
                        ctx.dma("sync", xt[:], x_spill[:, sl], sd,
                                wait=[(sv, None)])
                        ctx.dma("gpsimd", gt2[:], gdram[:, sl], sg,
                                wait=[(sv, None)])
                        for kf in range(NFC):
                            ctx.op("tensor",
                                   lambda e, ps=pb, kf=kf,
                                   l=WB[:, kf * 128:(kf + 1) * 128],
                                   r=AFULL[:, kf * T + tt * 512:
                                        kf * T + (tt + 1) * 512],
                                   s0=(kf == 0), s1=(kf == NFC - 1):
                                   e.matmul(ps, l, r, start=s0, stop=s1),
                                   wait=([(sv, None)] if kf == 0 else []),
                                   inc=(st if kf == NFC - 1 else None))
                        ctx.op("vector", lambda e, ps=pb:
                               e.tensor_mul(stile[:], ps[:], gt2[:]),
                               wait=[(st, None), (sg, None), (sd, None),
                                     (sdo, None)],
                               inc=sv)
                        ctx.op("vector", lambda e: e.tensor_add(
                            stile[:], stile[:], xt[:]), inc=sv)
                        ctx.dma("sync", x_spill[:, sl], stile[:], sdo,
                                wait=[(sv, None)])
                ctx.op("sync", lambda e: None,
                       wait=[(sd, None), (sdo, None)])
                ctx.op("gpsimd", lambda e: None, wait=[(sg, None)])


def declare_l3_inputs(nc):
    hw = {}
    E = "ExternalInput"
    hw["l3_win"] = nc.dram_tensor("l3_win", [8, 128, 4096], F32, kind=E).ap()
    hw["l3_wx"] = nc.dram_tensor("l3_wx", [16, 128, 2080], F32, kind=E).ap()
    hw["l3_wout"] = nc.dram_tensor("l3_wout", [16, 128, 1024], F32, kind=E).ap()
    hw["l3_convw"] = nc.dram_tensor("l3_convw", [128, 64], F32, kind=E).ap()
    hw["l3_convb"] = nc.dram_tensor("l3_convb", [128, 16], F32, kind=E).ap()
    hw["l3_alogl"] = nc.dram_tensor("l3_alogl", [128, 256], F32, kind=E).ap()
    hw["l3_dcol"] = nc.dram_tensor("l3_dcol", [128, 16], F32, kind=E).ap()
    _rs = nc.dram_tensor("l3_redsel", [16, 128, 128], BF16, kind=E)
    hw["l3_redsel"] = _rs.ap()
    hw["l3_redsel_h"] = _rs
    hw["l3_sel8"] = nc.dram_tensor("l3_sel8", [8, 128], F32, kind=E).ap()
    hw["l3f_k0"] = nc.dram_tensor("l3f_k0", [16, 128, 4096], mybir.dt.uint8, kind=E).ap()
    hw["l3f_k1"] = nc.dram_tensor("l3f_k1", [32, 128, 4096], mybir.dt.uint8, kind=E).ap()
    hw["l3f_k2"] = nc.dram_tensor("l3f_k2", [32, 128, 1024], mybir.dt.uint8, kind=E).ap()
    hw["l3f_s"] = nc.dram_tensor("l3f_s", [1, 4], F32, kind=E).ap()
    hw["meta_l3"] = nc.dram_tensor("meta_l3", [1, 24], F32, kind=E).ap()
    hw["xidram"] = nc.dram_tensor("xidram", [2048, T], BF16)
    hw["zdram"] = nc.dram_tensor("zdram", [2048, T], BF16)
    hw["hdram"] = nc.dram_tensor("hdram", [2048, T], BF16)
    hw["ddram"] = nc.dram_tensor("ddram", [2048, T], BF16)
    hw["udram"] = nc.dram_tensor("udram", [2048, T], BF16)
    hw["dsdram"] = nc.dram_tensor("dsdram", [2048, 1], F32)
    hw["bcdram"] = nc.dram_tensor("bcdram", [32, T], F32)
    hw["cc_sl_in"] = nc.dram_tensor("cc_sl_in", [128, 48], F32).ap()
    hw["cc_sl_out"] = nc.dram_tensor("cc_sl_out", [512, 48], F32).ap()
    hw["cc_sm_in"] = nc.dram_tensor("cc_sm_in", [128, 512], F32).ap()
    hw["cc_sm_out"] = nc.dram_tensor("cc_sm_out", [512, 512], F32).ap()
    return hw


def prep_l3_inputs(inputs):
    sh = {}
    sh["l3_win"] = np.ascontiguousarray(
        inputs["L3_attn_W_in"].T.reshape(8, 128, 4096))
    sh["l3_wx"] = np.ascontiguousarray(
        inputs["L3_attn_W_x"].T.reshape(16, 128, 2080))
    sh["l3_wout"] = np.ascontiguousarray(
        inputs["L3_attn_W_out"].T.reshape(16, 128, 1024))
    cw = inputs["L3_attn_conv_w"].reshape(2048, 4)
    sh["l3_convw"] = np.ascontiguousarray(
        cw.reshape(16, 128, 4).transpose(1, 0, 2).reshape(128, 64))
    sh["l3_convb"] = np.ascontiguousarray(
        inputs["L3_attn_conv_b"].reshape(16, 128).T)
    sh["l3_alogl"] = np.ascontiguousarray(
        inputs["L3_attn_A_log"].reshape(256, 128).T)
    sh["l3_dcol"] = np.ascontiguousarray(
        inputs["L3_attn_D"].reshape(16, 128).T)
    rs = np.zeros((16, 128, 128), np.float32)
    for j in range(16):
        for p in range(128):
            rs[j, p, 8 * j + p // 16] = 1.0
    sh["l3_redsel"] = rs.astype(ml_dtypes.bfloat16)
    sel8 = np.zeros((8, 128), np.float32)
    for i in range(8):
        sel8[i, i * 16:(i + 1) * 16] = 1.0
    sh["l3_sel8"] = sel8
    sh["l3f_k0"] = np.ascontiguousarray(
        inputs["L3_ffn_k0"].astype(np.uint8).reshape(16, 128, 4096))
    sh["l3f_k1"] = np.ascontiguousarray(
        inputs["L3_ffn_k1"].astype(np.uint8).reshape(32, 128, 4096))
    sh["l3f_k2"] = np.ascontiguousarray(
        inputs["L3_ffn_k2"].astype(np.uint8).reshape(32, 128, 1024))
    sh["l3f_s"] = np.array([[float(inputs["L3_ffn_s0"]),
                             float(inputs["L3_ffn_s1"]),
                             float(inputs["L3_ffn_s2"]), 0.0]], np.float32)
    return sh


def l3_core_meta(c):
    j = c % 4
    m = np.zeros((1, 24), np.float32)
    for i in range(4):
        for k in range(4):
            m[0, i * 4 + k] = 1.0 if (i < k < j) else 0.0
    for i in range(4):
        m[0, 16 + i] = 1.0 if i < j else 0.0
    for r in range(4):
        m[0, 20 + r] = 1.0 if r == j - 1 else 0.0
    return {"meta_l3": m}


def emit_l3_ffn(ctx, com, hw, xn, gdram, P):
    sd = ctx.sem("f3d")
    sv = ctx.sem("f3v")
    ss = ctx.sem("f3s")
    st = ctx.sem("f3t")
    sgd = ctx.sem("f3gd")
    with ctx.scope():
        R1 = ctx.sbuf("l3_r1", [128, 32 * 1024], BF16)
        R2 = ctx.sbuf("l3_r2", [128, 32 * 1024], BF16)
        ks = ctx.sbuf("l3_ks", [128, 4096], mybir.dt.uint8)
        kb = ctx.sbuf("l3_kb", [128, 4096], BF16)
        scol = ctx.sbuf("l3_scol", [128, 4], F32)
        srow = ctx.sbuf("l3_srow", [1, 4], F32)
        hnegc = ctx.sbuf("l3_hneg", [128, 1], BF16)
        rsrow = ctx.sbuf("l3_rsrow", [1, 2048], BF16)
        rsf = ctx.sbuf("l3_rsf", [1, 1024], F32)
        rsl = ctx.sbuf("l3_rsl", [1, 1024], F32)
        gtile = ctx.sbuf("l3_gtile", [128, 512], F32)
        ctx.dma("sync", srow[:], hw["l3f_s"], sd)
        ctx.op("tensor", lambda e: e.matmul(
            P[0:128, 0:4], com.onesf_row[:], srow[:], start=True, stop=True),
            wait=[(sd, None)], inc=st)
        ctx.op("vector", lambda e: e.tensor_copy(scol[:], P[0:128, 0:4]),
               wait=[(st, None)], inc=sv)
        ctx.op("scalar", lambda e: e.mul(scol[:, 0:1], scol[:, 0:1], 4.0),
               wait=[(sv, None)], inc=ss)
        ctx.op("scalar", lambda e: e.mul(scol[:, 1:2], scol[:, 1:2], 4.0),
               inc=ss)
        ctx.op("scalar", lambda e: e.mul(scol[:, 2:3], scol[:, 2:3], 2.0),
               inc=ss)
        ctx.op("vector", lambda e: e.memset(hnegc[:], -0.5), inc=sv)
        for th in range(2):
            tk = slice(th * 1024, (th + 1) * 1024)
            for mc in range(32):
                for kf in range(8):
                    ctx.dma("sync", ks[:, kf * 128:(kf + 1) * 128],
                            hw["l3f_k0"][kf, :, mc * 128:(mc + 1) * 128],
                            sd, wait=[(sv, None)] if kf == 0 else ())
                    ctx.dma("sync", ks[:, 1024 + kf * 128:1024 + (kf + 1) * 128],
                            hw["l3f_k0"][8 + kf, :, mc * 128:(mc + 1) * 128],
                            sd)
                ctx.op("vector", lambda e: e.tensor_copy(
                    kb[:, 1024:2048], ks[:, 0:1024]),
                    wait=[(sd, None), (st, None)], inc=sv)
                ctx.op("vector", lambda e: e.tensor_copy(
                    kb[:, 2048:3072], ks[:, 1024:2048]), inc=sv)
                ctx.op("vector", lambda e: e.tensor_sub(
                    kb[:, 0:1024], kb[:, 1024:2048], kb[:, 2048:3072]),
                    inc=sv)
                for tq in range(2):
                    ts_ = slice(th * 1024 + tq * 512, th * 1024 + (tq + 1) * 512)
                    pb = P[:, (tq % 2) * 512:(tq % 2 + 1) * 512]
                    for kf in range(8):
                        ctx.op("tensor",
                               lambda e, ps=pb, l_=kb[:, kf * 128:(kf + 1) * 128],
                               r=xn[:, kf * T:(kf + 1) * T][:, ts_],
                               s0=(kf == 0), s1=(kf == 7):
                               e.matmul(ps, l_, r, start=s0, stop=s1),
                               wait=([(sv, None), (ss, None)] if kf == 0 else []),
                               inc=(st if kf == 7 else None))
                    ctx.op("scalar",
                           lambda e, ps=pb, mc=mc, tq=tq:
                           e.activation(R1[:, mc * 1024 + tq * 512:
                                           mc * 1024 + (tq + 1) * 512],
                                        ps[:], AF.Relu, scale=scol[:, 0:1]),
                           wait=[(st, None)], inc=ss)
            RX, NMC = R1, 32
            for tq in range(2):
                for mc in range(NMC):
                    ctx.op("tensor",
                           lambda e, mc=mc, tq=tq, RX=RX:
                           e.matmul(P[0:1, 1024 + tq * 512:1024 + (tq + 1) * 512],
                                    hnegc[:],
                                    RX[:, mc * 1024 + tq * 512:
                                       mc * 1024 + (tq + 1) * 512],
                                    start=(mc == 0), stop=(mc == NMC - 1)),
                           wait=([(ss, None), (sv, None)] if mc == 0 else []),
                           inc=(st if mc == NMC - 1 else None))
            ctx.op("vector", lambda e: e.tensor_copy(
                rsf[0:1, :], P[0:1, 1024:2048]), wait=[(st, None)], inc=sv)
            ctx.op("vector", lambda e: e.tensor_copy(
                rsrow[0:1, 0:1024], rsf[0:1, :]), inc=sv)
            ctx.op("vector", lambda e: e.tensor_sub(
                rsl[0:1, :], rsf[0:1, :], rsrow[0:1, 0:1024]), inc=sv)
            ctx.op("vector", lambda e: e.tensor_copy(
                rsrow[0:1, 1024:2048], rsl[0:1, :]), inc=sv)
            for mc in range(32):
                for kf in range(32):
                    ctx.dma("sync", ks[:, kf * 128:(kf + 1) * 128],
                            hw["l3f_k1"][kf, :, mc * 128:(mc + 1) * 128],
                            sd, wait=[(sv, None)] if kf == 0 else ())
                ctx.op("vector", lambda e: e.tensor_copy(kb[:], ks[:]),
                       wait=[(sd, None), (st, None)], inc=sv)
                for tq in range(2):
                    pb = P[:, (tq % 2) * 512:(tq % 2 + 1) * 512]
                    for kf in range(32):
                        ctx.op("tensor",
                               lambda e, ps=pb, l_=kb[:, kf * 128:(kf + 1) * 128],
                               r=R1[:, kf * 1024 + tq * 512:
                                    kf * 1024 + (tq + 1) * 512],
                               s0=(kf == 0):
                               e.matmul(ps, l_, r, start=s0, stop=False),
                               wait=([(sv, None), (ss, None)] if kf == 0 else []))
                    ctx.op("tensor",
                           lambda e, ps=pb, tq=tq:
                           e.matmul(ps, com.ones_row[:],
                                    rsrow[0:1, tq * 512:(tq + 1) * 512],
                                    start=False, stop=False))
                    ctx.op("tensor",
                           lambda e, ps=pb, tq=tq:
                           e.matmul(ps, com.ones_row[:],
                                    rsrow[0:1, 1024 + tq * 512:
                                          1024 + (tq + 1) * 512],
                                    start=False, stop=True),
                           inc=st)
                    ctx.op("scalar",
                           lambda e, ps=pb, mc=mc, tq=tq:
                           e.activation(R2[:, mc * 1024 + tq * 512:
                                           mc * 1024 + (tq + 1) * 512],
                                        ps[:], AF.Relu, scale=scol[:, 1:2]),
                           wait=[(st, None)], inc=ss)
            RX, NMC = R2, 32
            for tq in range(2):
                for mc in range(NMC):
                    ctx.op("tensor",
                           lambda e, mc=mc, tq=tq, RX=RX:
                           e.matmul(P[0:1, 1024 + tq * 512:1024 + (tq + 1) * 512],
                                    hnegc[:],
                                    RX[:, mc * 1024 + tq * 512:
                                       mc * 1024 + (tq + 1) * 512],
                                    start=(mc == 0), stop=(mc == NMC - 1)),
                           wait=([(ss, None), (sv, None)] if mc == 0 else []),
                           inc=(st if mc == NMC - 1 else None))
            ctx.op("vector", lambda e: e.tensor_copy(
                rsf[0:1, :], P[0:1, 1024:2048]), wait=[(st, None)], inc=sv)
            ctx.op("vector", lambda e: e.tensor_copy(
                rsrow[0:1, 0:1024], rsf[0:1, :]), inc=sv)
            ctx.op("vector", lambda e: e.tensor_sub(
                rsl[0:1, :], rsf[0:1, :], rsrow[0:1, 0:1024]), inc=sv)
            ctx.op("vector", lambda e: e.tensor_copy(
                rsrow[0:1, 1024:2048], rsl[0:1, :]), inc=sv)
            for mc in range(8):
                for kf in range(32):
                    ctx.dma("sync", ks[:, kf * 128:(kf + 1) * 128],
                            hw["l3f_k2"][kf, :, mc * 128:(mc + 1) * 128],
                            sd, wait=[(sv, None)] if kf == 0 else ())
                ctx.op("vector", lambda e: e.tensor_copy(kb[:], ks[:]),
                       wait=[(sd, None), (st, None)], inc=sv)
                for tq in range(2):
                    pb = P[:, (tq % 2) * 512:(tq % 2 + 1) * 512]
                    for kf in range(32):
                        ctx.op("tensor",
                               lambda e, ps=pb, l_=kb[:, kf * 128:(kf + 1) * 128],
                               r=R2[:, kf * 1024 + tq * 512:
                                    kf * 1024 + (tq + 1) * 512],
                               s0=(kf == 0):
                               e.matmul(ps, l_, r, start=s0, stop=False),
                               wait=([(sv, None), (ss, None)] if kf == 0 else []))
                    ctx.op("tensor",
                           lambda e, ps=pb, tq=tq:
                           e.matmul(ps, com.ones_row[:],
                                    rsrow[0:1, tq * 512:(tq + 1) * 512],
                                    start=False, stop=False))
                    ctx.op("tensor",
                           lambda e, ps=pb, tq=tq:
                           e.matmul(ps, com.ones_row[:],
                                    rsrow[0:1, 1024 + tq * 512:
                                          1024 + (tq + 1) * 512],
                                    start=False, stop=True),
                           inc=st)
                    ctx.op("scalar",
                           lambda e, ps=pb: e.activation(
                               gtile[:], ps[:], AF.Silu, scale=scol[:, 2:3]),
                           wait=[(st, None), (sgd, None)], inc=ss)
                    ctx.dma("sync",
                            gdram[:, mc * T + th * 1024 + tq * 512:
                                  mc * T + th * 1024 + (tq + 1) * 512],
                            gtile[:], sgd, wait=[(ss, None)])
        ctx.op("sync", lambda e: None, wait=[(sgd, None)])


def emit_l3(ctx, com, hw, glob):
    P = glob["P"]
    gdram = glob["gdram"]
    x_spill = glob["x_spill"]
    sd = ctx.sem("a3d")
    sg = ctx.sem("a3g")
    sv = ctx.sem("a3v")
    ss = ctx.sem("a3s")
    st = ctx.sem("a3t")
    sp = ctx.sem("a3p")
    sdo = ctx.sem("a3o")
    scc = ctx.sem("a3cc")
    BK = [P[:, k * 512:(k + 1) * 512] for k in range(8)]

    def rep_bc(row0):
        return bass.AP(hw["bcdram"], row0 * T, [[0, 8], [T, 16], [1, T]])

    with ctx.scope():
        slin = ctx.sbuf("l3_slin", [128, 48], F32)
        mcol = ctx.sbuf("l3_mcol", [128, 24], F32)
        HIN = ctx.sbuf("l3_hin", [128, 256], F32)
        ACOL = ctx.sbuf("l3_acol", [128, 256], F32)
        mrow = ctx.sbuf("l3_mrow", [1, 24], F32)
        sel8f = ctx.sbuf("l3_sel8f", [8, 128], F32)
        sel8b = ctx.sbuf("l3_sel8b", [8, 128], BF16)
        ctx.dma("sync", mrow[:], hw["meta_l3"], sd)
        ctx.dma("sync", sel8f[:], hw["l3_sel8"], sd)
        ctx.dma("sync", ACOL[:], hw["l3_alogl"], sd)
        ctx.op("tensor", lambda e: e.matmul(
            P[:, 0:24], com.onesf_row[:], mrow[:], start=True, stop=True),
            wait=[(sd, None)], inc=st)
        ctx.op("vector", lambda e: e.tensor_copy(mcol[:], P[:, 0:24]),
               wait=[(st, None)], inc=sv)
        ctx.op("vector", lambda e: e.tensor_copy(sel8b[:], sel8f[:]),
               wait=[(sd, None)], inc=sv)
        ctx.op("scalar", lambda e: e.activation(ACOL[:], ACOL[:], AF.Exp),
               wait=[(sd, None)], inc=ss)
        ctx.op("scalar", lambda e: e.mul(ACOL[:], ACOL[:], -1.0), inc=ss)
        ctx.op("sync", lambda e: None, wait=[(sd, None)])
        ctx.barrier()

        with ctx.scope():
            xn = ctx.sbuf("l3_xn", [128, NFC * T], BF16)
            with ctx.scope():
                xtmp = ctx.sbuf("l3_xtmp", [128, NFC * T], F32)
                ctx.dma("sync", xtmp[:], x_spill, sd)
                for eng in ENGINES:
                    ctx.op(eng, lambda e: None, wait=[(sd, None)])
                emit_ln(ctx, com, xtmp, xn, P[0:1, 0:4096], P[:, 0:1024],
                        tag="l3n")
            emit_l3_ffn(ctx, com, hw, xn, gdram, P)

            with ctx.scope():
                WS = ctx.sbuf("l3_ws", [128, 128], F32)
                WB = ctx.sbuf("l3_wb", [128, 8 * 128], BF16)
                xic = ctx.sbuf("l3_xic", [128, T], BF16)
                sliv = ctx.sbuf("l3_sliv", [128, 48], F32)
                slre = ctx.sbuf("l3_slre", [128, 48], F32)
                for part, dram in ((0, hw["xidram"]), (1, hw["zdram"])):
                    for ec in range(16):
                        for kf in range(NFC):
                            ctx.dma("sync", WS[:],
                                    hw["l3_win"][kf, :,
                                                 part * 2048 + ec * 128:
                                                 part * 2048 + (ec + 1) * 128],
                                    sd, wait=[(sv, None)])
                            ctx.op("vector", lambda e, kf=kf: e.tensor_copy(
                                WB[:, kf * 128:(kf + 1) * 128], WS[:]),
                                wait=[(sd, None), (st, None)], inc=sv)
                        for tt in range(NTT):
                            pb = BK[2 + (tt % 2)]
                            for kf in range(NFC):
                                ctx.op("tensor",
                                       lambda e, ps=pb,
                                       l_=WB[:, kf * 128:(kf + 1) * 128],
                                       r=xn[:, kf * T + tt * 512:
                                            kf * T + (tt + 1) * 512],
                                       s0=(kf == 0), s1=(kf == NFC - 1):
                                       e.matmul(ps, l_, r, start=s0, stop=s1),
                                       wait=([(sv, None), (ss, None)]
                                             if kf == 0 else []),
                                       inc=(st if kf == NFC - 1 else None))
                            if part == 0:
                                ctx.op("vector",
                                       lambda e, ps=pb, tt=tt:
                                       e.tensor_copy(
                                           xic[:, tt * 512:(tt + 1) * 512],
                                           ps[:]),
                                       wait=[(st, None)] +
                                       ([(sdo, None)] if tt == 0 else []),
                                       inc=sv)
                            else:
                                ctx.op("scalar",
                                       lambda e, ps=pb, tt=tt:
                                       e.activation(
                                           xic[:, tt * 512:(tt + 1) * 512],
                                           ps[:], AF.Silu),
                                       wait=[(st, None)] +
                                       ([(sdo, None)] if tt == 0 else []),
                                       inc=ss)
                        if part == 0:
                            ctx.op("vector", lambda e, ec=ec: e.tensor_copy(
                                sliv[:, ec * 3:(ec + 1) * 3],
                                xic[:, 2045:2048]), inc=sv)
                        ctx.dma("sync", dram[ec * 128:(ec + 1) * 128, :],
                                xic[:], sdo, wait=[(sv, None), (ss, None)])
                ctx.dma("gpsimd", hw["cc_sl_in"], sliv[:], sg,
                        wait=[(sv, None)])
                ctx.op("gpsimd", lambda e: e.collective_compute(
                    "AllGather", ALU.bypass, replica_groups=GROUPS,
                    ins=[hw["cc_sl_in"]], outs=[hw["cc_sl_out"]]),
                    wait=[(sg, None)], inc=scc)
                for r in range(4):
                    ctx.dma("gpsimd", slre[:],
                            hw["cc_sl_out"][128 * r:128 * (r + 1), :], sg,
                            wait=[(scc, None), (sv, None)])
                    if r == 0:
                        ctx.op("vector", lambda e: e.tensor_scalar(
                            slin[:], slre[:], mcol[:, 20:21], None, ALU.mult),
                            wait=[(sg, None)], inc=sv)
                    else:
                        ctx.op("vector", lambda e, r=r:
                               e.scalar_tensor_tensor(
                                   slin[:], slre[:], mcol[:, 20 + r:21 + r],
                                   slin[:], ALU.mult, ALU.add),
                               wait=[(sg, None)], inc=sv)
                ctx.op("sync", lambda e: None,
                       wait=[(sd, None), (sdo, None)])
                ctx.op("gpsimd", lambda e: None, wait=[(sg, None)])

        with ctx.scope():
            hc = ctx.sbuf("l3_hc", [128, 16 * T], BF16)
            xie = ctx.sbuf("l3_xie", [128, T + 3], BF16)
            cwc = ctx.sbuf("l3_cwc", [128, 64], F32)
            cbc = ctx.sbuf("l3_cbc", [128, 16], F32)
            cvt = ctx.sbuf("l3_cvt", [128, T], F32)
            WS = ctx.sbuf("l3b_ws", [128, 128], F32)
            WB = ctx.sbuf("l3b_wb", [128, 16 * 128], BF16)
            dbst = ctx.sbuf("l3_dbst", [128, T], BF16)
            ub = ctx.sbuf("l3_ub", [128, T], BF16)
            sp1 = ctx.sbuf("l3_sp1", [128, T], F32)
            sp2 = ctx.sbuf("l3_sp2", [128, T], F32)
            dsums = ctx.sbuf("l3_dsums", [128, 16], F32)
            bcsb = ctx.sbuf("l3_bcsb", [32, T], F32)
            ctx.dma("sync", cwc[:], hw["l3_convw"], sd)
            ctx.dma("sync", cbc[:], hw["l3_convb"], sd)
            for ec in range(16):
                ctx.dma("sync", xie[:, 3:T + 3],
                        hw["xidram"][ec * 128:(ec + 1) * 128, :], sd,
                        wait=[(sv, None)])
                ctx.op("vector", lambda e, ec=ec: e.tensor_copy(
                    xie[:, 0:3], slin[:, ec * 3:(ec + 1) * 3]),
                    wait=[(sd, None)], inc=sv)
                ctx.op("vector", lambda e, ec=ec: e.tensor_scalar(
                    cvt[:], xie[:, 0:T], cwc[:, ec * 4:ec * 4 + 1], None,
                    ALU.mult), wait=[(ss, None)], inc=sv)
                for tau in range(1, 4):
                    ctx.op("vector", lambda e, ec=ec, tau=tau:
                           e.scalar_tensor_tensor(
                               cvt[:], xie[:, tau:T + tau],
                               cwc[:, ec * 4 + tau:ec * 4 + tau + 1],
                               cvt[:], ALU.mult, ALU.add), inc=sv)
                ctx.op("scalar", lambda e, ec=ec: e.activation(
                    hc[:, ec * T:(ec + 1) * T], cvt[:], AF.Silu,
                    bias=cbc[:, ec:ec + 1]),
                    wait=[(sv, None)], inc=ss)
                ctx.dma("sync", hw["hdram"][ec * 128:(ec + 1) * 128, :],
                        hc[:, ec * T:(ec + 1) * T], sdo, wait=[(ss, None)])
            ctx.barrier()
            for fm in range(17):
                M = 128 if fm < 16 else 32
                for kf in range(16):
                    ctx.dma("sync", WS[:, 0:M],
                            hw["l3_wx"][kf, :, fm * 128:fm * 128 + M], sd,
                            wait=[(sv, None)])
                    ctx.op("vector", lambda e, kf=kf, M=M: e.tensor_copy(
                        WB[:, kf * 128:kf * 128 + M], WS[:, 0:M]),
                        wait=[(sd, None), (st, None)], inc=sv)
                for tt in range(NTT):
                    pb = P[0:M, 1024 + (tt % 2) * 512:
                           1024 + (tt % 2 + 1) * 512]
                    for kf in range(16):
                        ctx.op("tensor",
                               lambda e, ps=pb, kf=kf, M=M,
                               l_=WB[:, kf * 128:kf * 128 + M],
                               r=hc[:, kf * T + tt * 512:
                                    kf * T + (tt + 1) * 512],
                               s0=(kf == 0), s1=(kf == 15):
                               e.matmul(ps, l_, r, start=s0, stop=s1),
                               wait=([(sv, None), (ss, None)]
                                     if kf == 0 else []),
                               inc=(st if kf == 15 else None))
                    tsl = slice(tt * 512, (tt + 1) * 512)
                    if fm < 16:
                        ctx.op("scalar", lambda e, ps=pb, tsl=tsl:
                               e.activation(sp1[:, tsl], ps[:], AF.Abs),
                               wait=[(st, None)] +
                               ([(sp, None), (sv, None)] if tt == 0 else []),
                               inc=ss)
                        ctx.op("scalar", lambda e, tsl=tsl: e.activation(
                            sp2[:, tsl], sp1[:, tsl], AF.Exp, scale=-1.0),
                            inc=ss)
                        ctx.op("scalar", lambda e, tsl=tsl: e.activation(
                            sp1[:, tsl], sp2[:, tsl], AF.Ln,
                            bias=com.cst[:, 0:1]), inc=ss)
                        ctx.op("vector", lambda e, ps=pb, tsl=tsl:
                               e.scalar_tensor_tensor(
                                   dbst[:, tsl], ps[:], 0.0, sp1[:, tsl],
                                   ALU.max, ALU.add),
                               wait=[(ss, None)] +
                               ([(sdo, None)] if tt == 0 else []), inc=sv)
                    else:
                        ctx.op("vector", lambda e, ps=pb, tsl=tsl:
                               e.tensor_copy(bcsb[0:32, tsl], ps[:]),
                               wait=[(st, None)], inc=sv)
                if fm < 16:
                    ctx.op("gpsimd", lambda e, fm=fm: e.tensor_mul(
                        ub[:], dbst[:], hc[:, fm * T:(fm + 1) * T]),
                        wait=[(sv, None), (sdo, None)], inc=sp)
                    ctx.op("vector", lambda e, fm=fm: e.reduce_sum(
                        dsums[:, fm:fm + 1], dbst[:], axis=AX), inc=sv)
                    ctx.dma("sync", hw["ddram"][fm * 128:(fm + 1) * 128, :],
                            dbst[:], sdo, wait=[(sv, None)])
                    ctx.dma("sync", hw["udram"][fm * 128:(fm + 1) * 128, :],
                            ub[:], sdo, wait=[(sp, None)])
                else:
                    ctx.dma("sync", hw["bcdram"][0:32, :], bcsb[:], sdo,
                            wait=[(sv, None)])
            ctx.dma("sync",
                    bass.AP(hw["dsdram"], 0, [[16, 128], [1, 16]]),
                    dsums[:], sdo, wait=[(sv, None)])
            ctx.op("sync", lambda e: None,
                   wait=[(sd, None), (sdo, None)])

        with ctx.scope():
            d8 = ctx.sbuf("l3_d8", [8, T], BF16)
            u8 = ctx.sbuf("l3_u8", [8, T], BF16)
            brep = ctx.sbuf("l3_brep", [128, T], F32)
            a_t = ctx.sbuf("l3_a", [128, T], F32)
            b_t = ctx.sbuf("l3_b", [128, T], F32)
            h_t = ctx.sbuf("l3_h", [128, T], F32)
            dsr = ctx.sbuf("l3_dsr", [128, 1], F32)
            SSUM = ctx.sbuf("l3_ssum", [128, 256], F32)
            HLOC = ctx.sbuf("l3_hloc", [128, 256], F32)
            pack = ctx.sbuf("l3_pack", [128, 512], F32)
            blk = ctx.sbuf("l3_blk", [128, 512], F32)
            facc = ctx.sbuf("l3_facc", [128, 256], F32)
            term = ctx.sbuf("l3_term", [128, 256], F32)
            ctx.dma("sync", brep[:], rep_bc(0), sd)
            for g in range(256):
                ctx.dma("sync", d8[:], hw["ddram"][8 * g:8 * g + 8, :], sd,
                        wait=[(st, None)])
                ctx.dma("sync", u8[:], hw["udram"][8 * g:8 * g + 8, :], sd)
                ctx.dma("gpsimd", dsr[:],
                        bass.AP(hw["dsdram"],
                                ((8 * g) % 128) * 16 + (8 * g) // 128,
                                [[16, 8], [0, 16], [1, 1]]),
                        sg, wait=[(sv, None)])
                for q in range(NTT):
                    qs = slice(q * 512, (q + 1) * 512)
                    pd = BK[4 + (q % 2)]
                    pu = BK[6 + (q % 2)]
                    ctx.op("tensor", lambda e, pd=pd, qs=qs: e.matmul(
                        pd, sel8b[:], d8[:, qs], start=True, stop=True),
                        wait=[(ss, None)] + ([(sd, None)] if q == 0 else []),
                        inc=st)
                    ctx.op("tensor", lambda e, pu=pu, qs=qs: e.matmul(
                        pu, sel8b[:], u8[:, qs], start=True, stop=True),
                        wait=[(sv, None)] + ([(sd, None)] if q == 0 else []),
                        inc=st)
                    ctx.op("scalar", lambda e, pd=pd, qs=qs, g=g:
                           e.activation(a_t[:, qs], pd[:], AF.Exp,
                                        scale=ACOL[:, g:g + 1]),
                           wait=[(st, None), (sv, None)], inc=ss)
                    ctx.op("vector", lambda e, pu=pu, qs=qs: e.tensor_mul(
                        b_t[:, qs], pu[:], brep[:, qs]),
                        wait=[(st, None)], inc=sv)
                    init = 0.0 if q == 0 else h_t[:, q * 512 - 1:q * 512]
                    ctx.op("vector", lambda e, qs=qs, init=init:
                           e.tensor_tensor_scan(h_t[:, qs], a_t[:, qs],
                                                b_t[:, qs], init,
                                                ALU.mult, ALU.add),
                           wait=[(ss, None)], inc=sv)
                ctx.op("vector", lambda e, g=g: e.tensor_copy(
                    HLOC[:, g:g + 1], h_t[:, T - 1:T]), inc=sv)
                ctx.op("vector", lambda e, g=g: e.tensor_mul(
                    SSUM[:, g:g + 1], ACOL[:, g:g + 1], dsr[:]),
                    wait=[(sg, None)], inc=sv)
            ctx.op("vector", lambda e: e.tensor_copy(pack[:, 0:256],
                                                     SSUM[:]), inc=sv)
            ctx.op("vector", lambda e: e.tensor_copy(pack[:, 256:512],
                                                     HLOC[:]), inc=sv)
            ctx.dma("gpsimd", hw["cc_sm_in"], pack[:], sg, wait=[(sv, None)])
            ctx.op("gpsimd", lambda e: e.collective_compute(
                "AllGather", ALU.bypass, replica_groups=GROUPS,
                ins=[hw["cc_sm_in"]], outs=[hw["cc_sm_out"]]),
                wait=[(sg, None)], inc=scc)
            ctx.op("vector", lambda e: e.memset(HIN[:], 0.0),
                   wait=[(scc, None)], inc=sv)
            for i in range(4):
                for k in range(4):
                    ctx.dma("gpsimd", blk[:, 0:256],
                            hw["cc_sm_out"][128 * k:128 * (k + 1), 0:256],
                            sg, wait=[(sv, None)])
                    if k == 0:
                        ctx.op("vector", lambda e, i=i: e.tensor_scalar(
                            facc[:], blk[:, 0:256],
                            mcol[:, i * 4:i * 4 + 1], None, ALU.mult),
                            wait=[(sg, None), (ss, None)], inc=sv)
                    else:
                        ctx.op("vector", lambda e, i=i, k=k:
                               e.scalar_tensor_tensor(
                                   facc[:], blk[:, 0:256],
                                   mcol[:, i * 4 + k:i * 4 + k + 1], facc[:],
                                   ALU.mult, ALU.add),
                               wait=[(sg, None)], inc=sv)
                ctx.op("scalar", lambda e: e.activation(facc[:], facc[:],
                                                        AF.Exp),
                       wait=[(sv, None)], inc=ss)
                ctx.dma("gpsimd", blk[:, 256:512],
                        hw["cc_sm_out"][128 * i:128 * (i + 1), 256:512], sg,
                        wait=[(sv, None)])
                ctx.op("vector", lambda e: e.tensor_mul(
                    term[:], facc[:], blk[:, 256:512]),
                    wait=[(ss, None), (sg, None)], inc=sv)
                ctx.op("vector", lambda e, i=i: e.scalar_tensor_tensor(
                    HIN[:], term[:], mcol[:, 16 + i:17 + i], HIN[:],
                    ALU.mult, ALU.add), inc=sv)

        with ctx.scope():
            YB = ctx.sbuf("l3_yb", [128, 16 * T], BF16)
            d8 = ctx.sbuf("l3d_d8", [8, T], BF16)
            u8 = ctx.sbuf("l3d_u8", [8, T], BF16)
            brep = ctx.sbuf("l3d_brep", [128, T], F32)
            crep = ctx.sbuf("l3d_crep", [128, T], F32)
            a_t = ctx.sbuf("l3d_a", [128, T], F32)
            b_t = ctx.sbuf("l3d_b", [128, T], F32)
            h_t = ctx.sbuf("l3d_h", [128, T], F32)
            hcb = ctx.sbuf("l3d_hcb", [128, T], BF16)
            rselb = ctx.sbuf("l3d_rselb", [128, 16 * 128], BF16)
            dcolc = ctx.sbuf("l3d_dcol", [128, 16], F32)
            hch = ctx.sbuf("l3d_hch", [128, T], BF16)
            zt = ctx.sbuf("l3d_zt", [128, T], BF16)
            ctx.dma("sync", rselb[:],
                    bass.AP(hw["l3_redsel_h"], 0,
                            [[128, 128], [16384, 16], [1, 128]]), sd)
            ctx.dma("sync", dcolc[:], hw["l3_dcol"], sd)
            ctx.dma("sync", brep[:], rep_bc(0), sd)
            ctx.dma("sync", crep[:], rep_bc(16), sd)
            for ec in range(16):
                for j in range(16):
                    g = ec * 16 + j
                    ctx.dma("sync", d8[:], hw["ddram"][8 * g:8 * g + 8, :],
                            sd, wait=[(st, None)])
                    ctx.dma("sync", u8[:], hw["udram"][8 * g:8 * g + 8, :],
                            sd)
                    for q in range(NTT):
                        qs = slice(q * 512, (q + 1) * 512)
                        pd = BK[4 + (q % 2)]
                        pu = BK[6 + (q % 2)]
                        ctx.op("tensor", lambda e, pd=pd, qs=qs: e.matmul(
                            pd, sel8b[:], d8[:, qs], start=True, stop=True),
                            wait=[(ss, None)] +
                            ([(sd, None)] if q == 0 else []),
                            inc=st)
                        ctx.op("tensor", lambda e, pu=pu, qs=qs: e.matmul(
                            pu, sel8b[:], u8[:, qs], start=True, stop=True),
                            wait=[(sv, None)] +
                            ([(sd, None)] if q == 0 else []),
                            inc=st)
                        ctx.op("scalar", lambda e, pd=pd, qs=qs, g=g:
                               e.activation(a_t[:, qs], pd[:], AF.Exp,
                                            scale=ACOL[:, g:g + 1]),
                               wait=[(st, None), (sv, None)], inc=ss)
                        ctx.op("vector", lambda e, pu=pu, qs=qs:
                               e.tensor_mul(b_t[:, qs], pu[:], brep[:, qs]),
                               wait=[(st, None)], inc=sv)
                        init = (HIN[:, g:g + 1] if q == 0
                                else h_t[:, q * 512 - 1:q * 512])
                        ctx.op("vector", lambda e, qs=qs, init=init:
                               e.tensor_tensor_scan(h_t[:, qs], a_t[:, qs],
                                                    b_t[:, qs], init,
                                                    ALU.mult, ALU.add),
                               wait=[(ss, None)], inc=sv)
                        ctx.op("gpsimd", lambda e, qs=qs: e.tensor_mul(
                            hcb[:, qs], h_t[:, qs], crep[:, qs]),
                            wait=[(sv, None), (st, None)], inc=sp)
                        ctx.op("tensor", lambda e, j=j, q=q, qs=qs: e.matmul(
                            BK[q], rselb[:, j * 128:(j + 1) * 128],
                            hcb[:, qs], start=(j == 0), stop=(j == 15)),
                            wait=[(sp, None)] +
                            ([(sv, None), (sd, None)] if j == 0 else []),
                            inc=st)
                ctx.dma("gpsimd", hch[:],
                        hw["hdram"][ec * 128:(ec + 1) * 128, :], sg,
                        wait=[(sv, None)])
                ctx.dma("gpsimd", zt[:],
                        hw["zdram"][ec * 128:(ec + 1) * 128, :], sg)
                for q in range(NTT):
                    qs = slice(q * 512, (q + 1) * 512)
                    ysl = slice(ec * T + q * 512, ec * T + (q + 1) * 512)
                    ctx.op("vector", lambda e, q=q, qs=qs, ysl=ysl, ec=ec:
                           e.scalar_tensor_tensor(
                               YB[:, ysl], hch[:, qs], dcolc[:, ec:ec + 1],
                               BK[q][:], ALU.mult, ALU.add),
                           wait=[(st, None), (sg, None)], inc=sv)
                    ctx.op("vector", lambda e, qs=qs, ysl=ysl: e.tensor_mul(
                        YB[:, ysl], YB[:, ysl], zt[:, qs]), inc=sv)
            ctx.barrier()
            with ctx.scope():
                WS = ctx.sbuf("l3o_ws", [128, 128], F32)
                WB = ctx.sbuf("l3o_wb", [128, 16 * 128], BF16)
                xt = ctx.sbuf("l3o_xt", [128, 512], F32)
                gt2 = ctx.sbuf("l3o_gt", [128, 512], F32)
                stile = ctx.sbuf("l3o_stile", [128, 512], F32)
                for mc in range(NFC):
                    for kf in range(16):
                        ctx.dma("sync", WS[:],
                                hw["l3_wout"][kf, :,
                                              mc * 128:(mc + 1) * 128],
                                sd, wait=[(sv, None)])
                        ctx.op("vector", lambda e, kf=kf: e.tensor_copy(
                            WB[:, kf * 128:(kf + 1) * 128], WS[:]),
                            wait=[(sd, None), (st, None)], inc=sv)
                    for tt in range(NTT):
                        pb = BK[2 + (tt % 2)]
                        sl = slice(mc * T + tt * 512, mc * T + (tt + 1) * 512)
                        ctx.dma("sync", xt[:], x_spill[:, sl], sd,
                                wait=[(sv, None)])
                        ctx.dma("gpsimd", gt2[:], gdram[:, sl], sg,
                                wait=[(sv, None)])
                        for kf in range(16):
                            ctx.op("tensor",
                                   lambda e, ps=pb, kf=kf, tt=tt,
                                   l_=WB[:, kf * 128:(kf + 1) * 128],
                                   r=YB[:, kf * T + tt * 512:
                                        kf * T + (tt + 1) * 512],
                                   s0=(kf == 0), s1=(kf == 15):
                                   e.matmul(ps, l_, r, start=s0, stop=s1),
                                   wait=([(sv, None)] if kf == 0 else []),
                                   inc=(st if kf == 15 else None))
                        ctx.op("vector", lambda e, ps=pb:
                               e.tensor_mul(stile[:], ps[:], gt2[:]),
                               wait=[(st, None), (sg, None), (sd, None),
                                     (sdo, None)],
                               inc=sv)
                        ctx.op("vector", lambda e: e.tensor_add(
                            stile[:], stile[:], xt[:]), inc=sv)
                        ctx.dma("sync", x_spill[:, sl], stile[:], sdo,
                                wait=[(sv, None)])
                ctx.op("sync", lambda e: None,
                       wait=[(sd, None), (sdo, None)])
                ctx.op("gpsimd", lambda e: None, wait=[(sg, None)])


def declare_l0_inputs(nc):
    hw = {}
    hw["l0_wq"] = nc.dram_tensor("l0_wq", [8, 128, 1024], F32, kind="ExternalInput").ap()
    hw["l0_wk"] = nc.dram_tensor("l0_wk", [8, 128, 1024], F32, kind="ExternalInput").ap()
    hw["l0_wv"] = nc.dram_tensor("l0_wv", [8, 128, 1024], F32, kind="ExternalInput").ap()
    hw["l0_wo"] = nc.dram_tensor("l0_wo", [8, 128, 1024], F32, kind="ExternalInput").ap()
    hw["l0_bq"] = nc.dram_tensor("l0_bq", [128, 8], F32, kind="ExternalInput").ap()
    hw["l0_bk"] = nc.dram_tensor("l0_bk", [128, 8], F32, kind="ExternalInput").ap()
    hw["l0_bo"] = nc.dram_tensor("l0_bo", [128, 8], F32, kind="ExternalInput").ap()
    hw["l0_bv_row"] = nc.dram_tensor("l0_bv_row", [1, 1024], F32, kind="ExternalInput").ap()
    hw["l0_rm"] = nc.dram_tensor("l0_rm", [64, 256], F32, kind="ExternalInput").ap()
    hw["l0_wg"] = nc.dram_tensor("l0_wg", [128, 64], F32, kind="ExternalInput").ap()
    hw["l0_bg"] = nc.dram_tensor("l0_bg", [1, 8], F32, kind="ExternalInput").ap()
    hw["l0_w1"] = nc.dram_tensor("l0_w1", [8, 8, 128, 2048], F32, kind="ExternalInput").ap()
    hw["l0_w3"] = nc.dram_tensor("l0_w3", [8, 8, 128, 2048], F32, kind="ExternalInput").ap()
    hw["l0_w2"] = nc.dram_tensor("l0_w2", [8, 16, 128, 1024], F32, kind="ExternalInput").ap()
    hw["gt_dram"] = nc.dram_tensor("gt_dram", [8, T], F32,
                                   kind=("ExternalOutput" if "gt" in DEBUG_TAPS
                                         else "Internal")).ap()
    hw["cc_kv_in"] = nc.dram_tensor("cc_kv_in", [128, 2080], F32).ap()
    hw["cc_kv_out"] = nc.dram_tensor("cc_kv_out", [128, 2080], F32).ap()
    return hw


def prep_l0_inputs(inputs):
    shared = {}
    shared["l0_wq"] = np.ascontiguousarray(inputs["L0_attn_Wq"].reshape(8, 128, 1024))
    shared["l0_wk"] = np.ascontiguousarray(inputs["L0_attn_Wk"].reshape(8, 128, 1024))
    shared["l0_wv"] = np.ascontiguousarray(inputs["L0_attn_Wv"].reshape(8, 128, 1024))
    shared["l0_wo"] = np.ascontiguousarray(inputs["L0_attn_Wo"].reshape(8, 128, 1024))
    shared["l0_bq"] = np.ascontiguousarray(inputs["L0_attn_bq"].reshape(8, 128).T)
    shared["l0_bk"] = np.ascontiguousarray(inputs["L0_attn_bk"].reshape(8, 128).T)
    shared["l0_bo"] = np.ascontiguousarray(inputs["L0_attn_bo"].reshape(8, 128).T)
    shared["l0_bv_row"] = np.ascontiguousarray(inputs["L0_attn_bv"].reshape(1, 1024))
    shared["l0_rm"] = np.ascontiguousarray(inputs["L0_attn_random_matrix"])
    shared["l0_wg"] = np.ascontiguousarray(
        inputs["L0_ffn_Wg"].reshape(8, 128, 8).transpose(1, 0, 2).reshape(128, 64))
    shared["l0_bg"] = np.ascontiguousarray(inputs["L0_ffn_bg"].reshape(1, 8))
    shared["l0_w1"] = np.ascontiguousarray(inputs["L0_ffn_w1"].reshape(8, 8, 128, 2048))
    shared["l0_w3"] = np.ascontiguousarray(inputs["L0_ffn_w3"].reshape(8, 8, 128, 2048))
    shared["l0_w2"] = np.ascontiguousarray(inputs["L0_ffn_w2"].reshape(8, 16, 128, 1024))
    return shared


DEBUG_TAPS = []

def build_program(layers=(0,)):
    nc = bass.Bass("TRN2", target_bir_lowering=False)
    x_in = nc.dram_tensor("x_in", [128, NFC * T], F32, kind="ExternalInput").ap()
    x_out = nc.dram_tensor("x_out", [128, NFC * T], F32, kind="ExternalOutput").ap()
    gdram = nc.dram_tensor("gdram", [128, NFC * T], F32,
                           kind=("ExternalOutput" if "g" in DEBUG_TAPS
                                 else "Internal")).ap()
    x_spill_g = nc.dram_tensor("x_spill_g", [128, NFC * T], F32).ap()
    hw = {}
    if 0 in layers:
        hw.update(declare_l0_inputs(nc))
        if "lg" in DEBUG_TAPS:
            hw["lg_dbg"] = nc.dram_tensor("lg_dbg", [128, 128], F32,
                                          kind="ExternalOutput").ap()
    if 1 in layers:
        hw.update(declare_l1_inputs(nc))
    if 2 in layers:
        hw.update(declare_l2_inputs(nc))
    if 3 in layers:
        hw.update(declare_l3_inputs(nc))

    with ExitStack() as stack:
        with nc.Block() as block:
            ctx = Ctx(nc, stack, block)
            com = Com(ctx)
            glob = {
                "P": ctx.psum("P", [128, 4096], F32),
                "gdram": gdram,
                "x_spill": x_spill_g,
            }
            com.emit_init(ctx)
            sld = ctx.sem("xload")
            ctx.dma("sync", x_spill_g, x_in, sld)
            for eng in ENGINES:
                ctx.op(eng, lambda e: None, wait=[sld])
            ctx.flush()

            for li in layers:
                if li == 0:
                    emit_l0(ctx, com, hw, glob)
                elif li == 1:
                    emit_l1(ctx, com, hw, glob)
                elif li == 2:
                    emit_l2(ctx, com, hw, glob)
                elif li == 3:
                    emit_l3(ctx, com, hw, glob)
                else:
                    raise NotImplementedError(li)
                ctx.barrier()
                ctx.flush()

            ctx_verify_pending = True
            sst = ctx.sem("xstore")
            ctx.dma("sync", x_out, x_spill_g, sst)
            ctx.op("sync", lambda e: None, wait=[sst])
            ctx.flush()
            ctx.verify()
            build_program.last_ctx = ctx
    return nc


def shard_inputs(inputs):
    x = np.asarray(inputs["L0_attn_x"])
    shards = []
    for c in range(NCORE):
        b, j = divmod(c, 4)
        shards.append(to_fm(np.ascontiguousarray(x[b, j * T:(j + 1) * T])))
    return shards


def assemble_output(results):
    out = np.empty((B, S, D), np.float32)
    for c in range(NCORE):
        b, j = divmod(c, 4)
        out[b, j * T:(j + 1) * T] = from_fm(results[c]["x_out"], T)
    return out


def core_meta(c):
    j = c % 4
    w = np.zeros((1, 8), np.float32)
    w[0, :j] = 1.0
    return {"meta_wpref": w}


KLAYERS = (0, 1, 2, 3)


def kernel(**inputs):
    layers = KLAYERS
    shards = shard_inputs(inputs)
    shared = {}
    if 0 in layers:
        shared.update(prep_l0_inputs(inputs))
    if 1 in layers:
        shared.update(prep_l1_inputs(inputs))
    if 2 in layers:
        shared.update(prep_l2_inputs(inputs))
    if 3 in layers:
        shared.update(prep_l3_inputs(inputs))
    nc = build_program(layers=layers)
    in_maps = []
    for c in range(NCORE):
        m = dict(shared, x_in=shards[c], **core_meta(c))
        if 2 in layers:
            m.update(l2_core_meta(c))
        if 3 in layers:
            m.update(l3_core_meta(c))
        in_maps.append(m)
    res = run_bass_kernel_spmd(nc, in_maps, core_ids=list(range(NCORE)))
    return assemble_output(res.results)


# version 03 (final)
# speedup vs baseline: 1.0194x; 1.0194x over previous
import math
import numpy as np
import ml_dtypes
from contextlib import ExitStack

import concourse.bass as bass
import concourse.mybir as mybir
from concourse.bass_utils import run_bass_kernel_spmd

F32 = mybir.dt.float32
BF16 = mybir.dt.bfloat16
I32 = mybir.dt.int32
AF = mybir.ActivationFunctionType
ALU = mybir.AluOpType
AX = mybir.AxisListType.X

NCORE = 8
B = 2
S = 8192
D = 1024
T = 2048
NFC = D // 128
NTC = T // 128
NTT = T // 512
GROUPS = [[0, 1, 2, 3], [4, 5, 6, 7]]
LN_EPS = 1e-5

ENGINES = ("sync", "vector", "scalar", "tensor", "gpsimd")


class Sem:
    __slots__ = ("h", "n", "name")

    def __init__(self, h, name):
        self.h = h
        self.n = 0
        self.name = name


class Ctx:
    def __init__(self, nc, stack, block):
        self.nc = nc
        self.gstack = stack
        self.stack = stack
        self.block = block
        self.q = {e: [] for e in ENGINES}
        self.sems = {}
        self.trace = []
        self.sem_owner = {}
        self.tqueues = {e: [] for e in ENGINES}

    def verify(self):
        qs = {e: [] for e in ENGINES}
        for rec in self.trace:
            qs[rec[0]].append(rec)
        heads = {e: 0 for e in ENGINES}
        counts = {}
        progress = True
        total = sum(len(v) for v in qs.values())
        done = 0
        while progress:
            progress = False
            for e in ENGINES:
                while heads[e] < len(qs[e]):
                    _, waits, inc, _ = qs[e][heads[e]]
                    if all(counts.get(h, 0) >= v for h, v in waits):
                        if inc:
                            counts[inc[0]] = counts.get(inc[0], 0) + inc[1]
                        heads[e] += 1
                        done += 1
                        progress = True
                    else:
                        break
        if done != total:
            name2id = {id(s.h): n for n, s in self.sems.items()}
            msg = [f"DEADLOCK: {done}/{total} ops executed"]
            for e in ENGINES:
                if heads[e] < len(qs[e]):
                    _, waits, inc, nm = qs[e][heads[e]]
                    wd = [(name2id.get(h, h), v, counts.get(h, 0))
                          for h, v in waits]
                    msg.append(f"  {e}[{heads[e]}] {nm} waits {wd}")
            raise RuntimeError("\n".join(msg))

    def sem(self, name):
        if name not in self.sems:
            h = self.gstack.enter_context(self.nc.semaphore(name))
            self.sems[name] = Sem(h, name)
        return self.sems[name]

    def usem(self, name):
        return self.sem(name[:3])

    _uid = 0

    def sbuf(self, name, shape, dtype):
        Ctx._uid += 1
        return self.stack.enter_context(
            self.nc.sbuf_tensor(f"{name}_{Ctx._uid}", list(shape), dtype))

    def psum(self, name, shape, dtype=F32):
        return self.stack.enter_context(
            self.nc.psum_tensor(name, list(shape), dtype))

    def scope(self):
        from contextlib import contextmanager

        @contextmanager
        def _s():
            st = ExitStack()
            old = self.stack
            self.stack = st
            try:
                yield
                self.barrier()
            finally:
                self.flush()
                self.stack = old
                st.close()

        return _s()

    AUTO_ENGINES = ("vector", "scalar", "gpsimd")

    def op(self, eng, f, wait=(), inc=None, incn=1, chain=True, record=True):
        waits = []
        for w in wait:
            if isinstance(w, tuple):
                s, v = w
                waits.append((s.h, s.n if v is None else v))
            else:
                waits.append((w.h, w.n))
        auto = eng in self.AUTO_ENGINES
        if auto and chain:
            prev = getattr(self, "_last_done", {}).get(eng)
            if prev is not None:
                waits.append(prev)
            if inc is None:
                inc = self.sem("auto_" + eng)
                incn = 1
        incs = None
        if inc is not None:
            if inc.name != "bar":
                owner = self.sem_owner.setdefault(inc.name, eng)
                assert owner == eng, (
                    f"sem {inc.name} inc'd from {eng} and {owner}: "
                    f"multi-engine sems make coarse waits unsound")
            inc.n += incn
            incs = (inc.h, incn)
            if auto and chain and record:
                if not hasattr(self, "_last_done"):
                    self._last_done = {}
                self._last_done[eng] = (inc.h, inc.n)

        dd = {}
        for h, v in waits:
            k = id(h)
            if k not in dd or dd[k][1] < v:
                dd[k] = (h, v)
        waits = list(dd.values())
        self.trace.append((eng, [(id(h), v) for h, v in waits],
                           (id(incs[0]), incs[1]) if incs else None,
                           getattr(f, "__qualname__", "?")))

        def run(e, f=f, waits=waits, incs=incs):
            for i, (h, v) in enumerate(waits):
                e.wait_ge(h, v)
                if i % 2 == 1 and i + 1 < len(waits):
                    e.nop()
            r = f(e)
            if incs is not None:
                if r is None:
                    e.sem_inc(incs[0], incs[1])
                else:
                    r.then_inc(*incs)

        self.q[eng].append(run)

    def dma(self, eng, out, in_, sem, wait=()):
        self.op(eng, lambda e, o=out, i=in_: e.dma_start(out=o, in_=i),
                wait=wait, inc=sem, incn=16)

    def barrier(self):
        s = self.sem("bar")
        base = s.n
        for eng in ENGINES:
            self.op(eng, lambda e: None, inc=s, record=False)
        for eng in ENGINES:
            self.op(eng, lambda e, h=s.h, v=base + 5: e.wait_ge(h, v),
                    chain=False)
        if not hasattr(self, "_last_done"):
            self._last_done = {}
        for eng in self.AUTO_ENGINES:
            self._last_done[eng] = (s.h, base + 5)

    def flush(self):
        for eng in ENGINES:
            fns = self.q[eng]
            if not fns:
                continue
            self.q[eng] = []

            def body(e, fns=fns):
                for f in fns:
                    f(e)

            getattr(self.block, eng)(body)


def dma_q(ctx, q, out, in_, sem_s, sem_g, wait=()):
    if q == 0:
        ctx.dma("sync", out, in_, sem_s, wait=wait)
    else:
        ctx.op("gpsimd", lambda e, o=out, i=in_: e.dma_start(out=o, in_=i),
               wait=wait, inc=sem_g, incn=16, chain=False, record=False)


def to_fm(x2d):
    Tn, Dn = x2d.shape
    return np.ascontiguousarray(
        x2d.T.reshape(Dn // 128, 128, Tn).transpose(1, 0, 2).reshape(128, -1))


def from_fm(dev, Tn):
    nfc = dev.shape[1] // Tn
    return np.ascontiguousarray(
        dev.reshape(128, nfc, Tn).transpose(1, 0, 2).reshape(nfc * 128, Tn).T)


class Com:
    def __init__(self, ctx):
        self.ones_row = ctx.sbuf("ones_row", [1, 128], BF16)
        self.onesf_row = ctx.sbuf("onesf_row", [1, 128], F32)
        self.ones_col = ctx.sbuf("ones_col", [128, 1], BF16)
        self.ident = ctx.sbuf("ident", [128, 128], BF16)
        self.identf = ctx.sbuf("identf", [128, 128], F32)
        self.tril = ctx.sbuf("trilm", [128, 128], F32)
        self.cst = ctx.sbuf("cstc", [128, 8], F32)
        self.init_done = ctx.sem("com_init")

    def emit_init(self, ctx):
        c = self
        ctx.op("gpsimd", lambda e: e.memset(c.identf[:], 1.0))
        ctx.op("gpsimd", lambda e: e.affine_select(
            c.ident[:], c.identf[:], pattern=[[-1, 128]],
            compare_op=ALU.is_equal, fill=0.0, base=0, channel_multiplier=1))
        ctx.op("gpsimd", lambda e: e.affine_select(
            c.tril[:], c.identf[:], pattern=[[1, 128]],
            compare_op=ALU.is_ge, fill=0.0, base=0, channel_multiplier=-1))

        def gmisc(e):
            e.memset(c.ones_row[:], 1.0)
            e.memset(c.onesf_row[:], 1.0)
            e.memset(c.ones_col[:], 1.0)
            e.memset(c.cst[:, 0:1], 1.0)
            e.memset(c.cst[:, 1:2], 0.5 * math.pi)
            e.memset(c.cst[:, 2:3], LN_EPS)
            e.memset(c.cst[:, 3:4], 1e-6)
            return e.memset(c.cst[:, 4:8], 0.0)

        ctx.op("gpsimd", gmisc)
        ctx.op("gpsimd", lambda e: e.tensor_copy(c.identf[:], c.ident[:]),
               inc=self.init_done)
        for eng in ENGINES:
            ctx.op(eng, lambda e: None, wait=[self.init_done])


def emit_ln(ctx, com, xbuf, xn_bf, pstat, pbc, out_f32=None, tag="ln",
            gcols=None, bcols=None, Tn=T, cs=None):
    sv = ctx.sem("lnv")
    ss = ctx.sem("lns")
    st = ctx.sem("lnt")
    su = ctx.sem("lnu")
    sb = ctx.sem("lnb")
    NT = Tn // 512
    if cs is None:
        cs = Tn
    ctx.barrier()
    sqb = ctx.sbuf(tag + "_sqb", [128, 2 * Tn], BF16)
    rowm = ctx.sbuf(tag + "_rowm", [1, Tn], F32)
    rowv = ctx.sbuf(tag + "_rowv", [1, Tn], F32)
    rowt = ctx.sbuf(tag + "_rowt", [1, Tn], F32)
    rowr = ctx.sbuf(tag + "_rowr", [1, Tn], F32)
    bca = ctx.sbuf(tag + "_bca", [128, Tn], F32)
    bcb = ctx.sbuf(tag + "_bcb", [128, Tn], F32)
    lnt = ctx.sbuf(tag + "_lnt", [128, 512], F32)

    for f in range(NFC):
        xc = xbuf[:, f * cs:f * cs + Tn]
        ctx.op("vector", lambda e, o=xn_bf[:, f * Tn:(f + 1) * Tn], i=xc:
               e.tensor_copy(o, i), inc=sv)
        sq = sqb[:, (f % 2) * Tn:(f % 2 + 1) * Tn]
        ctx.op("scalar", lambda e, o=sq, i=xc: e.activation(o, i, AF.Square),
               wait=[(st, None)], inc=ss)
        for tt in range(NT):
            ctx.op("tensor",
                   lambda e, ps=pstat[0:1, tt * 512:(tt + 1) * 512],
                   r=xn_bf[:, f * Tn + tt * 512:f * Tn + (tt + 1) * 512],
                   s0=(f == 0), s1=(f == NFC - 1):
                   e.matmul(ps, com.ones_col[:], r, start=s0, stop=s1),
                   wait=[(sv, None)],
                   inc=(su if (f == NFC - 1) else None))
            ctx.op("tensor",
                   lambda e, ps=pstat[0:1, 2048 + tt * 512:2048 + (tt + 1) * 512],
                   r=sq[:, tt * 512:(tt + 1) * 512],
                   s0=(f == 0), s1=(f == NFC - 1):
                   e.matmul(ps, com.ones_col[:], r, start=s0, stop=s1),
                   wait=[(ss, None)],
                   inc=(st if tt == NT - 1 else None))

    mean = rowm[0:1, :]
    var = rowv[0:1, :]
    tmp = rowt[0:1, :]
    rstd = rowr[0:1, :]
    ctx.op("scalar", lambda e: e.mul(mean, pstat[0:1, 0:Tn], 1.0 / D),
           wait=[(st, None), (su, None)], inc=ss)
    ctx.op("scalar", lambda e: e.mul(var, pstat[0:1, 2048:2048 + Tn], 1.0 / D),
           inc=ss)
    ctx.op("vector", lambda e: e.tensor_mul(tmp, mean, mean),
           wait=[(ss, None)], inc=sv)
    ctx.op("vector", lambda e: e.scalar_tensor_tensor(
        var, tmp, -1.0, var, ALU.mult, ALU.add), inc=sv)
    ctx.op("scalar", lambda e: e.activation(tmp, var, AF.Ln,
                                            bias=com.cst[0:1, 2:3]),
           wait=[(sv, None)], inc=ss)
    ctx.op("scalar", lambda e: e.activation(rstd, tmp, AF.Exp, scale=-0.5),
           inc=ss)

    for tt in range(NT):
        sl = slice(tt * 512, (tt + 1) * 512)
        ctx.op("tensor", lambda e, ps=pbc[:, 0:512], r=mean[0:1, sl]:
               e.matmul(ps, com.onesf_row[:], r, start=True, stop=True),
               wait=[(ss, None), (sb, None), (sv, None)])
        ctx.op("tensor", lambda e, ps=pbc[:, 512:1024], r=rstd[0:1, sl]:
               e.matmul(ps, com.onesf_row[:], r, start=True, stop=True),
               inc=su)
        ctx.op("scalar", lambda e, o=bca[:, sl]: e.copy(o, pbc[:, 0:512]),
               wait=[(su, None)], inc=ss)
        ctx.op("vector", lambda e, o=bcb[:, sl]:
               e.tensor_copy(o, pbc[:, 512:1024]), wait=[(su, None)], inc=sb)

    for f in range(NFC):
        for tt in range(NT):
            sl = slice(f * cs + tt * 512, f * cs + (tt + 1) * 512)
            nsl = slice(f * Tn + tt * 512, f * Tn + (tt + 1) * 512)
            bsl = slice(tt * 512, (tt + 1) * 512)
            xt = xbuf[:, sl]
            dst = xn_bf[:, nsl] if out_f32 is None else out_f32[:, sl]
            ctx.op("vector", lambda e, x=xt, m=bca[:, bsl]:
                   e.tensor_sub(lnt[:], x, m),
                   wait=[(sb, None), (ss, None)], inc=sv)
            if gcols is None:
                ctx.op("vector", lambda e, o=dst, r=bcb[:, bsl]:
                       e.tensor_mul(o, lnt[:], r), inc=sv)
            else:
                ctx.op("vector", lambda e, r=bcb[:, bsl]:
                       e.tensor_mul(lnt[:], lnt[:], r), inc=sv)
                ctx.op("vector", lambda e, o=dst, f=f:
                       e.tensor_scalar(o, lnt[:], gcols[:, f:f + 1],
                                       bcols[:, f:f + 1], ALU.mult, ALU.add),
                       inc=sv)
            if out_f32 is not None:
                ctx.op("vector", lambda e, o=xn_bf[:, nsl], i=out_f32[:, sl]:
                       e.tensor_copy(o, i), inc=sv)


def emit_l0_gates(ctx, com, hw, xn_bf, gt_dram, sd, P):
    gtc = ctx.sbuf("l0_gtc", [1, 1024], F32)
    lg_dbg = hw.get("lg_dbg")
    sv = ctx.sem("f0v")
    ss = ctx.sem("f0s")
    st = ctx.sem("f0t")
    wgst = ctx.sbuf("l0_wgst", [128, 64], F32)
    wgb = ctx.sbuf("l0_wgb", [128, 64], BF16)
    bgr = ctx.sbuf("l0_bgr", [1, 8], F32)
    bgb = ctx.sbuf("l0_bgb", [1, 8], BF16)
    gs = ctx.sbuf("l0_gs", [128, 40], F32)
    m1 = ctx.sbuf("l0_m1", [128, 4], F32)
    pg = P[:, 0:8]
    pgt = P[0:1, 512:640]

    ctx.dma("sync", wgst[:], hw["l0_wg"], sd)
    ctx.dma("sync", bgr[:], hw["l0_bg"], sd)
    ctx.op("vector", lambda e: e.tensor_copy(wgb[:], wgst[:]),
           wait=[(sd, None)], inc=sv)
    ctx.op("vector", lambda e: e.tensor_copy(bgb[:], bgr[:]), inc=sv)
    for tc in range(NTC):
        for f in range(NFC):
            ctx.op("tensor",
                   lambda e, l=xn_bf[:, f * T + tc * 128:f * T + (tc + 1) * 128],
                   r=wgb[:, f * 8:(f + 1) * 8], s0=(f == 0), s1=(f == NFC - 1):
                   e.matmul(pg[:, 0:8], l, r, start=s0, stop=False),
                   wait=([(sv, None)] if f == 0 else []))
        ctx.op("tensor", lambda e: e.matmul(
            pg[:, 0:8], com.ones_row[:], bgb[:], start=False, stop=True),
            inc=st)
        if lg_dbg is not None:
            ctx.op("vector", lambda e: e.tensor_copy(gs[:, 24:32], pg[:, 0:8]),
                   wait=[(st, None)], inc=sv)
            ctx.dma("sync", lg_dbg[:, tc * 8:(tc + 1) * 8], gs[:, 24:32], sd,
                    wait=[(sv, None)])
        ctx.op("vector", lambda e: e.reduce_max(m1[:, 0:1], pg[:, 0:8], axis=AX),
               wait=[(st, None)], inc=sv)
        ctx.op("vector", lambda e: e.tensor_scalar(
            gs[:, 0:8], pg[:, 0:8], m1[:, 0:1], None, ALU.is_ge), inc=sv)
        ctx.op("vector", lambda e: e.scalar_tensor_tensor(
            gs[:, 8:16], gs[:, 0:8], -1e9, pg[:, 0:8], ALU.mult, ALU.add),
            inc=sv)
        ctx.op("vector", lambda e: e.reduce_max(m1[:, 1:2], gs[:, 8:16], axis=AX),
               inc=sv)
        ctx.op("vector", lambda e: e.tensor_scalar(
            gs[:, 16:24], gs[:, 8:16], m1[:, 1:2], None, ALU.is_ge), inc=sv)
        ctx.op("vector", lambda e: e.tensor_sub(m1[:, 2:3], m1[:, 0:1],
                                                m1[:, 1:2]), inc=sv)
        ctx.op("scalar", lambda e: e.activation(m1[:, 3:4], m1[:, 2:3],
                                                AF.Sigmoid),
               wait=[(sv, None)], inc=ss)
        ctx.op("vector", lambda e: e.tensor_scalar(
            m1[:, 2:3], m1[:, 3:4], -1.0, 1.0, ALU.mult, ALU.add),
            wait=[(ss, None)], inc=sv)
        ctx.op("vector", lambda e: e.tensor_scalar(
            gs[:, 24:32], gs[:, 0:8], m1[:, 3:4], None, ALU.mult), inc=sv)
        ctx.op("vector", lambda e: e.scalar_tensor_tensor(
            gs[:, 32:40], gs[:, 16:24], m1[:, 2:3], gs[:, 24:32],
            ALU.mult, ALU.add), inc=sv)
        for e_ in range(8):
            ctx.op("tensor", lambda e, e_=e_: e.transpose(
                pgt[0:1, 0:128], gs[:, 32 + e_:33 + e_], com.identf[:]),
                wait=[(sv, None)], inc=st)
            ctx.op("vector",
                   lambda e, o=gtc[0:1, e_ * 128:(e_ + 1) * 128]:
                   e.tensor_copy(o, pgt[0:1, 0:128]),
                   wait=[(st, None)] + ([(sd, None)] if e_ == 0 else []),
                   inc=sv)
        ctx.dma("sync", gt_dram[:, tc * 128:(tc + 1) * 128],
                gtc[0:1, :], sd, wait=[(sv, None)])
    ctx.op("sync", lambda e: None, wait=[(sd, None)])
    ctx.barrier()


def emit_l0_ffn(ctx, com, hw, xn_bf, gdram, P):
    sd = ctx.sem("f0d")
    sv = ctx.sem("f0v")
    ss = ctx.sem("f0s")
    st = ctx.sem("f0t")
    sgd = ctx.sem("f0gd")

    gt_dram = hw["gt_dram"]
    with ctx.scope():
        emit_l0_gates(ctx, com, hw, xn_bf, gt_dram, sd, P)
    grow = ctx.sbuf("l0_grow", [1, 1024], F32)

    acc = ctx.sbuf("l0_acc", [128, 8 * 1024], F32)
    xg = ctx.sbuf("l0_xg", [128, 8 * 1024], BF16)
    gb = ctx.sbuf("l0_gb", [128, 1024], F32)
    gbb = ctx.sbuf("l0_gbb", [128, 1024], BF16)
    wst = ctx.sbuf("l0_wst", [128, 2048], F32)
    w1b = ctx.sbuf("l0_w1b", [128, 4096], BF16)
    w3b = ctx.sbuf("l0_w3b", [128, 4096], BF16)
    w2b = ctx.sbuf("l0_w2b", [128, 4096], BF16)
    slt = ctx.sbuf("l0_slt", [128, 2048], F32)
    ht = ctx.sbuf("l0_ht", [128, 2048], BF16)
    gtile = ctx.sbuf("l0_gtile", [128, 512], F32)
    py = [P[:, i * 512:(i + 1) * 512] for i in range(8)]
    pw = [P[:, i * 512:(i + 1) * 512] for i in range(2)]

    for th in range(2):
        tsl = slice(th * 1024, (th + 1) * 1024)
        ctx.op("gpsimd", lambda e: e.memset(acc[:], 0.0),
               wait=[(sgd, None), (sv, None)], inc=ctx.sem("f0ga"))
        for e_ in range(8):
            ctx.dma("sync", grow[:],
                    gt_dram[e_:e_ + 1, th * 1024:(th + 1) * 1024], sd,
                    wait=[(sv, None)])
            for q in range(2):
                ctx.op("tensor",
                       lambda e, ps=pw[q], r=grow[0:1, q * 512:(q + 1) * 512]:
                       e.matmul(ps, com.onesf_row[:], r, start=True, stop=True),
                       wait=[(sv, None), (sd, None)], inc=st)
            for q in range(2):
                ctx.op("vector", lambda e, q=q:
                       e.tensor_copy(gb[:, q * 512:(q + 1) * 512], pw[q][:]),
                       wait=[(st, None)], inc=sv)
            ctx.op("vector", lambda e: e.tensor_copy(gbb[:], gb[:]), inc=sv)
            for f in range(NFC):
                ctx.op("vector",
                       lambda e, o=xg[:, f * 1024:(f + 1) * 1024],
                       i=xn_bf[:, f * T:(f + 1) * T][:, tsl]:
                       e.tensor_mul(o, i, gbb[:]),
                       wait=[(st, None)], inc=sv)
            for hh in range(4):
                sq0 = ctx.sem("f0q")
                for g2 in range(2):
                    for kf in range(4):
                        dma_q(ctx, kf % 2, wst[:, kf * 512:(kf + 1) * 512],
                              hw["l0_w1"][e_, g2 * 4 + kf, :,
                                          hh * 512:(hh + 1) * 512], sd, sq0,
                              wait=[(sv, None)] if kf < 2 else ())
                    ctx.op("vector", lambda e, g2=g2: e.tensor_copy(
                        w1b[:, g2 * 2048:(g2 + 1) * 2048], wst[:]),
                        wait=[(sd, None), (sq0, None), (st, None)], inc=sv)
                for g2 in range(2):
                    for kf in range(4):
                        dma_q(ctx, kf % 2, wst[:, kf * 512:(kf + 1) * 512],
                              hw["l0_w3"][e_, g2 * 4 + kf, :,
                                          hh * 512:(hh + 1) * 512], sd, sq0,
                              wait=[(sv, None)] if kf < 2 else ())
                    ctx.op("vector", lambda e, g2=g2: e.tensor_copy(
                        w3b[:, g2 * 2048:(g2 + 1) * 2048], wst[:]),
                        wait=[(sd, None), (sq0, None), (st, None)], inc=sv)
                for g2 in range(2):
                    for hc in range(2):
                        dma_q(ctx, hc % 2, wst[:, hc * 1024:(hc + 1) * 1024],
                              hw["l0_w2"][e_, hh * 4 + g2 * 2 + hc, :, :],
                              sd, sq0, wait=[(sv, None)])
                    ctx.op("vector", lambda e, g2=g2: e.tensor_copy(
                        w2b[:, g2 * 2048:(g2 + 1) * 2048], wst[:]),
                        wait=[(sd, None), (sq0, None), (st, None)], inc=sv)
                for tq in range(2):
                    tk = slice(th * 1024 + tq * 512, th * 1024 + (tq + 1) * 512)
                    tkh = slice(tq * 512, (tq + 1) * 512)
                    for hc in range(4):
                        for kf in range(NFC):
                            ctx.op("tensor",
                                   lambda e, ps=py[hc],
                                   l=w1b[:, kf * 512 + hc * 128:
                                         kf * 512 + (hc + 1) * 128],
                                   r=xn_bf[:, kf * T:(kf + 1) * T][:, tk],
                                   s0=(kf == 0), s1=(kf == NFC - 1):
                                   e.matmul(ps, l, r, start=s0, stop=s1),
                                   wait=([(sv, None), (ctx.sem("f0ga"), None)]
                                         if (hc == 0 and kf == 0) else []),
                                   inc=(st if kf == NFC - 1 else None))
                        for kf in range(NFC):
                            ctx.op("tensor",
                                   lambda e, ps=py[4 + hc],
                                   l=w3b[:, kf * 512 + hc * 128:
                                         kf * 512 + (hc + 1) * 128],
                                   r=xg[:, kf * 1024:(kf + 1) * 1024][:, tkh],
                                   s0=(kf == 0), s1=(kf == NFC - 1):
                                   e.matmul(ps, l, r, start=s0, stop=s1),
                                   inc=(st if kf == NFC - 1 else None))
                    for hc in range(4):
                        ctx.op("scalar", lambda e, ps=py[hc], hc=hc:
                               e.activation(slt[:, hc * 512:(hc + 1) * 512],
                                            ps[:], AF.Silu),
                               wait=[(st, None)], inc=ss)
                    for hc in range(4):
                        ctx.op("vector",
                               lambda e, o=ht[:, hc * 512:(hc + 1) * 512],
                               i=slt[:, hc * 512:(hc + 1) * 512],
                               ps=py[4 + hc]:
                               e.tensor_mul(o, i, ps[:]),
                               wait=[(ss, None), (st, None)], inc=sv)
                    for oc in range(8):
                        pwb = pw[oc % 2]
                        for hc in range(4):
                            ctx.op("tensor",
                                   lambda e, ps=pwb,
                                   l=w2b[:, hc * 1024 + oc * 128:
                                         hc * 1024 + (oc + 1) * 128],
                                   r=ht[:, hc * 512:(hc + 1) * 512],
                                   s0=(hc == 0), s1=(hc == 3):
                                   e.matmul(ps, l, r, start=s0, stop=s1),
                                   wait=([(sv, None)] if hc == 0 else []),
                                   inc=(st if hc == 3 else None))
                        ctx.op("vector",
                               lambda e, ps=pwb,
                               o=acc[:, oc * 1024:(oc + 1) * 1024][:, tkh]:
                               e.tensor_add(o, o, ps[:]),
                               wait=[(st, None)], inc=sv)
        for oc in range(8):
            for tq in range(2):
                ctx.op("scalar",
                       lambda e, i=acc[:, oc * 1024 + tq * 512:
                                       oc * 1024 + (tq + 1) * 512]:
                       e.activation(gtile[:], i, AF.Silu),
                       wait=[(sv, None), (sgd, None)], inc=ss)
                ctx.dma("sync",
                        gdram[:, oc * T + th * 1024 + tq * 512:
                              oc * T + th * 1024 + (tq + 1) * 512],
                        gtile[:], sgd, wait=[(ss, None)])
        ctx.op("sync", lambda e: None, wait=[(sgd, None)])
        ctx.barrier()


def emit_l0_attn(ctx, com, hw, xn_bf, xbuf, gdram, cc_in, cc_out, P):
    sd = ctx.sem("a0d")
    sv = ctx.sem("a0v")
    ss = ctx.sem("a0s")
    st = ctx.sem("a0t")
    scc = ctx.sem("a0cc")

    pv = P[:, 0:1024]
    pu = P[:, 1024:1280]
    pkv = P[:, 1536:4096]

    def kvtile(j):
        bank, sl = divmod(j, 7)
        return pkv[:, bank * 512 + sl * 65: bank * 512 + sl * 65 + 65]

    with ctx.scope():
        KB = ctx.sbuf("l0_kb", [128, NFC * T], BF16)
        QB = ctx.sbuf("l0_qb", [128, NFC * T], BF16)
        kvb = ctx.sbuf("l0_kvb", [128, 2080], BF16)
        qpt = ctx.sbuf("l0_qpt", [128, 1024], BF16)
        ABF = KB

        with ctx.scope():
            WS = ctx.sbuf("l0_ws", [128, 512], F32)
            WB = ctx.sbuf("l0_wb", [128, 8 * 1024], BF16)
            bcol = ctx.sbuf("l0_bcol", [128, 8], F32)
            bvrow_b = ctx.sbuf("l0_bvrowb", [1, 1024], BF16)
            rmb = ctx.sbuf("l0_rmb", [128, 256], BF16)
            vt = ctx.sbuf("l0_vt", [128, 16 * 65], BF16)
            elt = ctx.sbuf("l0_elt", [128, 512], F32)
            el2 = ctx.sbuf("l0_el2", [128, 512], F32)
            kpt = ctx.sbuf("l0_kpt", [128, 256], BF16)
            kvsb = ctx.sbuf("l0_kvsb", [128, 2080], F32)
            recb = ctx.sbuf("l0_recb", [128, 512], F32)
            drow = ctx.sbuf("l0_drow", [1, 512], F32)

            ctx.dma("sync", WS[0:64, 0:256], hw["l0_rm"], sd)
            ctx.dma("sync", WS[64:128, 0:256], hw["l0_rm"], sd)
            ctx.op("vector", lambda e: e.tensor_copy(rmb[:], WS[:, 0:256]),
                   wait=[(sd, None)], inc=sv)

            for (wname, bname, dst) in (("l0_wk", "l0_bk", KB),
                                        ("l0_wq", "l0_bq", QB)):
                ctx.dma("sync", bcol[:], hw[bname], sd, wait=[(sv, None)])
                for kf in range(NFC):
                    for hf in range(2):
                        ctx.dma("sync", WS[:],
                                hw[wname][kf, :, hf * 512:(hf + 1) * 512], sd,
                                wait=[(sv, None)])
                        ctx.op("vector", lambda e, kf=kf, hf=hf: e.tensor_copy(
                            WB[:, kf * 1024 + hf * 512:kf * 1024 + (hf + 1) * 512],
                            WS[:]),
                            wait=[(sd, None), (st, None)], inc=sv)
                for mc in range(NFC):
                    for tt in range(NTT):
                        pb = pv[:, (tt % 2) * 512:(tt % 2 + 1) * 512]
                        for kf in range(NFC):
                            ctx.op("tensor",
                                   lambda e, ps=pb,
                                   l=WB[:, kf * 1024 + mc * 128:
                                         kf * 1024 + (mc + 1) * 128],
                                   r=xn_bf[:, kf * T + tt * 512:
                                           kf * T + (tt + 1) * 512],
                                   s0=(kf == 0), s1=(kf == NFC - 1):
                                   e.matmul(ps, l, r, start=s0, stop=s1),
                                   wait=([(sv, None)] if kf == 0 else []),
                                   inc=(st if kf == NFC - 1 else None))
                        ctx.op("vector",
                               lambda e, ps=pb, mc=mc,
                               o=dst[:, mc * T + tt * 512:mc * T + (tt + 1) * 512]:
                               e.tensor_scalar(o, ps[:], bcol[:, mc:mc + 1],
                                               None, ALU.add),
                               wait=[(st, None)], inc=sv)
            ctx.barrier()

            for hf in range(2):
                ctx.dma("sync", WS[0:1, 0:512],
                        hw["l0_bv_row"][0:1, hf * 512:(hf + 1) * 512], sd,
                        wait=[(sv, None)])
                ctx.op("vector", lambda e, hf=hf: e.tensor_copy(
                    bvrow_b[0:1, hf * 512:(hf + 1) * 512], WS[0:1, 0:512]),
                    wait=[(sd, None)], inc=sv)
            for kf in range(NFC):
                for hf in range(2):
                    ctx.dma("sync", WS[:],
                            hw["l0_wv"][kf, :, hf * 512:(hf + 1) * 512], sd,
                            wait=[(sv, None)])
                    ctx.op("vector", lambda e, kf=kf, hf=hf: e.tensor_copy(
                        WB[:, kf * 1024 + hf * 512:kf * 1024 + (hf + 1) * 512],
                        WS[:]),
                        wait=[(sd, None)], inc=sv)
            ctx.op("gpsimd", lambda e: [
                e.memset(vt[:, h * 65 + 64:h * 65 + 65], 1.0)
                for h in range(16)] and None)
            ctx.barrier()

            for tc in range(NTC):
                tsl = slice(tc * 128, (tc + 1) * 128)
                for kf in range(NFC):
                    for nn in range(2):
                        ctx.op("tensor",
                               lambda e, ps=pv[:, nn * 512:(nn + 1) * 512],
                               l=xn_bf[:, kf * T:(kf + 1) * T][:, tsl],
                               r=WB[:, kf * 1024 + nn * 512:
                                     kf * 1024 + (nn + 1) * 512],
                               s0=(kf == 0):
                               e.matmul(ps, l, r, start=s0, stop=False),
                               wait=([(sv, None)] if (kf == 0 and nn == 0)
                                     else []))
                for nn in range(2):
                    ctx.op("tensor",
                           lambda e, ps=pv[:, nn * 512:(nn + 1) * 512],
                           r=bvrow_b[0:1, nn * 512:(nn + 1) * 512]:
                           e.matmul(ps, com.ones_row[:], r, start=False,
                                    stop=True),
                           inc=st)
                for h in range(16):
                    ctx.op("vector",
                           lambda e, h=h:
                           e.tensor_copy(vt[:, h * 65:h * 65 + 64],
                                         pv[:, h * 64:(h + 1) * 64]),
                           wait=[(st, None)], inc=sv)
                for h in range(16):
                    hp, hb = divmod(h, 2)
                    ctx.op("tensor",
                           lambda e, hp=hp, hb=hb, tc=tc:
                           e.matmul(pu[:, 0:256],
                                    KB[64 * hb:64 * (hb + 1),
                                       hp * T + tc * 128:hp * T + (tc + 1) * 128],
                                    rmb[64 * hb:64 * (hb + 1), :],
                                    start=True, stop=True),
                           wait=[(sv, None)], inc=st)
                    ctx.op("vector", lambda e: e.tensor_scalar(
                        elt[:, 0:256], pu[:, 0:256], 0.0, None, ALU.min),
                        wait=[(st, None), (ss, None)], inc=sv)
                    ctx.op("scalar", lambda e: e.activation(
                        el2[:, 0:256], elt[:, 0:256], AF.Exp),
                        wait=[(sv, None)], inc=ss)
                    ctx.op("vector", lambda e: e.scalar_tensor_tensor(
                        kpt[:], pu[:, 0:256], 0.0, el2[:, 0:256],
                        ALU.max, ALU.add),
                        wait=[(ss, None), (st, None)], inc=sv)
                    for mc in range(2):
                        ctx.op("tensor",
                               lambda e, ps=kvtile(h * 2 + mc), mc=mc, h=h,
                               tc=tc:
                               e.matmul(ps, kpt[:, mc * 128:(mc + 1) * 128],
                                        vt[:, h * 65:(h + 1) * 65],
                                        start=(tc == 0), stop=(tc == NTC - 1)),
                               wait=[(sv, None)],
                               inc=(st if mc == 1 else None))
            for j in range(32):
                ctx.op("vector", lambda e, j=j:
                       e.tensor_copy(kvsb[:, j * 65:(j + 1) * 65], kvtile(j)),
                       wait=[(st, None)], inc=sv)
            sg0 = ctx.sem("a0g")
            ctx.dma("gpsimd", cc_in, kvsb[:], sg0, wait=[(sv, None)])
            ctx.op("gpsimd", lambda e: e.collective_compute(
                "AllReduce", ALU.add, replica_groups=GROUPS,
                ins=[cc_in], outs=[cc_out]), wait=[(sg0, None)], inc=scc)
            ctx.dma("gpsimd", kvsb[:], cc_out, sg0, wait=[(scc, None)])
            ctx.op("vector", lambda e: e.tensor_copy(kvb[:], kvsb[:]),
                   wait=[(sg0, None)], inc=sv)
            ctx.barrier()

            for h in range(16):
                hp, hb = divmod(h, 2)
                for tt in range(NTT):
                    qsl = slice(hp * T + tt * 512, hp * T + (tt + 1) * 512)
                    for mc in range(2):
                        ctx.op("tensor",
                               lambda e, ps=pkv[:, mc * 512:(mc + 1) * 512],
                               mc=mc, hb=hb, qsl=qsl:
                               e.matmul(ps, rmb[64 * hb:64 * (hb + 1),
                                                mc * 128:(mc + 1) * 128],
                                        QB[64 * hb:64 * (hb + 1), qsl],
                                        start=True, stop=True),
                               wait=[(sv, None)], inc=st)
                        ctx.op("vector",
                               lambda e, ps=pkv[:, mc * 512:(mc + 1) * 512]:
                               e.tensor_scalar(elt[:], ps[:], 0.0, None,
                                               ALU.min),
                               wait=[(st, None), (ss, None)], inc=sv)
                        ctx.op("scalar", lambda e: e.activation(
                            el2[:], elt[:], AF.Exp),
                            wait=[(sv, None)], inc=ss)
                        ctx.op("vector",
                               lambda e, ps=pkv[:, mc * 512:(mc + 1) * 512],
                               mc=mc:
                               e.scalar_tensor_tensor(
                                   qpt[:, mc * 512:(mc + 1) * 512],
                                   ps[:], 0.0, el2[:], ALU.max, ALU.add),
                               wait=[(ss, None), (st, None)], inc=sv)
                    for mc in range(2):
                        ctx.op("tensor",
                               lambda e, mc=mc, h=h:
                               e.matmul(pv[0:65, 0:512],
                                        kvb[:, (h * 2 + mc) * 65:
                                            (h * 2 + mc + 1) * 65],
                                        qpt[:, mc * 512:(mc + 1) * 512],
                                        start=(mc == 0), stop=(mc == 1)),
                               wait=[(sv, None)],
                               inc=(st if mc == 1 else None))
                    ctx.op("vector", lambda e: e.tensor_scalar(
                        drow[0:1, 0:512], pv[64:65, 0:512], 1e-6, None,
                        ALU.max),
                        wait=[(st, None)], inc=sv)
                    ctx.op("vector", lambda e: e.reciprocal(
                        el2[0:1, 0:512], drow[0:1, 0:512]), inc=sv)
                    ctx.op("tensor", lambda e: e.matmul(
                        pv[0:64, 512:1024], com.onesf_row[0:1, 0:64],
                        el2[0:1, 0:512], start=True, stop=True),
                        wait=[(sv, None)], inc=st)
                    ctx.op("scalar", lambda e: e.copy(recb[0:64, :],
                                                      pv[0:64, 512:1024]),
                           wait=[(st, None), (sv, None)], inc=ss)
                    ctx.op("vector",
                           lambda e, hb=hb, qsl=qsl:
                           e.tensor_mul(ABF[64 * hb:64 * (hb + 1), qsl],
                                        pv[0:64, 0:512], recb[0:64, :]),
                           wait=[(ss, None)], inc=sv)

        with ctx.scope():
            WS = ctx.sbuf("l0_ws2", [128, 1024], F32)
            WB = ctx.sbuf("l0_wb2", [128, 8 * 1024], BF16)
            bcol = ctx.sbuf("l0_bcol2", [128, 8], F32)
            stile = ctx.sbuf("l0_stile", [128, 512], F32)
            gbuf = ctx.sbuf("l0_gbuf", [128, 512], F32)
            ctx.dma("sync", bcol[:], hw["l0_bo"], sd)
            for kf in range(NFC):
                ctx.dma("sync", WS[:], hw["l0_wo"][kf], sd, wait=[(sv, None)])
                ctx.op("vector", lambda e, kf=kf: e.tensor_copy(
                    WB[:, kf * 1024:(kf + 1) * 1024], WS[:]),
                    wait=[(sd, None)], inc=sv)
            ctx.barrier()
            for mc in range(NFC):
                for tt in range(NTT):
                    pb = pv[:, (tt % 2) * 512:(tt % 2 + 1) * 512]
                    sg0 = ctx.sem("a0g")
                    ctx.dma("gpsimd", gbuf[:],
                            gdram[:, mc * T + tt * 512:mc * T + (tt + 1) * 512],
                            sg0, wait=[(sv, None)])
                    for kf in range(NFC):
                        ctx.op("tensor",
                               lambda e, ps=pb,
                               l=WB[:, kf * 1024 + mc * 128:
                                     kf * 1024 + (mc + 1) * 128],
                               r=ABF[:, kf * T + tt * 512:kf * T + (tt + 1) * 512],
                               s0=(kf == 0), s1=(kf == NFC - 1):
                               e.matmul(ps, l, r, start=s0, stop=s1),
                               wait=([(sv, None)] if kf == 0 else []),
                               inc=(st if kf == NFC - 1 else None))
                    ctx.op("vector",
                           lambda e, ps=pb, mc=mc:
                           e.tensor_scalar(stile[:], ps[:], bcol[:, mc:mc + 1],
                                           None, ALU.add),
                           wait=[(st, None), (ctx.sem("a0g"), None)], inc=sv)
                    ctx.op("vector", lambda e: e.tensor_mul(
                        stile[:], stile[:], gbuf[:]), inc=sv)
                    ctx.op("vector",
                           lambda e,
                           o=xbuf[:, mc * T + tt * 512:mc * T + (tt + 1) * 512]:
                           e.tensor_add(o, o, stile[:]), inc=sv)


def emit_l0(ctx, com, hw, glob):
    P = glob["P"]
    with ctx.scope():
        glob["xbuf"] = ctx.sbuf("l0_xbuf", [128, NFC * T], F32)
        glob["xn"] = ctx.sbuf("l0_xn", [128, NFC * T], BF16)
        sld = ctx.sem("xld")
        ctx.dma("sync", glob["xbuf"][:], glob["x_spill"], sld)
        for eng in ENGINES:
            ctx.op(eng, lambda e: None, wait=[sld])
        with ctx.scope():
            emit_ln(ctx, com, glob["xbuf"], glob["xn"],
                    P[0:1, 0:4096], P[:, 0:1024], tag="ln0")
        with ctx.scope():
            emit_l0_ffn(ctx, com, hw, glob["xn"], glob["gdram"], P)
        emit_l0_attn(ctx, com, hw, glob["xn"], glob["xbuf"], glob["gdram"],
                     hw["cc_kv_in"], hw["cc_kv_out"], P)
        ctx.dma("sync", glob["x_spill"], glob["xbuf"][:], sld)
        ctx.op("sync", lambda e: None, wait=[sld])

def emit_mlp_half(ctx, com, src_bf, w1ap, b1cols, w2ap, th, acc, bufs, P):
    sd = ctx.sem("m1d")
    sv = ctx.sem("m1v")
    ss = ctx.sem("m1s")
    st = ctx.sem("m1t")
    wst, w1b, w2b, rtile = bufs["wst"], bufs["w1b"], bufs["w2b"], bufs["rtile"]
    py = [P[:, i * 512:(i + 1) * 512] for i in range(4)]
    pw = [P[:, (4 + i) * 512:(5 + i) * 512] for i in range(2)]
    ctx.barrier()

    ctx.op("gpsimd", lambda e: e.memset(acc[:], 0.0),
           wait=[(sv, None), (ctx.sem("m1g"), None)], inc=ctx.sem("m1g"))
    sg = ctx.sem("m1g")
    sq = ctx.sem("m1q")
    for hh in range(8):
        for g2 in range(2):
            for kf in range(4):
                dma_q(ctx, kf % 2, wst[:, kf * 512:(kf + 1) * 512],
                      w1ap[g2 * 4 + kf, :, hh * 512:(hh + 1) * 512], sd, sq,
                      wait=[(sv, None)])
            ctx.op("vector", lambda e, g2=g2: e.tensor_copy(
                w1b[:, g2 * 2048:(g2 + 1) * 2048], wst[:]),
                wait=[(sd, None), (sq, None), (st, None)], inc=sv)
        for g2 in range(2):
            for hc in range(2):
                dma_q(ctx, hc % 2, wst[:, hc * 1024:(hc + 1) * 1024],
                      w2ap[hh * 4 + g2 * 2 + hc, :, :], sd, sq,
                      wait=[(sv, None)])
            ctx.op("vector", lambda e, g2=g2: e.tensor_copy(
                w2b[:, g2 * 2048:(g2 + 1) * 2048], wst[:]),
                wait=[(sd, None), (sq, None), (st, None)], inc=sv)
        for tq in range(2):
            tk = slice(th * 1024 + tq * 512, th * 1024 + (tq + 1) * 512)
            tkh = slice(tq * 512, (tq + 1) * 512)
            for hc in range(4):
                for kf in range(NFC):
                    ctx.op("tensor",
                           lambda e, ps=py[hc],
                           l=w1b[:, kf * 512 + hc * 128:kf * 512 + (hc + 1) * 128],
                           r=src_bf[:, kf * T:(kf + 1) * T][:, tk],
                           s0=(kf == 0), s1=(kf == NFC - 1):
                           e.matmul(ps, l, r, start=s0, stop=s1),
                           wait=([(sv, None), (sg, None)]
                                 if (hc == 0 and kf == 0) else []),
                           inc=(st if kf == NFC - 1 else None))
            for hc in range(4):
                ctx.op("scalar", lambda e, ps=py[hc], hc=hc, hh=hh:
                       e.activation(rtile[:, hc * 512:(hc + 1) * 512], ps[:],
                                    AF.Relu, bias=b1cols[:, hh * 4 + hc:
                                                         hh * 4 + hc + 1]),
                       wait=[(st, None), (sv, None)], inc=ss)
            for oc in range(8):
                pwb = pw[oc % 2]
                for hc in range(4):
                    ctx.op("tensor",
                           lambda e, ps=pwb,
                           l=w2b[:, hc * 1024 + oc * 128:hc * 1024 + (oc + 1) * 128],
                           r=rtile[:, hc * 512:(hc + 1) * 512],
                           s0=(hc == 0), s1=(hc == 3):
                           e.matmul(ps, l, r, start=s0, stop=s1),
                           wait=([(ss, None), (sv, None)] if hc == 0 else []),
                           inc=(st if hc == 3 else None))
                ctx.op("vector",
                       lambda e, ps=pwb,
                       o=acc[:, oc * 1024:(oc + 1) * 1024][:, tkh]:
                       e.tensor_add(o, o, ps[:]),
                       wait=[(st, None)], inc=sv)


def mlp_bufs(ctx, tag):
    return {
        "wst": ctx.sbuf(tag + "_wst", [128, 2048], F32),
        "w1b": ctx.sbuf(tag + "_w1b", [128, 4096], BF16),
        "w2b": ctx.sbuf(tag + "_w2b", [128, 4096], BF16),
        "rtile": ctx.sbuf(tag + "_rtile", [128, 2048], BF16),
    }


def declare_l1_inputs(nc):
    hw = {}
    E = "ExternalInput"
    hw["l1_wq"] = nc.dram_tensor("l1_wq", [3, 8, 128, 1024], F32, kind=E).ap()
    hw["l1_wk"] = nc.dram_tensor("l1_wk", [3, 8, 128, 1024], F32, kind=E).ap()
    hw["l1_wv"] = nc.dram_tensor("l1_wv", [3, 8, 128, 1024], F32, kind=E).ap()
    hw["l1_wo"] = nc.dram_tensor("l1_wo", [3, 8, 128, 1024], F32, kind=E).ap()
    hw["l1_bq"] = nc.dram_tensor("l1_bq", [3, 128, 8], F32, kind=E).ap()
    hw["l1_bk"] = nc.dram_tensor("l1_bk", [3, 128, 8], F32, kind=E).ap()
    hw["l1_bo"] = nc.dram_tensor("l1_bo", [3, 128, 8], F32, kind=E).ap()
    hw["l1_bv_row"] = nc.dram_tensor("l1_bv_row", [3, 1, 1024], F32, kind=E).ap()
    hw["l1_om"] = nc.dram_tensor("l1_om", [3, 64, 32], F32, kind=E).ap()
    hw["l1_ln1g"] = nc.dram_tensor("l1_ln1g", [3, 128, 8], F32, kind=E).ap()
    hw["l1_ln1b"] = nc.dram_tensor("l1_ln1b", [3, 128, 8], F32, kind=E).ap()
    hw["l1_ln2g"] = nc.dram_tensor("l1_ln2g", [3, 128, 8], F32, kind=E).ap()
    hw["l1_ln2b"] = nc.dram_tensor("l1_ln2b", [3, 128, 8], F32, kind=E).ap()
    hw["l1_w1"] = nc.dram_tensor("l1_w1", [3, 8, 128, 4096], F32, kind=E).ap()
    hw["l1_bf1"] = nc.dram_tensor("l1_bf1", [3, 128, 32], F32, kind=E).ap()
    hw["l1_w2"] = nc.dram_tensor("l1_w2", [3, 32, 128, 1024], F32, kind=E).ap()
    hw["l1_bf2"] = nc.dram_tensor("l1_bf2", [3, 128, 8], F32, kind=E).ap()
    hw["l1f_w1"] = nc.dram_tensor("l1f_w1", [8, 128, 4096], F32, kind=E).ap()
    hw["l1f_b1"] = nc.dram_tensor("l1f_b1", [128, 32], F32, kind=E).ap()
    hw["l1f_w2"] = nc.dram_tensor("l1f_w2", [32, 128, 1024], F32, kind=E).ap()
    hw["l1f_b2"] = nc.dram_tensor("l1f_b2", [128, 8], F32, kind=E).ap()
    hw["l1f_gamma"] = nc.dram_tensor("l1f_gamma", [128, 8], F32, kind=E).ap()
    hw["l1f_beta"] = nc.dram_tensor("l1f_beta", [128, 8], F32, kind=E).ap()
    hw["meta_wpref"] = nc.dram_tensor("meta_wpref", [1, 8], F32, kind=E).ap()
    hw["x_spill"] = nc.dram_tensor("x_spill", [128, NFC * T], F32).ap()
    hw["adram"] = nc.dram_tensor("adram", [128, NFC * T], BF16).ap()
    hw["cc_st_in"] = nc.dram_tensor("cc_st_in", [128, 520], F32).ap()
    hw["cc_st_out"] = nc.dram_tensor("cc_st_out", [512, 520], F32).ap()
    return hw


def cols8(v):
    return np.ascontiguousarray(np.asarray(v).reshape(8, 128).T)


def prep_l1_inputs(inputs):
    _ = inputs["L1_ffn_router_w"]
    sh = {}
    sh["l1_wq"] = np.ascontiguousarray(inputs["L1_attn_Wq"].reshape(3, 8, 128, 1024))
    sh["l1_wk"] = np.ascontiguousarray(inputs["L1_attn_Wk"].reshape(3, 8, 128, 1024))
    sh["l1_wv"] = np.ascontiguousarray(inputs["L1_attn_Wv"].reshape(3, 8, 128, 1024))
    sh["l1_wo"] = np.ascontiguousarray(inputs["L1_attn_Wo"].reshape(3, 8, 128, 1024))
    sh["l1_bq"] = np.stack([cols8(inputs["L1_attn_bq"][l]) for l in range(3)])
    sh["l1_bk"] = np.stack([cols8(inputs["L1_attn_bk"][l]) for l in range(3)])
    sh["l1_bo"] = np.stack([cols8(inputs["L1_attn_bo"][l]) for l in range(3)])
    sh["l1_bv_row"] = np.ascontiguousarray(inputs["L1_attn_bv"].reshape(3, 1, 1024))
    sh["l1_om"] = np.ascontiguousarray(inputs["L1_attn_omega"])
    sh["l1_ln1g"] = np.stack([cols8(inputs["L1_attn_ln1_g"][l]) for l in range(3)])
    sh["l1_ln1b"] = np.stack([cols8(inputs["L1_attn_ln1_b"][l]) for l in range(3)])
    sh["l1_ln2g"] = np.stack([cols8(inputs["L1_attn_ln2_g"][l]) for l in range(3)])
    sh["l1_ln2b"] = np.stack([cols8(inputs["L1_attn_ln2_b"][l]) for l in range(3)])
    sh["l1_w1"] = np.ascontiguousarray(inputs["L1_attn_W1"].reshape(3, 8, 128, 4096))
    sh["l1_bf1"] = np.stack([np.ascontiguousarray(
        inputs["L1_attn_bf1"][l].reshape(32, 128).T) for l in range(3)])
    sh["l1_w2"] = np.ascontiguousarray(inputs["L1_attn_W2"].reshape(3, 32, 128, 1024))
    sh["l1_bf2"] = np.stack([cols8(inputs["L1_attn_bf2"][l]) for l in range(3)])
    sh["l1f_w1"] = np.ascontiguousarray(inputs["L1_ffn_w1"][0].reshape(8, 128, 4096))
    sh["l1f_b1"] = np.ascontiguousarray(inputs["L1_ffn_b1"][0].reshape(32, 128).T)
    sh["l1f_w2"] = np.ascontiguousarray(inputs["L1_ffn_w2"][0].reshape(32, 128, 1024))
    sh["l1f_b2"] = cols8(inputs["L1_ffn_b2"][0])
    sh["l1f_gamma"] = cols8(inputs["L1_ffn_gamma"])
    sh["l1f_beta"] = cols8(inputs["L1_ffn_beta"])
    return sh


LN8 = math.log(8.0)


def emit_l1_ffn(ctx, com, hw, glob):
    P = glob["P"]
    xbuf, xn, gdram = glob["xbuf"], glob["xn"], glob["gdram"]
    sd = ctx.sem("m1d")
    sv = ctx.sem("m1v")
    ss = ctx.sem("m1s")
    sgd = ctx.sem("m1gd")
    with ctx.scope():
        bufs = mlp_bufs(ctx, "l1f")
        acc = ctx.sbuf("l1f_acc", [128, 8 * 1024], F32)
        stage = ctx.sbuf("l1f_stage", [128, 8 * 512], BF16)
        b1c = ctx.sbuf("l1f_b1c", [128, 32], F32)
        b2c = ctx.sbuf("l1f_b2c", [128, 8], F32)
        gmc = ctx.sbuf("l1f_gmc", [128, 8], F32)
        btc = ctx.sbuf("l1f_btc", [128, 8], F32)
        gtile = ctx.sbuf("l1f_gtile", [128, 512], F32)
        ctx.dma("sync", b1c[:], hw["l1f_b1"], sd)
        ctx.dma("sync", b2c[:], hw["l1f_b2"], sd)
        ctx.dma("sync", gmc[:], hw["l1f_gamma"], sd)
        ctx.dma("sync", btc[:], hw["l1f_beta"], sd)
        ctx.op("vector", lambda e: None, wait=[(sd, None)])
        for th in range(2):
            emit_mlp_half(ctx, com, xn, hw["l1f_w1"], b1c, hw["l1f_w2"],
                          th, acc, bufs, P)
            for oc in range(NFC):
                for tq in range(2):
                    asl = slice(oc * 1024 + tq * 512, oc * 1024 + (tq + 1) * 512)
                    xsl = slice(oc * T + th * 1024 + tq * 512,
                                oc * T + th * 1024 + (tq + 1) * 512)
                    ctx.op("vector", lambda e, a=acc[:, asl], oc=oc:
                           e.tensor_scalar(a, a, b2c[:, oc:oc + 1], None,
                                           ALU.add), inc=sv)
                    ctx.op("vector", lambda e, a=acc[:, asl], x=xbuf[:, xsl]:
                           e.tensor_add(a, a, x), inc=sv)
            for q_ in range(2):
                with ctx.scope():
                    av = acc[:, q_ * 512:]
                    emit_ln(ctx, com, av, stage, P[0:1, 0:4096], P[:, 0:1024],
                            out_f32=av, tag="l1fln", gcols=gmc, bcols=btc,
                            Tn=512, cs=1024)
            for oc in range(NFC):
                for tq in range(2):
                    asl = slice(oc * 1024 + tq * 512, oc * 1024 + (tq + 1) * 512)
                    gsl = slice(oc * T + th * 1024 + tq * 512,
                                oc * T + th * 1024 + (tq + 1) * 512)
                    ctx.op("scalar", lambda e, a=acc[:, asl]:
                           e.activation(gtile[:], a, AF.Silu),
                           wait=[(sv, None), (sgd, None)], inc=ss)
                    ctx.dma("sync", gdram[:, gsl], gtile[:], sgd,
                            wait=[(ss, None)])
        ctx.op("sync", lambda e: None, wait=[(sgd, None)])


def emit_l1_sub(ctx, com, hw, glob, l):
    P = glob["P"]
    xbuf, xn = glob["xbuf"], glob["xn"]
    sd = ctx.sem("s1d")
    sv = ctx.sem("s1v")
    ss = ctx.sem("s1s")
    st = ctx.sem("s1t")
    scc = ctx.sem("s1cc")
    adram = hw["adram"]

    with ctx.scope():
        QB = ctx.sbuf("s1_qb", [128, NFC * T], BF16)
        KB = ctx.sbuf("s1_kb", [128, NFC * T], BF16)
        VT = ctx.sbuf("s1_vt", [128, 16 * 1040], BF16)
        omb2 = ctx.sbuf("s1_omb2", [128, 64], BF16)
        negc = ctx.sbuf("s1_negc", [128, 1], BF16)
        sinf2 = ctx.sbuf("s1_sinf2", [64, 1040], BF16)

        if True:
            pass
        with ctx.scope():
            WS = ctx.sbuf("s1_ws", [128, 1024], F32)
            WBmc = ctx.sbuf("s1_wbmc", [128, 8 * 128], BF16)
            bcol = ctx.sbuf("s1_bcol", [128, 8], F32)
            omf = ctx.sbuf("s1_omf", [128, 64], F32)
            for hb in range(2):
                ctx.dma("sync", omf[64 * hb:64 * (hb + 1), 0:32],
                        hw["l1_om"][l], sd)
            ctx.op("scalar", lambda e: e.mul(omf[:, 32:64], omf[:, 0:32],
                                             -(64.0 ** -0.25)),
                   wait=[(sd, None)], inc=ss)
            ctx.op("scalar", lambda e: e.mul(omf[:, 0:32], omf[:, 0:32],
                                             64.0 ** -0.25), inc=ss)
            ctx.op("vector", lambda e: e.tensor_copy(omb2[:], omf[:]),
                   wait=[(ss, None)], inc=sv)
            ctx.op("gpsimd", lambda e: e.memset(negc[:], -1.0 / 16.0))
            ctx.op("gpsimd", lambda e: e.memset(VT[:], 1.0))
            for (wname, bname, dst) in (("l1_wk", "l1_bk", KB),
                                        ("l1_wq", "l1_bq", QB)):
                ctx.dma("sync", bcol[:], hw[bname][l], sd, wait=[(sv, None)])
                for mc in range(NFC):
                    for kf in range(NFC):
                        ctx.dma("sync", WS[:, kf * 128:(kf + 1) * 128],
                                hw[wname][l, kf, :, mc * 128:(mc + 1) * 128],
                                sd, wait=[(sv, None)] if kf == 0 else ())
                    ctx.op("vector", lambda e: e.tensor_copy(WBmc[:], WS[:]),
                           wait=[(sd, None), (st, None)], inc=sv)
                    for tt in range(NTT):
                        pb = P[:, (tt % 2) * 512:(tt % 2 + 1) * 512]
                        for kf in range(NFC):
                            ctx.op("tensor",
                                   lambda e, ps=pb,
                                   l_=WBmc[:, kf * 128:(kf + 1) * 128],
                                   r=xn[:, kf * T + tt * 512:
                                        kf * T + (tt + 1) * 512],
                                   s0=(kf == 0), s1=(kf == NFC - 1):
                                   e.matmul(ps, l_, r, start=s0, stop=s1),
                                   wait=([(sv, None)] if kf == 0 else []),
                                   inc=(st if kf == NFC - 1 else None))
                        ctx.op("vector",
                               lambda e, ps=pb, mc=mc,
                               o=dst[:, mc * T + tt * 512:mc * T + (tt + 1) * 512]:
                               e.tensor_scalar(o, ps[:], bcol[:, mc:mc + 1],
                                               None, ALU.add),
                               wait=[(st, None)], inc=sv)
        with ctx.scope():
            WS = ctx.sbuf("s1v_ws", [128, 512], F32)
            WBn = ctx.sbuf("s1v_wbn", [128, 4 * 512], BF16)
            bvb = ctx.sbuf("s1v_bvb", [1, 1024], BF16)
            for hf in range(2):
                ctx.dma("sync", WS[0:1, 0:512],
                        hw["l1_bv_row"][l, 0:1, hf * 512:(hf + 1) * 512], sd,
                        wait=[(sv, None)])
                ctx.op("vector", lambda e, hf=hf: e.tensor_copy(
                    bvb[0:1, hf * 512:(hf + 1) * 512], WS[0:1, 0:512]),
                    wait=[(sd, None)], inc=sv)
            for nn in range(4):
                for kf in range(NFC):
                    ctx.dma("sync", WS[:, 0:256],
                            hw["l1_wv"][l, kf, :, nn * 256:(nn + 1) * 256],
                            sd, wait=[(sv, None)])
                    ctx.op("vector", lambda e, kf=kf: e.tensor_copy(
                        WBn[:, kf * 256:(kf + 1) * 256], WS[:, 0:256]),
                        wait=[(sd, None), (st, None)], inc=sv)
                for tc in range(NTC):
                    tsl = slice(tc * 128, (tc + 1) * 128)
                    for kf in range(NFC):
                        ctx.op("tensor",
                               lambda e, ps=P[:, 0:256],
                               l_=xn[:, kf * T:(kf + 1) * T][:, tsl],
                               r=WBn[:, kf * 256:(kf + 1) * 256],
                               s0=(kf == 0):
                               e.matmul(ps, l_, r, start=s0, stop=False),
                               wait=([(sv, None)] if kf == 0 else []))
                    ctx.op("tensor",
                           lambda e, ps=P[:, 0:256], nn=nn,
                           r=bvb[0:1, nn * 256:(nn + 1) * 256]:
                           e.matmul(ps, com.ones_row[:], r, start=False,
                                    stop=True), inc=st)
                    for h4 in range(4):
                        h = nn * 4 + h4
                        ctx.op("vector",
                               lambda e, h=h, h4=h4, tc=tc:
                               e.tensor_copy(
                                   VT[:, tc * 1040 + h * 65:
                                      tc * 1040 + h * 65 + 64],
                                   P[:, h4 * 64:(h4 + 1) * 64]),
                               wait=[(st, None)], inc=sv)
            ctx.barrier()

        with ctx.scope():
            kfb = ctx.sbuf("s1_kfb", [128, 64], BF16)
            kfbias = ctx.sbuf("s1_kfbias", [128, 1], F32)
            kff = ctx.sbuf("s1_kff", [128, 64], F32)
            kstage = ctx.sbuf("s1_kstage", [128, 128], BF16)
            kffm = ctx.sbuf("s1_kffm", [64, 128], BF16)
            qfb = ctx.sbuf("s1_qfb", [64, 128], BF16)
            atb = ctx.sbuf("s1_atb", [128, 128], BF16)
            stb = ctx.sbuf("s1_stb", [64, 65], BF16)
            stf32 = ctx.sbuf("s1_stf32", [64, 65], F32)
            stf32c = ctx.sbuf("s1_stf32c", [64, 65], F32)
            dcol = ctx.sbuf("s1_dcol", [128, 1], F32)
            rcol = ctx.sbuf("s1_rcol", [128, 1], F32)
            abuf = ctx.sbuf("s1_abuf", [128, 512], F32)
            afm = ctx.sbuf("s1_afm", [64, 128], BF16)
            stot = ctx.sbuf("s1_stot", [128, 520], F32)
            sacc = ctx.sbuf("s1_sacc", [128, 520], F32)
            wcol = ctx.sbuf("s1_wcol", [128, 8], F32)
            wrow = ctx.sbuf("s1_wrow", [1, 8], F32)

            def emit_kf(h, tc):
                hp, hb = divmod(h, 2)
                tslq = slice(hp * T + tc * 128, hp * T + (tc + 1) * 128)
                ctx.op("scalar", lambda e, hb=hb, tslq=tslq:
                       e.activation(kstage[64 * hb:64 * (hb + 1), :],
                                    KB[64 * hb:64 * (hb + 1), tslq],
                                    AF.Square),
                       wait=[(st, None)], inc=ss)
                ctx.op("tensor", lambda e, hb=hb:
                       e.matmul(P[:, 2112:2113],
                                kstage[64 * hb:64 * (hb + 1), :],
                                negc[64 * hb:64 * (hb + 1), :],
                                start=True, stop=True),
                       wait=[(ss, None)], inc=st)
                ctx.op("vector", lambda e:
                       e.tensor_scalar(kfbias[:], P[:, 2112:2113], -LN8,
                                       None, ALU.add),
                       wait=[(st, None)], inc=sv)
                ctx.op("tensor", lambda e, hb=hb, tslq=tslq:
                       e.matmul(P[:, 2048:2112],
                                KB[64 * hb:64 * (hb + 1), tslq],
                                omb2[64 * hb:64 * (hb + 1), :],
                                start=True, stop=True),
                       wait=[(ss, None)], inc=st)
                ctx.op("scalar", lambda e:
                       e.activation(kff[:, 0:64], P[:, 2048:2112], AF.Exp,
                                    bias=kfbias[:]),
                       wait=[(st, None), (sv, None)], inc=ss)
                ctx.op("vector", lambda e: e.tensor_copy(kfb[:], kff[:]),
                       wait=[(ss, None)], inc=sv)

            for h in (range(16) if "a0" in L1_PARTS else []):
                for tc in range(NTC):
                    emit_kf(h, tc)
                    ctx.op("tensor", lambda e, h=h, tc=tc:
                           e.matmul(P[0:64, 1024:1089], kfb[:, 0:64],
                                    VT[:, tc * 1040 + h * 65:
                                       tc * 1040 + (h + 1) * 65],
                                    start=(tc == 0), stop=(tc == NTC - 1)),
                           wait=[(sv, None)], inc=st)
                ctx.op("vector", lambda e, h=h:
                       e.tensor_copy(stot[64 * (h % 2):64 * (h % 2) + 64,
                                          (h // 2) * 65:(h // 2) * 65 + 65],
                                     P[0:64, 1024:1089]),
                       wait=[(st, None)], inc=sv)
            if "a0" in L1_PARTS:
                sg1 = ctx.sem("s1g")
                ctx.dma("gpsimd", hw["cc_st_in"], stot[:], sg1, wait=[(sv, None)])
                ctx.op("gpsimd", lambda e: e.collective_compute(
                    "AllGather", ALU.bypass, replica_groups=GROUPS,
                    ins=[hw["cc_st_in"]], outs=[hw["cc_st_out"]]),
                    wait=[(sg1, None)], inc=scc)
            ctx.dma("sync", wrow[:], hw["meta_wpref"], sd)
            ctx.op("tensor", lambda e: e.matmul(
                P[:, 0:8], com.onesf_row[:], wrow[:], start=True, stop=True),
                wait=[(sd, None)], inc=st)
            ctx.op("vector", lambda e: e.tensor_copy(wcol[:], P[:, 0:8]),
                   wait=[(st, None)], inc=sv)
            sg1 = ctx.sem("s1g")
            for r in range(4):
                ctx.dma("gpsimd", stot[:],
                        hw["cc_st_out"][128 * r:128 * (r + 1), :], sg1,
                        wait=[(scc, None), (sv, None)])
                if r == 0:
                    ctx.op("vector", lambda e:
                           e.tensor_scalar(sacc[:], stot[:], wcol[:, 0:1],
                                           None, ALU.mult),
                           wait=[(sg1, None)], inc=sv)
                else:
                    ctx.op("vector", lambda e, r=r:
                           e.scalar_tensor_tensor(sacc[:], stot[:],
                                                  wcol[:, r:r + 1], sacc[:],
                                                  ALU.mult, ALU.add),
                           wait=[(sg1, None)], inc=sv)
            for h in range(16):
                ctx.op("vector", lambda e, h=h:
                       e.tensor_copy(sinf2[0:64, h * 65:(h + 1) * 65],
                                     sacc[64 * (h % 2):64 * (h % 2) + 64,
                                          (h // 2) * 65:(h // 2) * 65 + 65]),
                       inc=sv)

            LV = L1_A1_LEVEL
            for h in (range(L1_A1_HEADS) if "a1" in L1_PARTS else []):
                hp, hb = divmod(h, 2)
                ctx.op("vector", lambda e: e.memset(stf32[:], 0.0), inc=sv)
                for tc in range(NTC):
                    emit_kf(h, tc)
                    tslq = slice(hp * T + tc * 128, hp * T + (tc + 1) * 128)
                    if LV >= 1:
                        ctx.op("tensor", lambda e: e.transpose(
                            P[0:64, 2176:2304], kff[:], com.identf[:]),
                            wait=[(ss, None)], inc=st)
                        ctx.op("scalar", lambda e: e.copy(kffm[:],
                                                          P[0:64, 2176:2304]),
                               wait=[(st, None)], inc=ss)
                    if LV >= 2:
                        ctx.op("tensor", lambda e, hb=hb, tslq=tslq:
                               e.matmul(P[0:64, 2304:2432],
                                        omb2[64 * hb:64 * (hb + 1), :],
                                        QB[64 * hb:64 * (hb + 1), tslq],
                                        start=True, stop=True),
                               wait=[(ss, None)], inc=st)
                        ctx.op("scalar", lambda e: e.activation(
                            qfb[:], P[0:64, 2304:2432], AF.Exp),
                            wait=[(st, None)], inc=ss)
                    if LV >= 3:
                        ctx.op("tensor", lambda e: e.matmul(
                            P[:, 2560:2688], kffm[:], qfb[:],
                            start=True, stop=True),
                            wait=[(ss, None)], inc=st)
                        ctx.op("vector", lambda e: e.tensor_mul(
                            atb[:], P[:, 2560:2688], com.tril[:]),
                            wait=[(st, None)], inc=sv)
                    if LV >= 4:
                        ctx.op("vector", lambda e, h=h: e.tensor_add(
                            stf32c[:], stf32[:],
                            sinf2[0:64, h * 65:(h + 1) * 65]), inc=sv)
                        ctx.op("vector", lambda e: e.tensor_copy(
                            stb[:], stf32c[:]), inc=sv)
                    if LV >= 5:
                        ctx.op("tensor", lambda e: e.matmul(
                            P[:, 3072:3137], qfb[:], stb[:],
                            start=True, stop=False),
                            wait=[(sv, None), (ss, None)], inc=st)
                        ctx.op("tensor", lambda e, h=h, tc=tc: e.matmul(
                            P[:, 3072:3137], atb[:],
                            VT[:, tc * 1040 + h * 65:tc * 1040 + (h + 1) * 65],
                            start=False, stop=True),
                            wait=[(sv, None)], inc=st)
                    if LV >= 4:
                        ctx.op("tensor", lambda e, h=h, tc=tc: e.matmul(
                            P[0:64, 3584:3649], kfb[:],
                            VT[:, tc * 1040 + h * 65:tc * 1040 + (h + 1) * 65],
                            start=True, stop=True),
                            wait=[(sv, None)], inc=st)
                        ctx.op("vector", lambda e: e.tensor_add(
                            stf32[:], stf32[:], P[0:64, 3584:3649]),
                            wait=[(st, None)], inc=sv)
                    if LV >= 7:
                        ctx.op("vector", lambda e: e.tensor_scalar(
                            dcol[:], P[:, 3136:3137], 1e-6, None, ALU.add),
                            wait=[(st, None)], inc=sv)
                        ctx.op("vector", lambda e: e.reciprocal(rcol[:],
                                                                dcol[:]),
                               inc=sv)
                        ctx.op("vector", lambda e, tc=tc: e.tensor_scalar(
                            abuf[:, (tc % 8) * 64:(tc % 8 + 1) * 64],
                            P[:, 3072:3136], rcol[:], None, ALU.mult),
                            inc=sv)
                    if LV >= 8 and tc % 8 == 7:
                        for tci in range(tc - 7, tc + 1):
                            ctx.op("tensor", lambda e, tci=tci: e.transpose(
                                P[0:64, 2816:2944],
                                abuf[:, (tci % 8) * 64:(tci % 8 + 1) * 64],
                                com.identf[:]),
                                wait=([(sv, None)] if tci % 8 == 0
                                      else [(ss, None)]),
                                inc=st)
                            ctx.op("scalar", lambda e: e.copy(
                                afm[:], P[0:64, 2816:2944]),
                                wait=[(st, None), (sd, None)], inc=ss)
                            ctx.dma("sync",
                                    adram[64 * hb:64 * (hb + 1),
                                          hp * T + tci * 128:
                                          hp * T + (tci + 1) * 128],
                                    afm[:], sd, wait=[(ss, None)])
            ctx.barrier()

    if "wo" not in L1_PARTS:
        return
    with ctx.scope():
        WS = ctx.sbuf("s1o_ws", [128, 512], F32)
        WB = ctx.sbuf("s1o_wb", [128, 8 * 1024], BF16)
        bcol = ctx.sbuf("s1o_bcol", [128, 8], F32)
        astage = ctx.sbuf("s1o_astage", [128, 4096], BF16)
        stile = ctx.sbuf("s1o_stile", [128, 512], F32)
        g1c = ctx.sbuf("s1o_g1c", [128, 8], F32)
        b1c = ctx.sbuf("s1o_b1c", [128, 8], F32)
        ctx.dma("sync", bcol[:], hw["l1_bo"][l], sd)
        ctx.dma("sync", g1c[:], hw["l1_ln1g"][l], sd)
        ctx.dma("sync", b1c[:], hw["l1_ln1b"][l], sd)
        for kf in range(NFC):
            for hf in range(2):
                ctx.dma("sync", WS[:],
                        hw["l1_wo"][l, kf, :, hf * 512:(hf + 1) * 512],
                        sd, wait=[(sv, None)])
                ctx.op("vector", lambda e, kf=kf, hf=hf: e.tensor_copy(
                    WB[:, kf * 1024 + hf * 512:kf * 1024 + (hf + 1) * 512],
                    WS[:]),
                    wait=[(sd, None)], inc=sv)
        ctx.barrier()
        for tt in range(NTT):
            sg1 = ctx.sem("s1g")
            for kf in range(NFC):
                ctx.dma("gpsimd", astage[:, kf * 512:(kf + 1) * 512],
                        adram[:, kf * T + tt * 512:kf * T + (tt + 1) * 512],
                        sg1, wait=[(st, None)])
            for mc in range(NFC):
                pb = P[:, (mc % 2) * 512:(mc % 2 + 1) * 512]
                for kf in range(NFC):
                    ctx.op("tensor",
                           lambda e, ps=pb,
                           l_=WB[:, kf * 1024 + mc * 128:
                                 kf * 1024 + (mc + 1) * 128],
                           r=astage[:, kf * 512:(kf + 1) * 512],
                           s0=(kf == 0), s1=(kf == NFC - 1):
                           e.matmul(ps, l_, r, start=s0, stop=s1),
                           wait=([(ctx.sem("s1g"), None), (sv, None)]
                                 if kf == 0 else []),
                           inc=(st if kf == NFC - 1 else None))
                ctx.op("vector",
                       lambda e, ps=pb, mc=mc:
                       e.tensor_scalar(stile[:], ps[:], bcol[:, mc:mc + 1],
                                       None, ALU.add),
                       wait=[(st, None)], inc=sv)
                ctx.op("vector",
                       lambda e, mc=mc, tt=tt,
                       o=xbuf[:, mc * T + tt * 512:mc * T + (tt + 1) * 512]:
                       e.tensor_add(o, o, stile[:]), inc=sv)
        with ctx.scope():
            emit_ln(ctx, com, xbuf, xn, P[0:1, 0:4096], P[:, 0:1024],
                    out_f32=xbuf, tag="s1ln1", gcols=g1c, bcols=b1c)

    if "ffn" not in L1_PARTS:
        return
    with ctx.scope():
        bufs = mlp_bufs(ctx, "s1m")
        acc = ctx.sbuf("s1m_acc", [128, 8 * 1024], F32)
        b1c = ctx.sbuf("s1m_b1c", [128, 32], F32)
        b2c = ctx.sbuf("s1m_b2c", [128, 8], F32)
        ctx.dma("sync", b1c[:], hw["l1_bf1"][l], sd)
        ctx.dma("sync", b2c[:], hw["l1_bf2"][l], sd)
        ctx.op("vector", lambda e: None, wait=[(sd, None)])
        for th in range(2):
            emit_mlp_half(ctx, com, xn, hw["l1_w1"][l], b1c, hw["l1_w2"][l],
                          th, acc, bufs, P)
            for oc in range(NFC):
                for tq in range(2):
                    asl = slice(oc * 1024 + tq * 512, oc * 1024 + (tq + 1) * 512)
                    xsl = slice(oc * T + th * 1024 + tq * 512,
                                oc * T + th * 1024 + (tq + 1) * 512)
                    ctx.op("vector",
                           lambda e, a=acc[:, asl], x=xbuf[:, xsl], oc=oc:
                           e.scalar_tensor_tensor(x, a, b2c[:, oc:oc + 1], x,
                                                  ALU.add, ALU.add),
                           wait=[(ctx.sem("m1v"), None)], inc=sv)
    with ctx.scope():
        g2c = ctx.sbuf("s1m_g2c", [128, 8], F32)
        bb2c = ctx.sbuf("s1m_bb2c", [128, 8], F32)
        ctx.dma("sync", g2c[:], hw["l1_ln2g"][l], sd)
        ctx.dma("sync", bb2c[:], hw["l1_ln2b"][l], sd)
        ctx.op("vector", lambda e: None, wait=[(sd, None)])
        emit_ln(ctx, com, xbuf, xn, P[0:1, 0:4096], P[:, 0:1024],
                out_f32=xbuf, tag="s1ln2", gcols=g2c, bcols=bb2c)


L1_SUBS = [0, 1, 2]
L1_PARTS = {"proj", "a0", "a1", "wo", "ffn"}
L1_A1_LEVEL = 99
L1_A1_HEADS = 16


def emit_l1(ctx, com, hw, glob):
    P = glob["P"]
    gdram = glob["gdram"]
    sd = ctx.sem("l1d")
    sv = ctx.sem("l1v")
    with ctx.scope():
        glob["xbuf"] = ctx.sbuf("l1_xbuf", [128, NFC * T], F32)
        glob["xn"] = ctx.sbuf("l1_xn", [128, NFC * T], BF16)
        xbuf, xn = glob["xbuf"], glob["xn"]
        ctx.dma("sync", xbuf[:], glob["x_spill"], sd)
        for eng in ENGINES:
            ctx.op(eng, lambda e: None, wait=[(sd, None)])
        with ctx.scope():
            emit_ln(ctx, com, xbuf, xn, P[0:1, 0:4096], P[:, 0:1024],
                    out_f32=xbuf, tag="l1n")
        emit_l1_ffn(ctx, com, hw, glob)
        for l in L1_SUBS:
            emit_l1_sub(ctx, com, hw, glob, l)
        with ctx.scope():
            xt = ctx.sbuf("l1u_xt", [128, 512], F32)
            gt2 = ctx.sbuf("l1u_gt", [128, 512], F32)
            stile = ctx.sbuf("l1u_stile", [128, 512], F32)
            for f in range(NFC):
                for tt in range(NTT):
                    sl = slice(f * T + tt * 512, f * T + (tt + 1) * 512)
                    ctx.dma("sync", xt[:], glob["x_spill"][:, sl], sd,
                            wait=[(sv, None)])
                    ctx.dma("gpsimd", gt2[:], gdram[:, sl], ctx.sem("l1g"),
                            wait=[(sv, None)])
                    ctx.op("vector", lambda e, sl=sl:
                           e.tensor_mul(stile[:], xbuf[:, sl], gt2[:]),
                           wait=[(sd, None), (ctx.sem("l1g"), None)], inc=sv)
                    ctx.op("vector", lambda e, sl=sl:
                           e.tensor_add(xbuf[:, sl], xt[:], stile[:]), inc=sv)
        ctx.dma("sync", glob["x_spill"], glob["xbuf"][:], sd)
        ctx.op("sync", lambda e: None, wait=[(sd, None)])


def declare_l2_inputs(nc):
    hw = {}
    E = "ExternalInput"
    _wq = nc.dram_tensor("l2_wq", [8, 128, 1024], F32, kind=E)
    hw["l2_wq_h"] = _wq
    _wk = nc.dram_tensor("l2_wupk", [2, 128, 1024], F32, kind=E)
    hw["l2_wupk_h"] = _wk
    _wv = nc.dram_tensor("l2_wupv", [2, 128, 1024], F32, kind=E)
    hw["l2_wupv_h"] = _wv
    hw["l2_wdkv"] = nc.dram_tensor("l2_wdkv", [8, 128, 256], F32, kind=E).ap()
    hw["l2_wo"] = nc.dram_tensor("l2_wo", [8, 128, 1024], F32, kind=E).ap()
    hw["l2_wg"] = nc.dram_tensor("l2_wg", [128, 64], F32, kind=E).ap()
    hw["l2_we"] = nc.dram_tensor("l2_we", [8, 8, 128, 1024], F32, kind=E).ap()
    hw["l2_be"] = nc.dram_tensor("l2_be", [8, 1024], F32, kind=E).ap()
    hw["l2_theta"] = nc.dram_tensor("l2_theta", [1, 128], F32, kind=E).ap()
    hw["l2_pos"] = nc.dram_tensor("l2_pos", [1, 8192], F32, kind=E).ap()
    hw["l2_off"] = nc.dram_tensor("l2_off", [1, 8], I32, kind=E).ap()
    hw["gt2_dram"] = nc.dram_tensor("gt2_dram", [8, T], F32).ap()
    hw["cc_x_in"] = nc.dram_tensor("cc_x_in", [128, 16384], BF16).ap()
    hw["cc_x_out"] = nc.dram_tensor("cc_x_out", [512, 16384], BF16).ap()
    hw["cc_c_in"] = nc.dram_tensor("cc_c_in", [128, 4096], BF16).ap()
    hw["cc_c_out"] = nc.dram_tensor("cc_c_out", [512, 4096], BF16).ap()
    hw["cc_o_in"] = nc.dram_tensor("cc_o_in", [128, 16384], BF16).ap()
    hw["wq_stage"] = nc.dram_tensor("wq_stage", [1024, 256], F32).ap()
    hw["wk_stage"] = nc.dram_tensor("wk_stage", [256, 256], F32).ap()
    hw["wv_stage"] = nc.dram_tensor("wv_stage", [256, 256], F32).ap()
    hw["a_stage"] = nc.dram_tensor("a_stage", [512, 4096], BF16).ap()
    _co = nc.dram_tensor("cc_o_out", [512, 16384], BF16)
    hw["cc_o_out"] = _co.ap()
    hw["cc_o_out_h"] = _co
    return hw


def prep_l2_inputs(inputs):
    sh = {}
    perm = np.zeros(1024, np.int64)
    for h in range(16):
        perm[h * 64:h * 64 + 32] = h * 64 + np.arange(0, 64, 2)
        perm[h * 64 + 32:h * 64 + 64] = h * 64 + np.arange(1, 64, 2)
    wq = np.asarray(inputs["L2_attn_W_Q"])[:, perm]
    sh["l2_wq"] = np.ascontiguousarray(wq.reshape(8, 128, 1024))
    wk = np.asarray(inputs["L2_attn_W_up_K"])[:, perm]
    sh["l2_wupk"] = np.ascontiguousarray(wk.reshape(2, 128, 1024))
    sh["l2_wupv"] = np.ascontiguousarray(
        inputs["L2_attn_W_up_V"].reshape(2, 128, 1024))
    sh["l2_wdkv"] = np.ascontiguousarray(
        inputs["L2_attn_W_down_kv"].reshape(8, 128, 256))
    sh["l2_wo"] = np.ascontiguousarray(
        inputs["L2_attn_W_O"].reshape(8, 128, 1024))
    wg = np.asarray(inputs["L2_ffn_Wg"])
    sh["l2_wg"] = np.ascontiguousarray(
        wg.reshape(8, 128, 8).transpose(1, 0, 2).reshape(128, 64))
    sh["l2_we"] = np.ascontiguousarray(
        inputs["L2_ffn_We"].reshape(8, 8, 128, 1024))
    sh["l2_be"] = np.ascontiguousarray(inputs["L2_ffn_be"])
    th = 10000.0 ** (-np.arange(32, dtype=np.float64) / 32.0)
    tc = np.zeros((1, 128), np.float32)
    for blk in range(4):
        sgn = -1.0 if blk % 2 == 0 else 1.0
        tc[0, blk * 32:(blk + 1) * 32] = sgn * th
    sh["l2_theta"] = tc
    sh["l2_pos"] = np.arange(8192, dtype=np.float32).reshape(1, 8192)
    return sh


def l2_core_meta(c):
    j = c % 4
    off = np.array([[j * 256, j * 256 + 128, j * 256,
                     j * 2048, j * 2048 + 8192, 0, 0, 0]], np.int32)
    return {"l2_off": off}


def emit_l2_gates(ctx, com, hw, xn_bf, gt_dram, sd, P):
    gtc = ctx.sbuf("l2_gtc", [1, 1024], F32)
    sv = ctx.sem("f2v")
    ss = ctx.sem("f2s")
    st = ctx.sem("f2t")
    wgst = ctx.sbuf("l2_wgst", [128, 64], F32)
    wgb = ctx.sbuf("l2_wgb", [128, 64], BF16)
    gs = ctx.sbuf("l2_gs", [128, 40], F32)
    m1 = ctx.sbuf("l2_m1", [128, 4], F32)
    pg = P[:, 0:8]
    pgt = P[0:1, 512:640]
    ctx.dma("sync", wgst[:], hw["l2_wg"], sd)
    ctx.op("vector", lambda e: e.tensor_copy(wgb[:], wgst[:]),
           wait=[(sd, None)], inc=sv)
    for tc in range(NTC):
        for f in range(NFC):
            ctx.op("tensor",
                   lambda e, l=xn_bf[:, f * T + tc * 128:f * T + (tc + 1) * 128],
                   r=wgb[:, f * 8:(f + 1) * 8], s0=(f == 0), s1=(f == NFC - 1):
                   e.matmul(pg[:, 0:8], l, r, start=s0, stop=s1),
                   wait=([(sv, None)] if f == 0 else []),
                   inc=(st if f == NFC - 1 else None))
        ctx.op("vector", lambda e: e.reduce_max(m1[:, 0:1], pg[:, 0:8],
                                                axis=AX),
               wait=[(st, None)], inc=sv)
        ctx.op("vector", lambda e: e.tensor_scalar(
            gs[:, 0:8], pg[:, 0:8], m1[:, 0:1], None, ALU.is_ge), inc=sv)
        ctx.op("vector", lambda e: e.scalar_tensor_tensor(
            gs[:, 8:16], gs[:, 0:8], -1e9, pg[:, 0:8], ALU.mult, ALU.add),
            inc=sv)
        ctx.op("vector", lambda e: e.reduce_max(m1[:, 1:2], gs[:, 8:16],
                                                axis=AX), inc=sv)
        ctx.op("vector", lambda e: e.tensor_scalar(
            gs[:, 16:24], gs[:, 8:16], m1[:, 1:2], None, ALU.is_ge), inc=sv)
        ctx.op("vector", lambda e: e.tensor_sub(m1[:, 2:3], m1[:, 0:1],
                                                m1[:, 1:2]), inc=sv)
        ctx.op("scalar", lambda e: e.activation(m1[:, 3:4], m1[:, 2:3],
                                                AF.Sigmoid),
               wait=[(sv, None)], inc=ss)
        ctx.op("vector", lambda e: e.tensor_scalar(
            m1[:, 2:3], m1[:, 3:4], -1.0, 1.0, ALU.mult, ALU.add),
            wait=[(ss, None)], inc=sv)
        ctx.op("vector", lambda e: e.tensor_scalar(
            gs[:, 24:32], gs[:, 0:8], m1[:, 3:4], None, ALU.mult), inc=sv)
        ctx.op("vector", lambda e: e.scalar_tensor_tensor(
            gs[:, 32:40], gs[:, 16:24], m1[:, 2:3], gs[:, 24:32],
            ALU.mult, ALU.add), inc=sv)
        for e_ in range(8):
            ctx.op("tensor", lambda e, e_=e_: e.transpose(
                pgt[0:1, 0:128], gs[:, 32 + e_:33 + e_], com.identf[:]),
                wait=[(sv, None)], inc=st)
            ctx.op("vector",
                   lambda e, o=gtc[0:1, e_ * 128:(e_ + 1) * 128]:
                   e.tensor_copy(o, pgt[0:1, 0:128]),
                   wait=[(st, None)] + ([(sd, None)] if e_ == 0 else []),
                   inc=sv)
        ctx.dma("sync", gt_dram[:, tc * 128:(tc + 1) * 128],
                gtc[0:1, :], sd, wait=[(sv, None)])
    ctx.op("sync", lambda e: None, wait=[(sd, None)])
    ctx.barrier()


def emit_l2_ffn(ctx, com, hw, xn_bf, gdram, P):
    sd = ctx.sem("f2d")
    sv = ctx.sem("f2v")
    ss = ctx.sem("f2s")
    st = ctx.sem("f2t")
    sgd = ctx.sem("f2gd")
    gt_dram = hw["gt2_dram"]
    with ctx.scope():
        emit_l2_gates(ctx, com, hw, xn_bf, gt_dram, sd, P)
    with ctx.scope():
        grow = ctx.sbuf("l2_grow", [1, T], F32)
        gtsb = ctx.sbuf("l2_gtsb", [8, T], F32)
        gtb = ctx.sbuf("l2_gtb", [8, T], BF16)
        besb = ctx.sbuf("l2_besb", [8, 1024], F32)
        beb = ctx.sbuf("l2_beb", [8, 1024], BF16)
        acc = ctx.sbuf("l2_acc", [128, NFC * T], F32)
        xg = ctx.sbuf("l2_xg", [128, NFC * T], BF16)
        gb = ctx.sbuf("l2_gb", [128, 512], F32)
        gbb = ctx.sbuf("l2_gbb", [128, T], BF16)
        WS2 = ctx.sbuf("l2_ws", [128, 256], F32)
        web = ctx.sbuf("l2_web", [128, 8 * 128], BF16)
        gtile = ctx.sbuf("l2_gtile", [128, 512], F32)
        pw = [P[:, i * 512:(i + 1) * 512] for i in range(2)]
        ctx.dma("sync", gtsb[:], gt_dram, sd)
        ctx.dma("sync", besb[:], hw["l2_be"], sd)
        ctx.op("vector", lambda e: e.tensor_copy(gtb[:], gtsb[:]),
               wait=[(sd, None)], inc=sv)
        ctx.op("vector", lambda e: e.tensor_copy(beb[:], besb[:]), inc=sv)
        for e_ in range(8):
            ctx.dma("sync", grow[:], gt_dram[e_:e_ + 1, :], sd,
                    wait=[(sv, None)])
            for q in range(NTT):
                ctx.op("tensor",
                       lambda e, ps=pw[q % 2], q=q,
                       r=grow[0:1, q * 512:(q + 1) * 512]:
                       e.matmul(ps, com.onesf_row[:], r, start=True,
                                stop=True),
                       wait=[(sv, None), (sd, None)], inc=st)
                ctx.op("vector", lambda e, ps=pw[q % 2], q=q: e.tensor_copy(
                    gbb[:, q * 512:(q + 1) * 512], ps[:]),
                    wait=[(st, None)], inc=sv)
            for f in range(NFC):
                ctx.op("vector",
                       lambda e, o=xg[:, f * T:(f + 1) * T],
                       i=xn_bf[:, f * T:(f + 1) * T]:
                       e.tensor_mul(o, i, gbb[:]), inc=sv)
            sq2 = ctx.sem("f2q")
            for mc in range(NFC):
                for kf in range(NFC):
                    dma_q(ctx, kf % 2, WS2[:, (kf % 2) * 128:
                                           (kf % 2) * 128 + 128],
                          hw["l2_we"][e_, kf, :, mc * 128:(mc + 1) * 128],
                          sd, sq2, wait=[(sv, None)])
                    ctx.op("vector", lambda e, kf=kf: e.tensor_copy(
                        web[:, kf * 128:(kf + 1) * 128],
                        WS2[:, (kf % 2) * 128:(kf % 2) * 128 + 128]),
                        wait=[(sd, None), (sq2, None), (st, None)], inc=sv)
                for tt in range(NTT):
                    pb = pw[tt % 2]
                    for kf in range(NFC):
                        ctx.op("tensor",
                               lambda e, ps=pb,
                               l=web[:, kf * 128:(kf + 1) * 128],
                               r=xg[:, kf * T + tt * 512:
                                    kf * T + (tt + 1) * 512],
                               s0=(kf == 0),
                               s1=(kf == NFC - 1 and e_ != 7):
                               e.matmul(ps, l, r, start=s0, stop=s1),
                               wait=([(sv, None)] if kf == 0 else []),
                               inc=(st if (kf == NFC - 1 and e_ != 7)
                                    else None))
                    if e_ == 7:
                        ctx.op("tensor",
                               lambda e, ps=pb,
                               l=beb[0:8, mc * 128:(mc + 1) * 128],
                               r=gtb[0:8, tt * 512:(tt + 1) * 512]:
                               e.matmul(ps, l, r, start=False, stop=True),
                               inc=st)
                    sl = slice(mc * T + tt * 512, mc * T + (tt + 1) * 512)
                    if e_ == 0:
                        ctx.op("vector", lambda e, ps=pb, sl=sl:
                               e.tensor_copy(acc[:, sl], ps[:]),
                               wait=[(st, None)], inc=sv)
                    else:
                        ctx.op("vector", lambda e, ps=pb, sl=sl:
                               e.tensor_add(acc[:, sl], acc[:, sl], ps[:]),
                               wait=[(st, None)], inc=sv)
        for mc in range(NFC):
            for tt in range(NTT):
                sl = slice(mc * T + tt * 512, mc * T + (tt + 1) * 512)
                ctx.op("scalar", lambda e, sl=sl: e.activation(
                    gtile[:], acc[:, sl], AF.Silu),
                    wait=[(sv, None), (sgd, None)], inc=ss)
                ctx.dma("sync", gdram[:, sl], gtile[:], sgd,
                        wait=[(ss, None)])
        ctx.op("sync", lambda e: None, wait=[(sgd, None), (sd, None)])


def emit_l2(ctx, com, hw, glob):
    P = glob["P"]
    gdram = glob["gdram"]
    x_spill = glob["x_spill"]
    sd = ctx.sem("a2d")
    sdo = ctx.sem("a2o")
    sg = ctx.sem("a2g")
    sv = ctx.sem("a2v")
    ss = ctx.sem("a2s")
    st = ctx.sem("a2t")
    sp = ctx.sem("a2p")
    scc = ctx.sem("a2cc")
    BK = [P[:, k * 512:(k + 1) * 512] for k in range(8)]
    C1 = 6.28125
    C2 = 0.0019353071795864769
    TWO_PI = C1 + C2
    PI = math.pi

    with ctx.scope():
        with ctx.scope():
            xn = ctx.sbuf("l2_xn", [128, NFC * T], BF16)
            with ctx.scope():
                xtmp = ctx.sbuf("l2_xtmp", [128, NFC * T], F32)
                ctx.dma("sync", xtmp[:], x_spill, sd)
                for eng in ENGINES:
                    ctx.op(eng, lambda e: None, wait=[(sd, None)])
                emit_ln(ctx, com, xtmp, xn, P[0:1, 0:4096], P[:, 0:1024],
                        tag="l2n")
            emit_l2_ffn(ctx, com, hw, xn, gdram, P)
            with ctx.scope():
                WS = ctx.sbuf("l2a_ws", [128, 256], F32)
                WB = ctx.sbuf("l2a_wb", [128, 8 * 256], BF16)
                CT = ctx.sbuf("l2a_ct", [128, 2 * T], BF16)
                ctx.dma("sync", hw["cc_x_in"], xn[:], sdo)
                for kf in range(NFC):
                    ctx.dma("sync", WS[:], hw["l2_wdkv"][kf, :, :], sd,
                            wait=[(sv, None)])
                    ctx.op("vector", lambda e, kf=kf: e.tensor_copy(
                        WB[:, kf * 256:(kf + 1) * 256], WS[:]),
                        wait=[(sd, None), (st, None)], inc=sv)
                for cc in range(2):
                    for tt in range(NTT):
                        pb = BK[2 + (tt % 2)]
                        for kf in range(NFC):
                            ctx.op("tensor",
                                   lambda e, ps=pb,
                                   l=WB[:, kf * 256 + cc * 128:
                                        kf * 256 + (cc + 1) * 128],
                                   r=xn[:, kf * T + tt * 512:
                                        kf * T + (tt + 1) * 512],
                                   s0=(kf == 0), s1=(kf == NFC - 1):
                                   e.matmul(ps, l, r, start=s0, stop=s1),
                                   wait=([(sv, None)] if kf == 0 else []),
                                   inc=(st if kf == NFC - 1 else None))
                        ctx.op("vector",
                               lambda e, ps=pb, cc=cc, tt=tt:
                               e.tensor_copy(
                                   CT[:, cc * T + tt * 512:
                                      cc * T + (tt + 1) * 512], ps[:]),
                               wait=[(st, None)], inc=sv)
                ctx.dma("sync", hw["cc_c_in"], CT[:], sdo,
                        wait=[(sv, None)])
                ctx.op("sync", lambda e: None,
                       wait=[(sdo, None), (sd, None)])
                ctx.barrier()
                ctx.op("gpsimd", lambda e: e.collective_compute(
                    "AllGather", ALU.bypass, replica_groups=GROUPS,
                    ins=[hw["cc_x_in"]], outs=[hw["cc_x_out"]]), inc=scc)
                ctx.op("gpsimd", lambda e: e.collective_compute(
                    "AllGather", ALU.bypass, replica_groups=GROUPS,
                    ins=[hw["cc_c_in"]], outs=[hw["cc_c_out"]]), inc=scc)
                for eng in ENGINES:
                    ctx.op(eng, lambda e: None, wait=[(scc, None)])

        with ctx.scope():
            OT = ctx.sbuf("l2_ot", [128, 2 * 8192], BF16)
            ofs = ctx.sbuf("l2_ofs", [1, 8], I32)
            ctx.dma("sync", ofs[:], hw["l2_off"], sd)
            ctx.op("gpsimd", lambda e: None, wait=[(sd, None)])

            def dstage_w(e, src, dst, nk):
                with e.register() as r1:
                    e.reg_load(r1, ofs[0:1, 0:1])
                    return e.dma_start(
                        out=dst,
                        in_=bass.AP(src, r1, [[131072, nk], [1024, 128],
                                              [1, 256]]))
            ctx.op("gpsimd", lambda e: dstage_w(e, hw["l2_wq_h"],
                                                hw["wq_stage"], 8),
                   inc=sg, incn=16)
            ctx.op("gpsimd", lambda e: dstage_w(e, hw["l2_wupk_h"],
                                                hw["wk_stage"], 2),
                   inc=sg, incn=16)
            ctx.op("gpsimd", lambda e: dstage_w(e, hw["l2_wupv_h"],
                                                hw["wv_stage"], 2),
                   inc=sg, incn=16)
            ctx.op("sync", lambda e: None, wait=[(sg, None)])
            with ctx.scope():
                QR = ctx.sbuf("l2_qr", [128, 8192], BF16)
                KR = ctx.sbuf("l2_kr", [128, 8192], BF16)
                VS = ctx.sbuf("l2_vs", [128, 2 * 64 * 65], BF16)
                XT = ctx.sbuf("l2_xt", [128, 8 * 512], BF16)
                CT2 = ctx.sbuf("l2_ct2", [128, 2 * 512], BF16)
                WQb = ctx.sbuf("l2_wqb", [128, 8 * 128], BF16)
                WKb = ctx.sbuf("l2_wkb", [128, 2 * 128], BF16)
                VWb = ctx.sbuf("l2_vwb", [128, 2 * 256], BF16)
                WS = ctx.sbuf("l2h_ws", [128, 256], F32)
                posr = ctx.sbuf("l2_posr", [1, 512], F32)
                ftile = ctx.sbuf("l2_ft", [128, 512], F32)
                nint = ctx.sbuf("l2_ni", [128, 512], I32)
                nf = ctx.sbuf("l2_nf", [128, 512], F32)
                rt = ctx.sbuf("l2_rt", [128, 512], F32)
                mk = ctx.sbuf("l2_mk", [128, 512], F32)
                Ctile = ctx.sbuf("l2_cos", [128, 512], F32)
                Stile = ctx.sbuf("l2_sin", [128, 512], F32)
                qtmp = ctx.sbuf("l2_qtmp", [128, 512], BF16)
                qsw = ctx.sbuf("l2_qsw", [128, 512], BF16)
                t1 = ctx.sbuf("l2_t1", [128, 512], F32)
                t2 = ctx.sbuf("l2_t2", [128, 512], F32)
                thc = ctx.sbuf("l2_thc", [1, 128], F32)
                Et = ctx.sbuf("l2_et", [128, 2 * 512], BF16)
                Em = ctx.sbuf("l2_em", [128, 512], BF16)
                denr = ctx.sbuf("l2_den", [1, 512], F32)
                rden = ctx.sbuf("l2_rden", [1, 512], F32)
                rbc = ctx.sbuf("l2_rbc", [64, 512], F32)
                ctx.dma("sync", thc[:], hw["l2_theta"], sd)
                for kc in range(2):
                    ctx.dma("sync", WS[:],
                            hw["wv_stage"][kc * 128:(kc + 1) * 128, :], sd,
                            wait=[(sv, None)])
                    ctx.op("vector", lambda e, kc=kc: e.tensor_copy(
                        VWb[:, kc * 256:(kc + 1) * 256], WS[:]),
                        wait=[(sd, None)], inc=sv)
                for ps_ in range(2):
                    for kf in range(NFC):
                        ctx.dma("sync", WS[:, 0:128],
                                hw["wq_stage"][kf * 128:(kf + 1) * 128,
                                               ps_ * 128:(ps_ + 1) * 128],
                                sd, wait=[(sv, None)])
                        ctx.op("vector", lambda e, kf=kf: e.tensor_copy(
                            WQb[:, kf * 128:(kf + 1) * 128], WS[:, 0:128]),
                            wait=[(sd, None), (st, None)], inc=sv)
                    for kc in range(2):
                        ctx.dma("sync", WS[:, 0:128],
                                hw["wk_stage"][kc * 128:(kc + 1) * 128,
                                               ps_ * 128:(ps_ + 1) * 128],
                                sd, wait=[(sv, None)])
                        ctx.op("vector", lambda e, kc=kc: e.tensor_copy(
                            WKb[:, kc * 128:(kc + 1) * 128], WS[:, 0:128]),
                            wait=[(sd, None), (st, None)], inc=sv)
                    ctx.op("vector", lambda e: e.memset(VS[:], 1.0),
                           wait=[(st, None)], inc=sv)
                    for tau in range(16):
                        r_ = tau // 4
                        lt = tau % 4
                        for kf in range(NFC):
                            ctx.dma("sync", XT[:, kf * 512:(kf + 1) * 512],
                                    hw["cc_x_out"][r_ * 128:(r_ + 1) * 128,
                                                   kf * T + lt * 512:
                                                   kf * T + (lt + 1) * 512],
                                    sd, wait=[(sv, None)] if kf == 0 else ())
                        for kc in range(2):
                            ctx.dma("sync", CT2[:, kc * 512:(kc + 1) * 512],
                                    hw["cc_c_out"][r_ * 128:(r_ + 1) * 128,
                                                   kc * T + lt * 512:
                                                   kc * T + (lt + 1) * 512],
                                    sd)
                        ctx.dma("sync", posr[:],
                                hw["l2_pos"][0:1, tau * 512:(tau + 1) * 512],
                                sd)
                        ctx.op("tensor", lambda e: e.matmul(
                            BK[5], thc[:], posr[:], start=True, stop=True),
                            wait=[(sd, None), (ss, None)], inc=st)
                        ctx.op("vector", lambda e: e.tensor_scalar(
                            ftile[:], BK[5][:], 1.0 / TWO_PI, None,
                            ALU.mult), wait=[(st, None)], inc=sv)
                        ctx.op("vector", lambda e: e.tensor_copy(
                            nint[:], ftile[:]), inc=sv)
                        ctx.op("vector", lambda e: e.tensor_copy(
                            nf[:], nint[:]), inc=sv)
                        ctx.op("vector", lambda e: e.scalar_tensor_tensor(
                            rt[:], nf[:], -C1, BK[5][:], ALU.mult, ALU.add),
                            inc=sv)
                        ctx.op("vector", lambda e: e.scalar_tensor_tensor(
                            rt[:], nf[:], -C2, rt[:], ALU.mult, ALU.add),
                            inc=sv)
                        ctx.op("vector", lambda e: e.tensor_scalar(
                            mk[:], rt[:], PI, None, ALU.is_ge), inc=sv)
                        ctx.op("vector", lambda e: e.scalar_tensor_tensor(
                            rt[:], mk[:], -TWO_PI, rt[:], ALU.mult,
                            ALU.add), inc=sv)
                        ctx.op("vector", lambda e: e.tensor_scalar(
                            mk[:], rt[:], -PI, None, ALU.is_ge), inc=sv)
                        ctx.op("vector", lambda e: e.tensor_scalar(
                            mk[:], mk[:], -1.0, 1.0, ALU.mult, ALU.add),
                            inc=sv)
                        ctx.op("vector", lambda e: e.scalar_tensor_tensor(
                            rt[:], mk[:], TWO_PI, rt[:], ALU.mult, ALU.add),
                            inc=sv)
                        ctx.op("scalar", lambda e: e.activation(
                            Stile[:], rt[:], AF.Sin), wait=[(sv, None)],
                            inc=ss)
                        ctx.op("scalar", lambda e: e.activation(
                            rt[:], rt[:], AF.Abs), inc=ss)
                        ctx.op("scalar", lambda e: e.activation(
                            Ctile[:], rt[:], AF.Sin, scale=-1.0,
                            bias=com.cst[:, 1:2]), inc=ss)
                        for kf in range(NFC):
                            ctx.op("tensor",
                                   lambda e, l=WQb[:, kf * 128:(kf + 1) * 128],
                                   r=XT[:, kf * 512:(kf + 1) * 512],
                                   s0=(kf == 0), s1=(kf == NFC - 1):
                                   e.matmul(BK[4], l, r, start=s0, stop=s1),
                                   wait=([(sv, None), (sd, None)]
                                         if kf == 0 else []),
                                   inc=(st if kf == NFC - 1 else None))
                        for src, dst, nmm in ((BK[4], QR, None),
                                              (None, KR, 2)):
                            if nmm == 2:
                                for kc in range(2):
                                    ctx.op("tensor",
                                           lambda e,
                                           l=WKb[:, kc * 128:(kc + 1) * 128],
                                           r=CT2[:, kc * 512:(kc + 1) * 512],
                                           s0=(kc == 0), s1=(kc == 1):
                                           e.matmul(BK[4], l, r, start=s0,
                                                    stop=s1),
                                           wait=[(sv, None)],
                                           inc=(st if kc == 1 else None))
                            ctx.op("vector", lambda e: e.tensor_copy(
                                qtmp[:], BK[4][:]),
                                wait=[(st, None)], inc=sv)
                            for bs in range(4):
                                a0 = bs * 32
                                b0 = (bs ^ 1) * 32
                                ctx.dma("gpsimd", qsw[b0:b0 + 32, :],
                                        qtmp[a0:a0 + 32, :], sg,
                                        wait=[(sv, None)] if bs == 0 else ())
                            ctx.op("vector", lambda e: e.tensor_mul(
                                t1[:], qtmp[:], Ctile[:]),
                                wait=[(ss, None)], inc=sv)
                            ctx.op("vector", lambda e: e.tensor_mul(
                                t2[:], qsw[:], Stile[:]),
                                wait=[(sg, None)], inc=sv)
                            ctx.op("vector",
                                   lambda e, dst=dst, tau=tau: e.tensor_add(
                                       dst[:, tau * 512:(tau + 1) * 512],
                                       t1[:], t2[:]), inc=sv)
                        for sub in range(4):
                            kt = tau * 4 + sub
                            for s_ in range(2):
                                for kc in range(2):
                                    ctx.op("tensor",
                                           lambda e, sub=sub, s_=s_, kc=kc,
                                           l=CT2[:, kc * 512 + sub * 128:
                                                kc * 512 + (sub + 1) * 128],
                                           r=VWb[:, kc * 256 + ps_ * 128 +
                                                s_ * 64:kc * 256 +
                                                ps_ * 128 + s_ * 64 + 64],
                                           s0=(kc == 0), s1=(kc == 1):
                                           e.matmul(BK[6][:, 0:64], l, r,
                                                    start=s0, stop=s1),
                                           wait=([(sv, None)]
                                                 if kc == 0 else []),
                                           inc=(st if kc == 1 else None))
                                ctx.op("vector",
                                       lambda e, kt=kt, s_=s_: e.tensor_copy(
                                           VS[:, s_ * 4160 + kt * 65:
                                              s_ * 4160 + kt * 65 + 64],
                                           BK[6][:, 0:64]),
                                       wait=[(st, None)], inc=sv)
                    for s_ in range(2):
                        hb = 64 * s_
                        for tau in range(16):
                            kmax = 4 * tau + 3
                            for kt in range(kmax + 1):
                                pe_ = BK[0 + (kt % 2)]
                                ctx.op("tensor",
                                       lambda e, pe_=pe_, kt=kt, tau=tau,
                                       hb=hb:
                                       e.matmul(
                                           pe_,
                                           KR[hb:hb + 64,
                                              kt * 128:(kt + 1) * 128],
                                           QR[hb:hb + 64,
                                              tau * 512:(tau + 1) * 512],
                                           start=True, stop=True),
                                       wait=[(ss, None)], inc=st)
                                eb = Et[:, (kt % 2) * 512:
                                        (kt % 2 + 1) * 512]
                                ctx.op("scalar",
                                       lambda e, pe_=pe_, eb=eb:
                                       e.activation(eb, pe_[:], AF.Exp,
                                                    scale=0.125),
                                       wait=[(st, None), (sp, None)],
                                       inc=ss)
                                if kt >= 4 * tau:
                                    ctx.op("gpsimd",
                                           lambda e, eb=eb, tau=tau, kt=kt:
                                           e.affine_select(
                                               Em[:], eb,
                                               pattern=[[1, 512]],
                                               compare_op=ALU.is_ge,
                                               fill=0.0,
                                               base=tau * 512 - kt * 128,
                                               channel_multiplier=-1),
                                           wait=[(ss, None), (st, None)],
                                           inc=sp)
                                    rhs_ap = Em[:]
                                    avw = [(sp, None)]
                                else:
                                    rhs_ap = eb
                                    avw = [(ss, None)]
                                if kt == 0:
                                    avw = avw + [(sv, None)]
                                ctx.op("tensor",
                                       lambda e, s_=s_, kt=kt,
                                       rhs_ap=rhs_ap, kmax=kmax:
                                       e.matmul(
                                           BK[2][0:65, :],
                                           VS[:, s_ * 4160 + kt * 65:
                                              s_ * 4160 + (kt + 1) * 65],
                                           rhs_ap,
                                           start=(kt == 0),
                                           stop=(kt == kmax)),
                                       wait=avw, inc=st)
                            ctx.op("vector", lambda e: e.tensor_copy(
                                denr[:], BK[2][64:65, :]),
                                wait=[(st, None)], inc=sv)
                            ctx.op("vector", lambda e: e.reciprocal(
                                rden[:], denr[:]), inc=sv)
                            ctx.op("tensor", lambda e: e.matmul(
                                BK[3][0:64, :], com.onesf_row[0:1, 0:64],
                                rden[:], start=True, stop=True),
                                wait=[(sv, None)], inc=st)
                            ctx.op("vector", lambda e: e.tensor_copy(
                                rbc[:], BK[3][0:64, :]),
                                wait=[(st, None)], inc=sv)
                            ctx.op("vector",
                                   lambda e, s_=s_, ps_=ps_, tau=tau, hb=hb:
                                   e.tensor_mul(
                                       OT[hb:hb + 64,
                                          ps_ * 8192 + tau * 512:
                                          ps_ * 8192 + (tau + 1) * 512],
                                       BK[2][0:64, :], rbc[:]),
                                   inc=sv)
            ctx.dma("sync", hw["cc_o_in"], OT[:], sdo, wait=[(sv, None)])
            ctx.op("sync", lambda e: None, wait=[(sdo, None), (sd, None)])
            ctx.op("gpsimd", lambda e: None, wait=[(sg, None)])
            ctx.barrier()
            ctx.op("gpsimd", lambda e: e.collective_compute(
                "AllGather", ALU.bypass, replica_groups=GROUPS,
                ins=[hw["cc_o_in"]], outs=[hw["cc_o_out"]]), inc=scc)
            for eng in ENGINES:
                ctx.op(eng, lambda e: None, wait=[(scc, None)])
            with ctx.scope():
                AFULL = ctx.sbuf("l2_af", [128, NFC * T], BF16)
                for r_ in range(4):
                    def dext(e, r_=r_):
                        with e.register() as r1:
                            e.reg_load(r1, ofs[0:1, 3:4])
                            e.reg_add(r1, r1, r_ * 128 * 16384)
                            return e.dma_start(
                                out=hw["a_stage"][r_ * 128:(r_ + 1) * 128,
                                                  :],
                                in_=bass.AP(hw["cc_o_out_h"], r1,
                                            [[16384, 128], [8192, 2],
                                             [1, 2048]]))
                    ctx.op("gpsimd", dext, inc=sg, incn=16)
                ctx.op("sync", lambda e: None, wait=[(sg, None)])
                for r_ in range(4):
                    for pb in range(2):
                        ctx.dma("sync",
                                AFULL[:, (2 * r_ + pb) * T:
                                      (2 * r_ + pb + 1) * T],
                                hw["a_stage"][r_ * 128:(r_ + 1) * 128,
                                              pb * T:(pb + 1) * T], sd)
                ctx.op("sync", lambda e: None, wait=[(sd, None)])
                ctx.barrier()
                WS = ctx.sbuf("l2o_ws", [128, 128], F32)
                WB = ctx.sbuf("l2o_wb", [128, 8 * 128], BF16)
                xt = ctx.sbuf("l2o_xt", [128, 512], F32)
                gt2 = ctx.sbuf("l2o_gt", [128, 512], F32)
                stile = ctx.sbuf("l2o_stile", [128, 512], F32)
                for mc in range(NFC):
                    for kf in range(NFC):
                        ctx.dma("sync", WS[:],
                                hw["l2_wo"][kf, :, mc * 128:(mc + 1) * 128],
                                sd, wait=[(sv, None)])
                        ctx.op("vector", lambda e, kf=kf: e.tensor_copy(
                            WB[:, kf * 128:(kf + 1) * 128], WS[:]),
                            wait=[(sd, None), (st, None)], inc=sv)
                    for tt in range(NTT):
                        pb = BK[2 + (tt % 2)]
                        sl = slice(mc * T + tt * 512, mc * T + (tt + 1) * 512)
                        ctx.dma("sync", xt[:], x_spill[:, sl], sd,
                                wait=[(sv, None)])
                        ctx.dma("gpsimd", gt2[:], gdram[:, sl], sg,
                                wait=[(sv, None)])
                        for kf in range(NFC):
                            ctx.op("tensor",
                                   lambda e, ps=pb, kf=kf,
                                   l=WB[:, kf * 128:(kf + 1) * 128],
                                   r=AFULL[:, kf * T + tt * 512:
                                        kf * T + (tt + 1) * 512],
                                   s0=(kf == 0), s1=(kf == NFC - 1):
                                   e.matmul(ps, l, r, start=s0, stop=s1),
                                   wait=([(sv, None)] if kf == 0 else []),
                                   inc=(st if kf == NFC - 1 else None))
                        ctx.op("vector", lambda e, ps=pb:
                               e.tensor_mul(stile[:], ps[:], gt2[:]),
                               wait=[(st, None), (sg, None), (sd, None),
                                     (sdo, None)],
                               inc=sv)
                        ctx.op("vector", lambda e: e.tensor_add(
                            stile[:], stile[:], xt[:]), inc=sv)
                        ctx.dma("sync", x_spill[:, sl], stile[:], sdo,
                                wait=[(sv, None)])
                ctx.op("sync", lambda e: None,
                       wait=[(sd, None), (sdo, None)])
                ctx.op("gpsimd", lambda e: None, wait=[(sg, None)])


def declare_l3_inputs(nc):
    hw = {}
    E = "ExternalInput"
    hw["l3_win"] = nc.dram_tensor("l3_win", [8, 128, 4096], F32, kind=E).ap()
    hw["l3_wx"] = nc.dram_tensor("l3_wx", [16, 128, 2080], F32, kind=E).ap()
    hw["l3_wout"] = nc.dram_tensor("l3_wout", [16, 128, 1024], F32, kind=E).ap()
    hw["l3_convw"] = nc.dram_tensor("l3_convw", [128, 64], F32, kind=E).ap()
    hw["l3_convb"] = nc.dram_tensor("l3_convb", [128, 16], F32, kind=E).ap()
    hw["l3_alogl"] = nc.dram_tensor("l3_alogl", [128, 256], F32, kind=E).ap()
    hw["l3_dcol"] = nc.dram_tensor("l3_dcol", [128, 16], F32, kind=E).ap()
    _rs = nc.dram_tensor("l3_redsel", [16, 128, 128], BF16, kind=E)
    hw["l3_redsel"] = _rs.ap()
    hw["l3_redsel_h"] = _rs
    hw["l3_sel8"] = nc.dram_tensor("l3_sel8", [8, 128], F32, kind=E).ap()
    hw["l3f_k0"] = nc.dram_tensor("l3f_k0", [16, 128, 4096], mybir.dt.uint8, kind=E).ap()
    hw["l3f_k1"] = nc.dram_tensor("l3f_k1", [32, 128, 4096], mybir.dt.uint8, kind=E).ap()
    hw["l3f_k2"] = nc.dram_tensor("l3f_k2", [32, 128, 1024], mybir.dt.uint8, kind=E).ap()
    hw["l3f_s"] = nc.dram_tensor("l3f_s", [1, 4], F32, kind=E).ap()
    hw["meta_l3"] = nc.dram_tensor("meta_l3", [1, 24], F32, kind=E).ap()
    hw["xidram"] = nc.dram_tensor("xidram", [2048, T], BF16)
    hw["zdram"] = nc.dram_tensor("zdram", [2048, T], BF16)
    hw["hdram"] = nc.dram_tensor("hdram", [2048, T], BF16)
    hw["ddram"] = nc.dram_tensor("ddram", [2048, T], BF16)
    hw["udram"] = nc.dram_tensor("udram", [2048, T], BF16)
    hw["dsdram"] = nc.dram_tensor("dsdram", [2048, 1], F32)
    hw["bcdram"] = nc.dram_tensor("bcdram", [32, T], F32)
    hw["cc_sl_in"] = nc.dram_tensor("cc_sl_in", [128, 48], F32).ap()
    hw["cc_sl_out"] = nc.dram_tensor("cc_sl_out", [512, 48], F32).ap()
    hw["cc_sm_in"] = nc.dram_tensor("cc_sm_in", [128, 512], F32).ap()
    hw["cc_sm_out"] = nc.dram_tensor("cc_sm_out", [512, 512], F32).ap()
    return hw


def prep_l3_inputs(inputs):
    sh = {}
    sh["l3_win"] = np.ascontiguousarray(
        inputs["L3_attn_W_in"].T.reshape(8, 128, 4096))
    sh["l3_wx"] = np.ascontiguousarray(
        inputs["L3_attn_W_x"].T.reshape(16, 128, 2080))
    sh["l3_wout"] = np.ascontiguousarray(
        inputs["L3_attn_W_out"].T.reshape(16, 128, 1024))
    cw = inputs["L3_attn_conv_w"].reshape(2048, 4)
    sh["l3_convw"] = np.ascontiguousarray(
        cw.reshape(16, 128, 4).transpose(1, 0, 2).reshape(128, 64))
    sh["l3_convb"] = np.ascontiguousarray(
        inputs["L3_attn_conv_b"].reshape(16, 128).T)
    sh["l3_alogl"] = np.ascontiguousarray(
        inputs["L3_attn_A_log"].reshape(256, 128).T)
    sh["l3_dcol"] = np.ascontiguousarray(
        inputs["L3_attn_D"].reshape(16, 128).T)
    rs = np.zeros((16, 128, 128), np.float32)
    for j in range(16):
        for p in range(128):
            rs[j, p, 8 * j + p // 16] = 1.0
    sh["l3_redsel"] = rs.astype(ml_dtypes.bfloat16)
    sel8 = np.zeros((8, 128), np.float32)
    for i in range(8):
        sel8[i, i * 16:(i + 1) * 16] = 1.0
    sh["l3_sel8"] = sel8
    sh["l3f_k0"] = np.ascontiguousarray(
        inputs["L3_ffn_k0"].astype(np.uint8).reshape(16, 128, 4096))
    sh["l3f_k1"] = np.ascontiguousarray(
        inputs["L3_ffn_k1"].astype(np.uint8).reshape(32, 128, 4096))
    sh["l3f_k2"] = np.ascontiguousarray(
        inputs["L3_ffn_k2"].astype(np.uint8).reshape(32, 128, 1024))
    sh["l3f_s"] = np.array([[float(inputs["L3_ffn_s0"]),
                             float(inputs["L3_ffn_s1"]),
                             float(inputs["L3_ffn_s2"]), 0.0]], np.float32)
    return sh


def l3_core_meta(c):
    j = c % 4
    m = np.zeros((1, 24), np.float32)
    for i in range(4):
        for k in range(4):
            m[0, i * 4 + k] = 1.0 if (i < k < j) else 0.0
    for i in range(4):
        m[0, 16 + i] = 1.0 if i < j else 0.0
    for r in range(4):
        m[0, 20 + r] = 1.0 if r == j - 1 else 0.0
    return {"meta_l3": m}


def emit_l3_ffn(ctx, com, hw, xn, gdram, P):
    sd = ctx.sem("f3d")
    sv = ctx.sem("f3v")
    ss = ctx.sem("f3s")
    st = ctx.sem("f3t")
    sgd = ctx.sem("f3gd")
    with ctx.scope():
        R1 = ctx.sbuf("l3_r1", [128, 32 * 1024], BF16)
        R2 = ctx.sbuf("l3_r2", [128, 32 * 1024], BF16)
        ks = ctx.sbuf("l3_ks", [128, 4096], mybir.dt.uint8)
        kb = ctx.sbuf("l3_kb", [128, 4096], BF16)
        scol = ctx.sbuf("l3_scol", [128, 4], F32)
        srow = ctx.sbuf("l3_srow", [1, 4], F32)
        hnegc = ctx.sbuf("l3_hneg", [128, 1], BF16)
        rsrow = ctx.sbuf("l3_rsrow", [1, 2048], BF16)
        rsf = ctx.sbuf("l3_rsf", [1, 1024], F32)
        rsl = ctx.sbuf("l3_rsl", [1, 1024], F32)
        gtile = ctx.sbuf("l3_gtile", [128, 512], F32)
        ctx.dma("sync", srow[:], hw["l3f_s"], sd)
        ctx.op("tensor", lambda e: e.matmul(
            P[0:128, 0:4], com.onesf_row[:], srow[:], start=True, stop=True),
            wait=[(sd, None)], inc=st)
        ctx.op("vector", lambda e: e.tensor_copy(scol[:], P[0:128, 0:4]),
               wait=[(st, None)], inc=sv)
        ctx.op("scalar", lambda e: e.mul(scol[:, 0:1], scol[:, 0:1], 4.0),
               wait=[(sv, None)], inc=ss)
        ctx.op("scalar", lambda e: e.mul(scol[:, 1:2], scol[:, 1:2], 4.0),
               inc=ss)
        ctx.op("scalar", lambda e: e.mul(scol[:, 2:3], scol[:, 2:3], 2.0),
               inc=ss)
        ctx.op("vector", lambda e: e.memset(hnegc[:], -0.5), inc=sv)
        for th in range(2):
            tk = slice(th * 1024, (th + 1) * 1024)
            for mc in range(32):
                for kf in range(8):
                    ctx.dma("sync", ks[:, kf * 128:(kf + 1) * 128],
                            hw["l3f_k0"][kf, :, mc * 128:(mc + 1) * 128],
                            sd, wait=[(sv, None)] if kf == 0 else ())
                    ctx.dma("sync", ks[:, 1024 + kf * 128:1024 + (kf + 1) * 128],
                            hw["l3f_k0"][8 + kf, :, mc * 128:(mc + 1) * 128],
                            sd)
                ctx.op("vector", lambda e: e.tensor_copy(
                    kb[:, 1024:2048], ks[:, 0:1024]),
                    wait=[(sd, None), (st, None)], inc=sv)
                ctx.op("vector", lambda e: e.tensor_copy(
                    kb[:, 2048:3072], ks[:, 1024:2048]), inc=sv)
                ctx.op("vector", lambda e: e.tensor_sub(
                    kb[:, 0:1024], kb[:, 1024:2048], kb[:, 2048:3072]),
                    inc=sv)
                for tq in range(2):
                    ts_ = slice(th * 1024 + tq * 512, th * 1024 + (tq + 1) * 512)
                    pb = P[:, (tq % 2) * 512:(tq % 2 + 1) * 512]
                    for kf in range(8):
                        ctx.op("tensor",
                               lambda e, ps=pb, l_=kb[:, kf * 128:(kf + 1) * 128],
                               r=xn[:, kf * T:(kf + 1) * T][:, ts_],
                               s0=(kf == 0), s1=(kf == 7):
                               e.matmul(ps, l_, r, start=s0, stop=s1),
                               wait=([(sv, None), (ss, None)] if kf == 0 else []),
                               inc=(st if kf == 7 else None))
                    ctx.op("scalar",
                           lambda e, ps=pb, mc=mc, tq=tq:
                           e.activation(R1[:, mc * 1024 + tq * 512:
                                           mc * 1024 + (tq + 1) * 512],
                                        ps[:], AF.Relu, scale=scol[:, 0:1]),
                           wait=[(st, None)], inc=ss)
            RX, NMC = R1, 32
            for tq in range(2):
                for mc in range(NMC):
                    ctx.op("tensor",
                           lambda e, mc=mc, tq=tq, RX=RX:
                           e.matmul(P[0:1, 1024 + tq * 512:1024 + (tq + 1) * 512],
                                    hnegc[:],
                                    RX[:, mc * 1024 + tq * 512:
                                       mc * 1024 + (tq + 1) * 512],
                                    start=(mc == 0), stop=(mc == NMC - 1)),
                           wait=([(ss, None), (sv, None)] if mc == 0 else []),
                           inc=(st if mc == NMC - 1 else None))
            ctx.op("vector", lambda e: e.tensor_copy(
                rsf[0:1, :], P[0:1, 1024:2048]), wait=[(st, None)], inc=sv)
            ctx.op("vector", lambda e: e.tensor_copy(
                rsrow[0:1, 0:1024], rsf[0:1, :]), inc=sv)
            ctx.op("vector", lambda e: e.tensor_sub(
                rsl[0:1, :], rsf[0:1, :], rsrow[0:1, 0:1024]), inc=sv)
            ctx.op("vector", lambda e: e.tensor_copy(
                rsrow[0:1, 1024:2048], rsl[0:1, :]), inc=sv)
            for mc in range(32):
                for kf in range(32):
                    ctx.dma("sync", ks[:, kf * 128:(kf + 1) * 128],
                            hw["l3f_k1"][kf, :, mc * 128:(mc + 1) * 128],
                            sd, wait=[(sv, None)] if kf == 0 else ())
                ctx.op("vector", lambda e: e.tensor_copy(kb[:], ks[:]),
                       wait=[(sd, None), (st, None)], inc=sv)
                for tq in range(2):
                    pb = P[:, (tq % 2) * 512:(tq % 2 + 1) * 512]
                    for kf in range(32):
                        ctx.op("tensor",
                               lambda e, ps=pb, l_=kb[:, kf * 128:(kf + 1) * 128],
                               r=R1[:, kf * 1024 + tq * 512:
                                    kf * 1024 + (tq + 1) * 512],
                               s0=(kf == 0):
                               e.matmul(ps, l_, r, start=s0, stop=False),
                               wait=([(sv, None), (ss, None)] if kf == 0 else []))
                    ctx.op("tensor",
                           lambda e, ps=pb, tq=tq:
                           e.matmul(ps, com.ones_row[:],
                                    rsrow[0:1, tq * 512:(tq + 1) * 512],
                                    start=False, stop=False))
                    ctx.op("tensor",
                           lambda e, ps=pb, tq=tq:
                           e.matmul(ps, com.ones_row[:],
                                    rsrow[0:1, 1024 + tq * 512:
                                          1024 + (tq + 1) * 512],
                                    start=False, stop=True),
                           inc=st)
                    ctx.op("scalar",
                           lambda e, ps=pb, mc=mc, tq=tq:
                           e.activation(R2[:, mc * 1024 + tq * 512:
                                           mc * 1024 + (tq + 1) * 512],
                                        ps[:], AF.Relu, scale=scol[:, 1:2]),
                           wait=[(st, None)], inc=ss)
            RX, NMC = R2, 32
            for tq in range(2):
                for mc in range(NMC):
                    ctx.op("tensor",
                           lambda e, mc=mc, tq=tq, RX=RX:
                           e.matmul(P[0:1, 1024 + tq * 512:1024 + (tq + 1) * 512],
                                    hnegc[:],
                                    RX[:, mc * 1024 + tq * 512:
                                       mc * 1024 + (tq + 1) * 512],
                                    start=(mc == 0), stop=(mc == NMC - 1)),
                           wait=([(ss, None), (sv, None)] if mc == 0 else []),
                           inc=(st if mc == NMC - 1 else None))
            ctx.op("vector", lambda e: e.tensor_copy(
                rsf[0:1, :], P[0:1, 1024:2048]), wait=[(st, None)], inc=sv)
            ctx.op("vector", lambda e: e.tensor_copy(
                rsrow[0:1, 0:1024], rsf[0:1, :]), inc=sv)
            ctx.op("vector", lambda e: e.tensor_sub(
                rsl[0:1, :], rsf[0:1, :], rsrow[0:1, 0:1024]), inc=sv)
            ctx.op("vector", lambda e: e.tensor_copy(
                rsrow[0:1, 1024:2048], rsl[0:1, :]), inc=sv)
            for mc in range(8):
                for kf in range(32):
                    ctx.dma("sync", ks[:, kf * 128:(kf + 1) * 128],
                            hw["l3f_k2"][kf, :, mc * 128:(mc + 1) * 128],
                            sd, wait=[(sv, None)] if kf == 0 else ())
                ctx.op("vector", lambda e: e.tensor_copy(kb[:], ks[:]),
                       wait=[(sd, None), (st, None)], inc=sv)
                for tq in range(2):
                    pb = P[:, (tq % 2) * 512:(tq % 2 + 1) * 512]
                    for kf in range(32):
                        ctx.op("tensor",
                               lambda e, ps=pb, l_=kb[:, kf * 128:(kf + 1) * 128],
                               r=R2[:, kf * 1024 + tq * 512:
                                    kf * 1024 + (tq + 1) * 512],
                               s0=(kf == 0):
                               e.matmul(ps, l_, r, start=s0, stop=False),
                               wait=([(sv, None), (ss, None)] if kf == 0 else []))
                    ctx.op("tensor",
                           lambda e, ps=pb, tq=tq:
                           e.matmul(ps, com.ones_row[:],
                                    rsrow[0:1, tq * 512:(tq + 1) * 512],
                                    start=False, stop=False))
                    ctx.op("tensor",
                           lambda e, ps=pb, tq=tq:
                           e.matmul(ps, com.ones_row[:],
                                    rsrow[0:1, 1024 + tq * 512:
                                          1024 + (tq + 1) * 512],
                                    start=False, stop=True),
                           inc=st)
                    ctx.op("scalar",
                           lambda e, ps=pb: e.activation(
                               gtile[:], ps[:], AF.Silu, scale=scol[:, 2:3]),
                           wait=[(st, None), (sgd, None)], inc=ss)
                    ctx.dma("sync",
                            gdram[:, mc * T + th * 1024 + tq * 512:
                                  mc * T + th * 1024 + (tq + 1) * 512],
                            gtile[:], sgd, wait=[(ss, None)])
        ctx.op("sync", lambda e: None, wait=[(sgd, None)])


def emit_l3(ctx, com, hw, glob):
    P = glob["P"]
    gdram = glob["gdram"]
    x_spill = glob["x_spill"]
    sd = ctx.sem("a3d")
    sg = ctx.sem("a3g")
    sv = ctx.sem("a3v")
    ss = ctx.sem("a3s")
    st = ctx.sem("a3t")
    sp = ctx.sem("a3p")
    sdo = ctx.sem("a3o")
    scc = ctx.sem("a3cc")
    BK = [P[:, k * 512:(k + 1) * 512] for k in range(8)]

    def rep_bc(row0):
        return bass.AP(hw["bcdram"], row0 * T, [[0, 8], [T, 16], [1, T]])

    with ctx.scope():
        slin = ctx.sbuf("l3_slin", [128, 48], F32)
        mcol = ctx.sbuf("l3_mcol", [128, 24], F32)
        HIN = ctx.sbuf("l3_hin", [128, 256], F32)
        ACOL = ctx.sbuf("l3_acol", [128, 256], F32)
        mrow = ctx.sbuf("l3_mrow", [1, 24], F32)
        sel8f = ctx.sbuf("l3_sel8f", [8, 128], F32)
        sel8b = ctx.sbuf("l3_sel8b", [8, 128], BF16)
        ctx.dma("sync", mrow[:], hw["meta_l3"], sd)
        ctx.dma("sync", sel8f[:], hw["l3_sel8"], sd)
        ctx.dma("sync", ACOL[:], hw["l3_alogl"], sd)
        ctx.op("tensor", lambda e: e.matmul(
            P[:, 0:24], com.onesf_row[:], mrow[:], start=True, stop=True),
            wait=[(sd, None)], inc=st)
        ctx.op("vector", lambda e: e.tensor_copy(mcol[:], P[:, 0:24]),
               wait=[(st, None)], inc=sv)
        ctx.op("vector", lambda e: e.tensor_copy(sel8b[:], sel8f[:]),
               wait=[(sd, None)], inc=sv)
        ctx.op("scalar", lambda e: e.activation(ACOL[:], ACOL[:], AF.Exp),
               wait=[(sd, None)], inc=ss)
        ctx.op("scalar", lambda e: e.mul(ACOL[:], ACOL[:], -1.0), inc=ss)
        ctx.op("sync", lambda e: None, wait=[(sd, None)])
        ctx.barrier()

        with ctx.scope():
            xn = ctx.sbuf("l3_xn", [128, NFC * T], BF16)
            with ctx.scope():
                xtmp = ctx.sbuf("l3_xtmp", [128, NFC * T], F32)
                ctx.dma("sync", xtmp[:], x_spill, sd)
                for eng in ENGINES:
                    ctx.op(eng, lambda e: None, wait=[(sd, None)])
                emit_ln(ctx, com, xtmp, xn, P[0:1, 0:4096], P[:, 0:1024],
                        tag="l3n")
            emit_l3_ffn(ctx, com, hw, xn, gdram, P)

            with ctx.scope():
                WS = ctx.sbuf("l3_ws", [128, 128], F32)
                WB = ctx.sbuf("l3_wb", [128, 8 * 128], BF16)
                xic = ctx.sbuf("l3_xic", [128, T], BF16)
                sliv = ctx.sbuf("l3_sliv", [128, 48], F32)
                slre = ctx.sbuf("l3_slre", [128, 48], F32)
                for part, dram in ((0, hw["xidram"]), (1, hw["zdram"])):
                    for ec in range(16):
                        for kf in range(NFC):
                            ctx.dma("sync", WS[:],
                                    hw["l3_win"][kf, :,
                                                 part * 2048 + ec * 128:
                                                 part * 2048 + (ec + 1) * 128],
                                    sd, wait=[(sv, None)])
                            ctx.op("vector", lambda e, kf=kf: e.tensor_copy(
                                WB[:, kf * 128:(kf + 1) * 128], WS[:]),
                                wait=[(sd, None), (st, None)], inc=sv)
                        for tt in range(NTT):
                            pb = BK[2 + (tt % 2)]
                            for kf in range(NFC):
                                ctx.op("tensor",
                                       lambda e, ps=pb,
                                       l_=WB[:, kf * 128:(kf + 1) * 128],
                                       r=xn[:, kf * T + tt * 512:
                                            kf * T + (tt + 1) * 512],
                                       s0=(kf == 0), s1=(kf == NFC - 1):
                                       e.matmul(ps, l_, r, start=s0, stop=s1),
                                       wait=([(sv, None), (ss, None)]
                                             if kf == 0 else []),
                                       inc=(st if kf == NFC - 1 else None))
                            if part == 0:
                                ctx.op("vector",
                                       lambda e, ps=pb, tt=tt:
                                       e.tensor_copy(
                                           xic[:, tt * 512:(tt + 1) * 512],
                                           ps[:]),
                                       wait=[(st, None)] +
                                       ([(sdo, None)] if tt == 0 else []),
                                       inc=sv)
                            else:
                                ctx.op("scalar",
                                       lambda e, ps=pb, tt=tt:
                                       e.activation(
                                           xic[:, tt * 512:(tt + 1) * 512],
                                           ps[:], AF.Silu),
                                       wait=[(st, None)] +
                                       ([(sdo, None)] if tt == 0 else []),
                                       inc=ss)
                        if part == 0:
                            ctx.op("vector", lambda e, ec=ec: e.tensor_copy(
                                sliv[:, ec * 3:(ec + 1) * 3],
                                xic[:, 2045:2048]), inc=sv)
                        ctx.dma("sync", dram[ec * 128:(ec + 1) * 128, :],
                                xic[:], sdo, wait=[(sv, None), (ss, None)])
                ctx.dma("gpsimd", hw["cc_sl_in"], sliv[:], sg,
                        wait=[(sv, None)])
                ctx.op("gpsimd", lambda e: e.collective_compute(
                    "AllGather", ALU.bypass, replica_groups=GROUPS,
                    ins=[hw["cc_sl_in"]], outs=[hw["cc_sl_out"]]),
                    wait=[(sg, None)], inc=scc)
                for r in range(4):
                    ctx.dma("gpsimd", slre[:],
                            hw["cc_sl_out"][128 * r:128 * (r + 1), :], sg,
                            wait=[(scc, None), (sv, None)])
                    if r == 0:
                        ctx.op("vector", lambda e: e.tensor_scalar(
                            slin[:], slre[:], mcol[:, 20:21], None, ALU.mult),
                            wait=[(sg, None)], inc=sv)
                    else:
                        ctx.op("vector", lambda e, r=r:
                               e.scalar_tensor_tensor(
                                   slin[:], slre[:], mcol[:, 20 + r:21 + r],
                                   slin[:], ALU.mult, ALU.add),
                               wait=[(sg, None)], inc=sv)
                ctx.op("sync", lambda e: None,
                       wait=[(sd, None), (sdo, None)])
                ctx.op("gpsimd", lambda e: None, wait=[(sg, None)])

        with ctx.scope():
            hc = ctx.sbuf("l3_hc", [128, 16 * T], BF16)
            xie = ctx.sbuf("l3_xie", [128, T + 3], BF16)
            cwc = ctx.sbuf("l3_cwc", [128, 64], F32)
            cbc = ctx.sbuf("l3_cbc", [128, 16], F32)
            cvt = ctx.sbuf("l3_cvt", [128, T], F32)
            WS = ctx.sbuf("l3b_ws", [128, 128], F32)
            WB = ctx.sbuf("l3b_wb", [128, 16 * 128], BF16)
            dbst = ctx.sbuf("l3_dbst", [128, T], BF16)
            ub = ctx.sbuf("l3_ub", [128, T], BF16)
            sp1 = ctx.sbuf("l3_sp1", [128, T], F32)
            sp2 = ctx.sbuf("l3_sp2", [128, T], F32)
            dsums = ctx.sbuf("l3_dsums", [128, 16], F32)
            bcsb = ctx.sbuf("l3_bcsb", [32, T], F32)
            ctx.dma("sync", cwc[:], hw["l3_convw"], sd)
            ctx.dma("sync", cbc[:], hw["l3_convb"], sd)
            for ec in range(16):
                ctx.dma("sync", xie[:, 3:T + 3],
                        hw["xidram"][ec * 128:(ec + 1) * 128, :], sd,
                        wait=[(sv, None)])
                ctx.op("vector", lambda e, ec=ec: e.tensor_copy(
                    xie[:, 0:3], slin[:, ec * 3:(ec + 1) * 3]),
                    wait=[(sd, None)], inc=sv)
                ctx.op("vector", lambda e, ec=ec: e.tensor_scalar(
                    cvt[:], xie[:, 0:T], cwc[:, ec * 4:ec * 4 + 1], None,
                    ALU.mult), wait=[(ss, None)], inc=sv)
                for tau in range(1, 4):
                    ctx.op("vector", lambda e, ec=ec, tau=tau:
                           e.scalar_tensor_tensor(
                               cvt[:], xie[:, tau:T + tau],
                               cwc[:, ec * 4 + tau:ec * 4 + tau + 1],
                               cvt[:], ALU.mult, ALU.add), inc=sv)
                ctx.op("scalar", lambda e, ec=ec: e.activation(
                    hc[:, ec * T:(ec + 1) * T], cvt[:], AF.Silu,
                    bias=cbc[:, ec:ec + 1]),
                    wait=[(sv, None)], inc=ss)
                ctx.dma("sync", hw["hdram"][ec * 128:(ec + 1) * 128, :],
                        hc[:, ec * T:(ec + 1) * T], sdo, wait=[(ss, None)])
            ctx.barrier()
            for fm in range(17):
                M = 128 if fm < 16 else 32
                for kf in range(16):
                    ctx.dma("sync", WS[:, 0:M],
                            hw["l3_wx"][kf, :, fm * 128:fm * 128 + M], sd,
                            wait=[(sv, None)])
                    ctx.op("vector", lambda e, kf=kf, M=M: e.tensor_copy(
                        WB[:, kf * 128:kf * 128 + M], WS[:, 0:M]),
                        wait=[(sd, None), (st, None)], inc=sv)
                for tt in range(NTT):
                    pb = P[0:M, 1024 + (tt % 2) * 512:
                           1024 + (tt % 2 + 1) * 512]
                    for kf in range(16):
                        ctx.op("tensor",
                               lambda e, ps=pb, kf=kf, M=M,
                               l_=WB[:, kf * 128:kf * 128 + M],
                               r=hc[:, kf * T + tt * 512:
                                    kf * T + (tt + 1) * 512],
                               s0=(kf == 0), s1=(kf == 15):
                               e.matmul(ps, l_, r, start=s0, stop=s1),
                               wait=([(sv, None), (ss, None)]
                                     if kf == 0 else []),
                               inc=(st if kf == 15 else None))
                    tsl = slice(tt * 512, (tt + 1) * 512)
                    if fm < 16:
                        ctx.op("scalar", lambda e, ps=pb, tsl=tsl:
                               e.activation(sp1[:, tsl], ps[:], AF.Abs),
                               wait=[(st, None)] +
                               ([(sp, None), (sv, None)] if tt == 0 else []),
                               inc=ss)
                        ctx.op("scalar", lambda e, tsl=tsl: e.activation(
                            sp2[:, tsl], sp1[:, tsl], AF.Exp, scale=-1.0),
                            inc=ss)
                        ctx.op("scalar", lambda e, tsl=tsl: e.activation(
                            sp1[:, tsl], sp2[:, tsl], AF.Ln,
                            bias=com.cst[:, 0:1]), inc=ss)
                        ctx.op("vector", lambda e, ps=pb, tsl=tsl:
                               e.scalar_tensor_tensor(
                                   dbst[:, tsl], ps[:], 0.0, sp1[:, tsl],
                                   ALU.max, ALU.add),
                               wait=[(ss, None)] +
                               ([(sdo, None)] if tt == 0 else []), inc=sv)
                    else:
                        ctx.op("vector", lambda e, ps=pb, tsl=tsl:
                               e.tensor_copy(bcsb[0:32, tsl], ps[:]),
                               wait=[(st, None)], inc=sv)
                if fm < 16:
                    ctx.op("gpsimd", lambda e, fm=fm: e.tensor_mul(
                        ub[:], dbst[:], hc[:, fm * T:(fm + 1) * T]),
                        wait=[(sv, None), (sdo, None)], inc=sp)
                    ctx.op("vector", lambda e, fm=fm: e.reduce_sum(
                        dsums[:, fm:fm + 1], dbst[:], axis=AX), inc=sv)
                    ctx.dma("sync", hw["ddram"][fm * 128:(fm + 1) * 128, :],
                            dbst[:], sdo, wait=[(sv, None)])
                    ctx.dma("sync", hw["udram"][fm * 128:(fm + 1) * 128, :],
                            ub[:], sdo, wait=[(sp, None)])
                else:
                    ctx.dma("sync", hw["bcdram"][0:32, :], bcsb[:], sdo,
                            wait=[(sv, None)])
            ctx.dma("sync",
                    bass.AP(hw["dsdram"], 0, [[16, 128], [1, 16]]),
                    dsums[:], sdo, wait=[(sv, None)])
            ctx.op("sync", lambda e: None,
                   wait=[(sd, None), (sdo, None)])

        with ctx.scope():
            d8 = ctx.sbuf("l3_d8", [8, T], BF16)
            u8 = ctx.sbuf("l3_u8", [8, T], BF16)
            brep = ctx.sbuf("l3_brep", [128, T], F32)
            a_t = ctx.sbuf("l3_a", [128, T], F32)
            b_t = ctx.sbuf("l3_b", [128, T], F32)
            h_t = ctx.sbuf("l3_h", [128, T], F32)
            dsr = ctx.sbuf("l3_dsr", [128, 1], F32)
            SSUM = ctx.sbuf("l3_ssum", [128, 256], F32)
            HLOC = ctx.sbuf("l3_hloc", [128, 256], F32)
            pack = ctx.sbuf("l3_pack", [128, 512], F32)
            blk = ctx.sbuf("l3_blk", [128, 512], F32)
            facc = ctx.sbuf("l3_facc", [128, 256], F32)
            term = ctx.sbuf("l3_term", [128, 256], F32)
            ctx.dma("sync", brep[:], rep_bc(0), sd)
            for g in range(256):
                ctx.dma("sync", d8[:], hw["ddram"][8 * g:8 * g + 8, :], sd,
                        wait=[(st, None)])
                ctx.dma("sync", u8[:], hw["udram"][8 * g:8 * g + 8, :], sd)
                ctx.dma("gpsimd", dsr[:],
                        bass.AP(hw["dsdram"],
                                ((8 * g) % 128) * 16 + (8 * g) // 128,
                                [[16, 8], [0, 16], [1, 1]]),
                        sg, wait=[(sv, None)])
                for q in range(NTT):
                    qs = slice(q * 512, (q + 1) * 512)
                    ctx.op("tensor", lambda e, q=q, qs=qs: e.matmul(
                        BK[q], sel8b[:], d8[:, qs], start=True, stop=True),
                        wait=([(ss, None), (sd, None)] if q == 0 else []),
                        inc=(st if q == NTT - 1 else None))
                for q in range(NTT):
                    qs = slice(q * 512, (q + 1) * 512)
                    ctx.op("tensor", lambda e, q=q, qs=qs: e.matmul(
                        BK[4 + q], sel8b[:], u8[:, qs], start=True,
                        stop=True),
                        wait=([(sv, None)] if q == 0 else []),
                        inc=(st if q == NTT - 1 else None))
                ctx.op("scalar", lambda e, g=g:
                       e.activation(a_t[:], P[:, 0:T], AF.Exp,
                                    scale=ACOL[:, g:g + 1]),
                       wait=[(st, None), (sv, None)], inc=ss)
                ctx.op("vector", lambda e: e.tensor_mul(
                    b_t[:], P[:, T:2 * T], brep[:]),
                    wait=[(st, None)], inc=sv)
                ctx.op("vector", lambda e:
                       e.tensor_tensor_scan(h_t[:], a_t[:], b_t[:], 0.0,
                                            ALU.mult, ALU.add),
                       wait=[(ss, None)], inc=sv)
                ctx.op("vector", lambda e, g=g: e.tensor_copy(
                    HLOC[:, g:g + 1], h_t[:, T - 1:T]), inc=sv)
                ctx.op("vector", lambda e, g=g: e.tensor_mul(
                    SSUM[:, g:g + 1], ACOL[:, g:g + 1], dsr[:]),
                    wait=[(sg, None)], inc=sv)
            ctx.op("vector", lambda e: e.tensor_copy(pack[:, 0:256],
                                                     SSUM[:]), inc=sv)
            ctx.op("vector", lambda e: e.tensor_copy(pack[:, 256:512],
                                                     HLOC[:]), inc=sv)
            ctx.dma("gpsimd", hw["cc_sm_in"], pack[:], sg, wait=[(sv, None)])
            ctx.op("gpsimd", lambda e: e.collective_compute(
                "AllGather", ALU.bypass, replica_groups=GROUPS,
                ins=[hw["cc_sm_in"]], outs=[hw["cc_sm_out"]]),
                wait=[(sg, None)], inc=scc)
            ctx.op("vector", lambda e: e.memset(HIN[:], 0.0),
                   wait=[(scc, None)], inc=sv)
            for i in range(4):
                for k in range(4):
                    ctx.dma("gpsimd", blk[:, 0:256],
                            hw["cc_sm_out"][128 * k:128 * (k + 1), 0:256],
                            sg, wait=[(sv, None)])
                    if k == 0:
                        ctx.op("vector", lambda e, i=i: e.tensor_scalar(
                            facc[:], blk[:, 0:256],
                            mcol[:, i * 4:i * 4 + 1], None, ALU.mult),
                            wait=[(sg, None), (ss, None)], inc=sv)
                    else:
                        ctx.op("vector", lambda e, i=i, k=k:
                               e.scalar_tensor_tensor(
                                   facc[:], blk[:, 0:256],
                                   mcol[:, i * 4 + k:i * 4 + k + 1], facc[:],
                                   ALU.mult, ALU.add),
                               wait=[(sg, None)], inc=sv)
                ctx.op("scalar", lambda e: e.activation(facc[:], facc[:],
                                                        AF.Exp),
                       wait=[(sv, None)], inc=ss)
                ctx.dma("gpsimd", blk[:, 256:512],
                        hw["cc_sm_out"][128 * i:128 * (i + 1), 256:512], sg,
                        wait=[(sv, None)])
                ctx.op("vector", lambda e: e.tensor_mul(
                    term[:], facc[:], blk[:, 256:512]),
                    wait=[(ss, None), (sg, None)], inc=sv)
                ctx.op("vector", lambda e, i=i: e.scalar_tensor_tensor(
                    HIN[:], term[:], mcol[:, 16 + i:17 + i], HIN[:],
                    ALU.mult, ALU.add), inc=sv)

        with ctx.scope():
            YB = ctx.sbuf("l3_yb", [128, 16 * T], BF16)
            d8 = ctx.sbuf("l3d_d8", [8, T], BF16)
            u8 = ctx.sbuf("l3d_u8", [8, T], BF16)
            brep = ctx.sbuf("l3d_brep", [128, T], F32)
            crep = ctx.sbuf("l3d_crep", [128, T], F32)
            a_t = ctx.sbuf("l3d_a", [128, T], F32)
            b_t = ctx.sbuf("l3d_b", [128, T], F32)
            h_t = ctx.sbuf("l3d_h", [128, T], F32)
            hcb = ctx.sbuf("l3d_hcb", [128, T], BF16)
            rselb = ctx.sbuf("l3d_rselb", [128, 16 * 128], BF16)
            dcolc = ctx.sbuf("l3d_dcol", [128, 16], F32)
            hch = ctx.sbuf("l3d_hch", [128, T], BF16)
            zt = ctx.sbuf("l3d_zt", [128, T], BF16)
            ctx.dma("sync", rselb[:],
                    bass.AP(hw["l3_redsel_h"], 0,
                            [[128, 128], [16384, 16], [1, 128]]), sd)
            ctx.dma("sync", dcolc[:], hw["l3_dcol"], sd)
            ctx.dma("sync", brep[:], rep_bc(0), sd)
            ctx.dma("sync", crep[:], rep_bc(16), sd)
            for ec in range(16):
                for j in range(16):
                    g = ec * 16 + j
                    ctx.dma("sync", d8[:], hw["ddram"][8 * g:8 * g + 8, :],
                            sd, wait=[(st, None)])
                    ctx.dma("sync", u8[:], hw["udram"][8 * g:8 * g + 8, :],
                            sd)
                    for q in range(NTT):
                        qs = slice(q * 512, (q + 1) * 512)
                        pd = BK[4 + (q % 2)]
                        pu = BK[6 + (q % 2)]
                        ctx.op("tensor", lambda e, pd=pd, qs=qs: e.matmul(
                            pd, sel8b[:], d8[:, qs], start=True, stop=True),
                            wait=[(ss, None)] +
                            ([(sd, None)] if q == 0 else []),
                            inc=st)
                        ctx.op("tensor", lambda e, pu=pu, qs=qs: e.matmul(
                            pu, sel8b[:], u8[:, qs], start=True, stop=True),
                            wait=[(sv, None)] +
                            ([(sd, None)] if q == 0 else []),
                            inc=st)
                        ctx.op("scalar", lambda e, pd=pd, qs=qs, g=g:
                               e.activation(a_t[:, qs], pd[:], AF.Exp,
                                            scale=ACOL[:, g:g + 1]),
                               wait=[(st, None), (sv, None)], inc=ss)
                        ctx.op("vector", lambda e, pu=pu, qs=qs:
                               e.tensor_mul(b_t[:, qs], pu[:], brep[:, qs]),
                               wait=[(st, None)], inc=sv)
                        init = (HIN[:, g:g + 1] if q == 0
                                else h_t[:, q * 512 - 1:q * 512])
                        ctx.op("vector", lambda e, qs=qs, init=init:
                               e.tensor_tensor_scan(h_t[:, qs], a_t[:, qs],
                                                    b_t[:, qs], init,
                                                    ALU.mult, ALU.add),
                               wait=[(ss, None)], inc=sv)
                        ctx.op("gpsimd", lambda e, qs=qs: e.tensor_mul(
                            hcb[:, qs], h_t[:, qs], crep[:, qs]),
                            wait=[(sv, None), (st, None)], inc=sp)
                        ctx.op("tensor", lambda e, j=j, q=q, qs=qs: e.matmul(
                            BK[q], rselb[:, j * 128:(j + 1) * 128],
                            hcb[:, qs], start=(j == 0), stop=(j == 15)),
                            wait=[(sp, None)] +
                            ([(sv, None), (sd, None)] if j == 0 else []),
                            inc=st)
                ctx.dma("gpsimd", hch[:],
                        hw["hdram"][ec * 128:(ec + 1) * 128, :], sg,
                        wait=[(sv, None)])
                ctx.dma("gpsimd", zt[:],
                        hw["zdram"][ec * 128:(ec + 1) * 128, :], sg)
                for q in range(NTT):
                    qs = slice(q * 512, (q + 1) * 512)
                    ysl = slice(ec * T + q * 512, ec * T + (q + 1) * 512)
                    ctx.op("vector", lambda e, q=q, qs=qs, ysl=ysl, ec=ec:
                           e.scalar_tensor_tensor(
                               YB[:, ysl], hch[:, qs], dcolc[:, ec:ec + 1],
                               BK[q][:], ALU.mult, ALU.add),
                           wait=[(st, None), (sg, None)], inc=sv)
                    ctx.op("vector", lambda e, qs=qs, ysl=ysl: e.tensor_mul(
                        YB[:, ysl], YB[:, ysl], zt[:, qs]), inc=sv)
            ctx.barrier()
            with ctx.scope():
                WS = ctx.sbuf("l3o_ws", [128, 128], F32)
                WB = ctx.sbuf("l3o_wb", [128, 16 * 128], BF16)
                xt = ctx.sbuf("l3o_xt", [128, 512], F32)
                gt2 = ctx.sbuf("l3o_gt", [128, 512], F32)
                stile = ctx.sbuf("l3o_stile", [128, 512], F32)
                for mc in range(NFC):
                    for kf in range(16):
                        ctx.dma("sync", WS[:],
                                hw["l3_wout"][kf, :,
                                              mc * 128:(mc + 1) * 128],
                                sd, wait=[(sv, None)])
                        ctx.op("vector", lambda e, kf=kf: e.tensor_copy(
                            WB[:, kf * 128:(kf + 1) * 128], WS[:]),
                            wait=[(sd, None), (st, None)], inc=sv)
                    for tt in range(NTT):
                        pb = BK[2 + (tt % 2)]
                        sl = slice(mc * T + tt * 512, mc * T + (tt + 1) * 512)
                        ctx.dma("sync", xt[:], x_spill[:, sl], sd,
                                wait=[(sv, None)])
                        ctx.dma("gpsimd", gt2[:], gdram[:, sl], sg,
                                wait=[(sv, None)])
                        for kf in range(16):
                            ctx.op("tensor",
                                   lambda e, ps=pb, kf=kf, tt=tt,
                                   l_=WB[:, kf * 128:(kf + 1) * 128],
                                   r=YB[:, kf * T + tt * 512:
                                        kf * T + (tt + 1) * 512],
                                   s0=(kf == 0), s1=(kf == 15):
                                   e.matmul(ps, l_, r, start=s0, stop=s1),
                                   wait=([(sv, None)] if kf == 0 else []),
                                   inc=(st if kf == 15 else None))
                        ctx.op("vector", lambda e, ps=pb:
                               e.tensor_mul(stile[:], ps[:], gt2[:]),
                               wait=[(st, None), (sg, None), (sd, None),
                                     (sdo, None)],
                               inc=sv)
                        ctx.op("vector", lambda e: e.tensor_add(
                            stile[:], stile[:], xt[:]), inc=sv)
                        ctx.dma("sync", x_spill[:, sl], stile[:], sdo,
                                wait=[(sv, None)])
                ctx.op("sync", lambda e: None,
                       wait=[(sd, None), (sdo, None)])
                ctx.op("gpsimd", lambda e: None, wait=[(sg, None)])


def declare_l0_inputs(nc):
    hw = {}
    hw["l0_wq"] = nc.dram_tensor("l0_wq", [8, 128, 1024], F32, kind="ExternalInput").ap()
    hw["l0_wk"] = nc.dram_tensor("l0_wk", [8, 128, 1024], F32, kind="ExternalInput").ap()
    hw["l0_wv"] = nc.dram_tensor("l0_wv", [8, 128, 1024], F32, kind="ExternalInput").ap()
    hw["l0_wo"] = nc.dram_tensor("l0_wo", [8, 128, 1024], F32, kind="ExternalInput").ap()
    hw["l0_bq"] = nc.dram_tensor("l0_bq", [128, 8], F32, kind="ExternalInput").ap()
    hw["l0_bk"] = nc.dram_tensor("l0_bk", [128, 8], F32, kind="ExternalInput").ap()
    hw["l0_bo"] = nc.dram_tensor("l0_bo", [128, 8], F32, kind="ExternalInput").ap()
    hw["l0_bv_row"] = nc.dram_tensor("l0_bv_row", [1, 1024], F32, kind="ExternalInput").ap()
    hw["l0_rm"] = nc.dram_tensor("l0_rm", [64, 256], F32, kind="ExternalInput").ap()
    hw["l0_wg"] = nc.dram_tensor("l0_wg", [128, 64], F32, kind="ExternalInput").ap()
    hw["l0_bg"] = nc.dram_tensor("l0_bg", [1, 8], F32, kind="ExternalInput").ap()
    hw["l0_w1"] = nc.dram_tensor("l0_w1", [8, 8, 128, 2048], F32, kind="ExternalInput").ap()
    hw["l0_w3"] = nc.dram_tensor("l0_w3", [8, 8, 128, 2048], F32, kind="ExternalInput").ap()
    hw["l0_w2"] = nc.dram_tensor("l0_w2", [8, 16, 128, 1024], F32, kind="ExternalInput").ap()
    hw["gt_dram"] = nc.dram_tensor("gt_dram", [8, T], F32,
                                   kind=("ExternalOutput" if "gt" in DEBUG_TAPS
                                         else "Internal")).ap()
    hw["cc_kv_in"] = nc.dram_tensor("cc_kv_in", [128, 2080], F32).ap()
    hw["cc_kv_out"] = nc.dram_tensor("cc_kv_out", [128, 2080], F32).ap()
    return hw


def prep_l0_inputs(inputs):
    shared = {}
    shared["l0_wq"] = np.ascontiguousarray(inputs["L0_attn_Wq"].reshape(8, 128, 1024))
    shared["l0_wk"] = np.ascontiguousarray(inputs["L0_attn_Wk"].reshape(8, 128, 1024))
    shared["l0_wv"] = np.ascontiguousarray(inputs["L0_attn_Wv"].reshape(8, 128, 1024))
    shared["l0_wo"] = np.ascontiguousarray(inputs["L0_attn_Wo"].reshape(8, 128, 1024))
    shared["l0_bq"] = np.ascontiguousarray(inputs["L0_attn_bq"].reshape(8, 128).T)
    shared["l0_bk"] = np.ascontiguousarray(inputs["L0_attn_bk"].reshape(8, 128).T)
    shared["l0_bo"] = np.ascontiguousarray(inputs["L0_attn_bo"].reshape(8, 128).T)
    shared["l0_bv_row"] = np.ascontiguousarray(inputs["L0_attn_bv"].reshape(1, 1024))
    shared["l0_rm"] = np.ascontiguousarray(inputs["L0_attn_random_matrix"])
    shared["l0_wg"] = np.ascontiguousarray(
        inputs["L0_ffn_Wg"].reshape(8, 128, 8).transpose(1, 0, 2).reshape(128, 64))
    shared["l0_bg"] = np.ascontiguousarray(inputs["L0_ffn_bg"].reshape(1, 8))
    shared["l0_w1"] = np.ascontiguousarray(inputs["L0_ffn_w1"].reshape(8, 8, 128, 2048))
    shared["l0_w3"] = np.ascontiguousarray(inputs["L0_ffn_w3"].reshape(8, 8, 128, 2048))
    shared["l0_w2"] = np.ascontiguousarray(inputs["L0_ffn_w2"].reshape(8, 16, 128, 1024))
    return shared


DEBUG_TAPS = []

def build_program(layers=(0,)):
    nc = bass.Bass("TRN2", target_bir_lowering=False)
    x_in = nc.dram_tensor("x_in", [128, NFC * T], F32, kind="ExternalInput").ap()
    x_out = nc.dram_tensor("x_out", [128, NFC * T], F32, kind="ExternalOutput").ap()
    gdram = nc.dram_tensor("gdram", [128, NFC * T], F32,
                           kind=("ExternalOutput" if "g" in DEBUG_TAPS
                                 else "Internal")).ap()
    x_spill_g = nc.dram_tensor("x_spill_g", [128, NFC * T], F32).ap()
    hw = {}
    if 0 in layers:
        hw.update(declare_l0_inputs(nc))
        if "lg" in DEBUG_TAPS:
            hw["lg_dbg"] = nc.dram_tensor("lg_dbg", [128, 128], F32,
                                          kind="ExternalOutput").ap()
    if 1 in layers:
        hw.update(declare_l1_inputs(nc))
    if 2 in layers:
        hw.update(declare_l2_inputs(nc))
    if 3 in layers:
        hw.update(declare_l3_inputs(nc))

    with ExitStack() as stack:
        with nc.Block() as block:
            ctx = Ctx(nc, stack, block)
            com = Com(ctx)
            glob = {
                "P": ctx.psum("P", [128, 4096], F32),
                "gdram": gdram,
                "x_spill": x_spill_g,
            }
            com.emit_init(ctx)
            sld = ctx.sem("xload")
            ctx.dma("sync", x_spill_g, x_in, sld)
            for eng in ENGINES:
                ctx.op(eng, lambda e: None, wait=[sld])
            ctx.flush()

            for li in layers:
                if li == 0:
                    emit_l0(ctx, com, hw, glob)
                elif li == 1:
                    emit_l1(ctx, com, hw, glob)
                elif li == 2:
                    emit_l2(ctx, com, hw, glob)
                elif li == 3:
                    emit_l3(ctx, com, hw, glob)
                else:
                    raise NotImplementedError(li)
                ctx.barrier()
                ctx.flush()

            ctx_verify_pending = True
            sst = ctx.sem("xstore")
            ctx.dma("sync", x_out, x_spill_g, sst)
            ctx.op("sync", lambda e: None, wait=[sst])
            ctx.flush()
            ctx.verify()
            build_program.last_ctx = ctx
    return nc


def shard_inputs(inputs):
    x = np.asarray(inputs["L0_attn_x"])
    shards = []
    for c in range(NCORE):
        b, j = divmod(c, 4)
        shards.append(to_fm(np.ascontiguousarray(x[b, j * T:(j + 1) * T])))
    return shards


def assemble_output(results):
    out = np.empty((B, S, D), np.float32)
    for c in range(NCORE):
        b, j = divmod(c, 4)
        out[b, j * T:(j + 1) * T] = from_fm(results[c]["x_out"], T)
    return out


def core_meta(c):
    j = c % 4
    w = np.zeros((1, 8), np.float32)
    w[0, :j] = 1.0
    return {"meta_wpref": w}


KLAYERS = (0, 1, 2, 3)


def kernel(**inputs):
    layers = KLAYERS
    shards = shard_inputs(inputs)
    shared = {}
    if 0 in layers:
        shared.update(prep_l0_inputs(inputs))
    if 1 in layers:
        shared.update(prep_l1_inputs(inputs))
    if 2 in layers:
        shared.update(prep_l2_inputs(inputs))
    if 3 in layers:
        shared.update(prep_l3_inputs(inputs))
    nc = build_program(layers=layers)
    in_maps = []
    for c in range(NCORE):
        m = dict(shared, x_in=shards[c], **core_meta(c))
        if 2 in layers:
            m.update(l2_core_meta(c))
        if 3 in layers:
            m.update(l3_core_meta(c))
        in_maps.append(m)
    res = run_bass_kernel_spmd(nc, in_maps, core_ids=list(range(NCORE)))
    return assemble_output(res.results)
